# version 11
# baseline (speedup 1.0000x reference)
import os
import numpy as np
import concourse.bass as bass
import concourse.mybir as mybir
from concourse.bass_utils import run_bass_kernel_spmd
import concourse.replica_groups as replica_groups
from concourse import library_config
from concourse.bass import IndirectOffsetOnAxis

F32 = mybir.dt.float32
BF16 = mybir.dt.bfloat16
I16 = mybir.dt.int16
U32 = mybir.dt.uint32
ALU = mybir.AluOpType
ACT = mybir.ActivationFunctionType
AX = mybir.AxisListType

T = 2048
D = 1024
NB = 16
H = 16
DH = 64
E = 8
NTOK = 8 * T
KCAP = 2048
FF = 4096
LN_EPS = 1e-5
SHEP_EPS = 1e-4

PHASES = int(os.environ.get("MK_PHASES", "4"))
BSTOP = int(os.environ.get("MK_BSTOP", "99"))


def sl(i, n):
    return slice(i * n, (i + 1) * n)


def build_program():
    from contextlib import ExitStack
    nc = bass.Bass("TRN2", target_bir_lowering=False)
    es = ExitStack()

    def din(name, shape, dt=F32):
        return nc.dram_tensor(name, shape, dt, kind="ExternalInput").ap()

    x_in = din("x", [T, D])
    winT = din("winT", [D, 4 * D])
    binr = din("binr", [128, 32])
    woutT = din("woutT", [H, DH, D])
    bout_b = din("bout_b", [128, D])
    ident = din("ident", [128, 128])
    ident_bf = din("ident_bf", [128, 128], BF16)
    csel = din("csel", [128, 4], BF16)
    ones1f = din("ones1f", [1, 128])
    cvals = din("cvals", [128, 8])
    onesrow = din("onesrow", [1, 512], BF16)
    ws_r = din("ws_r", [D, E])
    w1 = din("w1", [D, FF])
    b1r = din("b1r", [128, 32])
    w2 = din("w2", [FF, D])
    b2_b = din("b2_b", [128, D])
    ind8f = din("ind8f", [128, 8])
    ind8Tf = din("ind8Tf", [8, 128])
    selblk = din("selblk", [128, 128])
    triblk = din("triblk", [128, 128])
    gidxp1 = din("gidxp1", [128, 1024])
    offv = din("offv", [128, 1])
    colsel = din("colsel", [8, 1024], BF16)
    bsrow = din("bsrow", [1, E])
    ones8f = din("ones8f", [8, 1])
    ind8f16 = din("ind8f16", [128, 8])
    tidg = din("tidg", [128, 512])
    myexp = din("myexp", [128, 128])
    wkT = din("wkT", [D, D])
    wvT = din("wvT", [D, D])
    wrT = din("wrT", [D, D])
    woT = din("woT", [D, D])
    wfcT = din("wfcT", [D, FF])
    wprojT = din("wprojT", [FF, D])
    decay_r = din("decay_r", [128, 8])
    first_r = din("first_r", [128, 8])
    mixk_r = din("mixk_r", [128, 8])
    mixv_r = din("mixv_r", [128, 8])
    mixr_r = din("mixr_r", [128, 8])
    out_d = nc.dram_tensor("out", [T, D], F32, kind="ExternalOutput").ap()
    rwkvb = nc.dram_tensor("rwkvb", [4 * D, D], BF16, kind="Internal").ap()
    w1b = nc.dram_tensor("w1b", [D, FF], BF16, kind="Internal").ap()
    w2b = nc.dram_tensor("w2b", [FF, D], BF16, kind="Internal").ap()
    wfcb = nc.dram_tensor("wfcb", [D, FF], BF16, kind="Internal").ap()
    wprojb = nc.dram_tensor("wprojb", [FF, D], BF16, kind="Internal").ap()
    scr_row = nc.dram_tensor("scr_row", [1, 8192], F32, kind="Internal").ap()
    scr2 = nc.dram_tensor("scr2", [8, T], F32, kind="Internal").ap()
    agx_in = nc.dram_tensor("agx_in", [T, D], BF16, kind="Internal").ap()
    agx_out = nc.dram_tensor("agx_out", [NTOK, D], BF16, kind="Internal").ap()
    agp_in = nc.dram_tensor("agp_in", [128, 128], F32, kind="Internal").ap()
    agp_out = nc.dram_tensor("agp_out", [8 * 128, 128], F32, kind="Internal").ap()
    agy_in = nc.dram_tensor("agy_in", [KCAP, D], BF16, kind="Internal").ap()
    agy_out = nc.dram_tensor("agy_out", [NTOK, D], BF16, kind="Internal").ap()
    idbufd = nc.dram_tensor("idbufd", [NTOK, 4], F32, kind="Internal").ap()
    idbufr = nc.dram_tensor("idbufr", [NTOK, 4], F32, kind="Internal").ap()
    gslotd = nc.dram_tensor("gslotd", [NTOK, 4], F32, kind="Internal").ap()

    cnt = {}
    sems = {}

    with es:
        def sb(name, shape, dt=F32):
            return es.enter_context(nc.sbuf_tensor(name, shape, dt))

        def pst(name, shape, dt=F32):
            return es.enter_context(nc.psum_tensor(name, shape, dt))

        def new_sem(name):
            sems[name] = es.enter_context(nc.semaphore(name))
            cnt[name] = 0

        def inc(name, n=1):
            cnt[name] += n
            return sems[name], n

        def _ndma():
            return sum(1 for i in nc.all_instructions()
                       if type(i).__name__ in ("InstDMACopy", "InstDMATranspose"))

        def DMA(eng, semname, **kw):
            n0 = _ndma()
            inst = eng.dma_start(**kw)
            n = _ndma() - n0
            inst.then_inc(sems[semname], 16)
            cnt[semname] += 16 * n
            eng.wait_ge(sems[semname], cnt[semname])

        stream = sb("stream", [128, NB, D])
        lnT = sb("lnT", [128, 8, T], BF16)
        TA = sb("TA", [128, T], BF16)
        TB = sb("TB", [128, T], BF16)
        SQ2 = sb("SQ2", [128, T], BF16)
        KQSQ = sb("KQSQ", [128, T], BF16)
        vaug = sb("vaug", [128, NB, 65], BF16)
        scrA = sb("scrA", [128, T])
        rrow = scrA[0:1, :]
        UB = sb("UB", [128, T])
        ZB = sb("ZB", [128, T])
        wsc = sb("wsc", [128, T], BF16)
        abuf = sb("abuf", [65, T], BF16)
        otb = sb("otb", [64, 4, T], BF16)
        whq = sb("whq", [64, 4, D], BF16)
        wslA = [sb(f"wslA{i}", [128, 8, 128], BF16) for i in range(2)]
        wslB = [sb(f"wslB{i}", [128, 8, 128], BF16) for i in range(2)]
        wslC = [sb(f"wslC{i}", [128, 8, 128], BF16) for i in range(2)]
        augK0 = sb("augK0", [1, T], BF16)
        augR0 = sb("augR0", [1, T], BF16)
        c_onesbf = sb("c_onesbf", [1, 512], BF16)
        smallv = sb("smallv", [128, 96])
        c_ident = sb("c_ident", [128, 128])
        c_identbf = sb("c_identbf", [128, 128], BF16)
        c_sel = sb("c_sel", [128, 4], BF16)
        c_ones1f = sb("c_ones1f", [1, 128])
        c_bout = sb("c_bout", [128, D])
        c_b2b = c_bout
        c_bin = sb("c_bin", [128, 32])
        c_cv = sb("c_cv", [128, 8])
        c_ind8f = sb("c_ind8f", [128, 8])
        c_ind8Tf = sb("c_ind8Tf", [8, 128])
        c_selblk = sb("c_selblk", [128, 128])
        c_triblk = sb("c_triblk", [128, 128])
        c_gidxp1 = sb("c_gidxp1", [128, 1024])
        d16 = None
        c_offv = sb("c_offv", [128, 1])
        c_colsel = abuf[0:8, 0:1024]
        c_wsr = sb("c_wsr", [128, 8, 8])
        c_bsb = sb("c_bsb", [128, 8])
        c_b1r = sb("c_b1r", [128, 32])
        moesm = sb("moesm", [128, 64])
        c_bsrow = sb("c_bsrow", [1, 8])
        c_ones81 = sb("c_ones81", [8, 1])
        c_ind8f16 = sb("c_ind8f16", [128, 8])
        c_tidg = sb("c_tidg", [128, 512])
        c_myexp = sb("c_myexp", [128, 128])
        d16 = c_myexp[0:16, 0:128]
        KIDX = sb("KIDX", [128, 400], U32)

        psA = pst("psA", [128, 1024])
        psB = pst("psB", [128, 512])
        psBbf = pst("psBbf", [128, 1024], BF16)
        from contextlib import ExitStack as _ES2
        esA = _ES2()
        psC = esA.enter_context(nc.psum_tensor("psC", [65, 2048], F32))

        for s in ["sd", "sg", "sv", "ss", "st", "sq", "so", "sgt", "sp"]:
            new_sem(s)

        blk = es.enter_context(nc.Block())

        def SEC(engname, fn):
            getattr(blk, engname)(fn)

        def blk2(tile, b):
            return tile[:, b:b + 1, :].squeeze()

        def a0(eng):
            DMA(eng, "sd", out=stream[:], in_=x_in.rearrange("(b p) d -> p b d", p=128))
            for src, dst in [(ident, c_ident), (ident_bf, c_identbf), (csel, c_sel),
                             (ones1f, c_ones1f), (bout_b, c_bout), (binr, c_bin), (cvals, c_cv),
                             (onesrow, c_onesbf)]:
                DMA(eng, "sd", out=dst[:], in_=src)
        SEC("sync", a0)
        LOADS = cnt["sd"]

        def a0g(eng):
            for hh in range(2):
                DMA(eng, "sg", out=wslA[hh][:],
                              in_=winT[:, sl(2 * hh, 128)].rearrange("(c p) e -> p c e", p=128))
                DMA(eng, "sg", out=wslB[hh][:],
                              in_=winT[:, sl(2 * hh + 1, 128)].rearrange("(c p) e -> p c e", p=128))
        SEC("gpsimd", a0g)
        wload = {0: cnt["sg"], 1: cnt["sg"]}

        def a0v(eng):
            eng.memset(vaug[:, :, 64:65], 1.0).then_inc(*inc("sv"))
            eng.drain()
        SEC("vector", a0v)

        def a1_stats(eng):
            eng.wait_ge(sems["sd"], LOADS)
            for b in range(0, NB, 2):
                eng.bn_stats(smallv[:, 0:6], blk2(stream, b)[:, 0:512])
                eng.bn_stats(smallv[:, 6:12], blk2(stream, b)[:, 512:1024])
                eng.bn_stats(smallv[:, 80:86], blk2(stream, b + 1)[:, 0:512])
                eng.bn_stats(smallv[:, 86:92], blk2(stream, b + 1)[:, 512:1024])
                eng.drain()
                eng.bn_aggr(smallv[:, 2 * b + 16:2 * b + 18], smallv[:, 0:12])
                ag = eng.bn_aggr(smallv[:, 2 * b + 18:2 * b + 20], smallv[:, 80:92])
                if b == NB - 2:
                    ag.then_inc(*inc("sv"))
                eng.drain()
        SEC("vector", a1_stats)
        SV_STATS = cnt["sv"]

        def a1_sqrt(eng):
            eng.wait_ge(sems["sv"], SV_STATS)
            for b in range(NB):
                aa = eng.activation(out=smallv[:, 48 + b:49 + b], in_=smallv[:, 2 * b + 17:2 * b + 18],
                               func=ACT.Sqrt, bias=c_cv[:, 0:1])
                if b == NB - 1:
                    aa.then_inc(*inc("ss"))
            eng.drain()
        SEC("scalar", a1_sqrt)
        SS_SQRT = cnt["ss"]

        def a1_recip(eng):
            eng.wait_ge(sems["ss"], SS_SQRT)
            eng.reciprocal(smallv[:, 64:80], smallv[:, 48:64]).then_inc(*inc("sv"))
            eng.drain()
        SEC("vector", a1_recip)
        SV_RSTD = cnt["sv"]

        st_tr = {}
        sv_norm = {}
        ss_ev = {}
        for b in range(NB):
            def a1_norm(eng, b=b):
                eng.wait_ge(sems["sv"], SV_RSTD)
                if b >= 2:
                    eng.wait_ge(sems["st"], st_tr[b - 2])
                eng.tensor_scalar(scrA[:, sl(b % 2, D)], blk2(stream, b),
                                  smallv[:, 2 * b + 16:2 * b + 17], smallv[:, 64 + b:65 + b],
                                  ALU.subtract, ALU.mult).then_inc(*inc("sv"))
                eng.drain()
            SEC("vector", a1_norm)
            sv_norm[b] = cnt["sv"]

            def a1_tr(eng, b=b):
                eng.wait_ge(sems["sv"], sv_norm[b])
                if b >= 1:
                    eng.wait_ge(sems["ss"], ss_ev[b - 1])
                for c in range(8):
                    tr = eng.transpose(psA[:, sl(c, 128)], scrA[:, sl(b % 2, D)][:, sl(c, 128)], c_ident[:])
                    if c == 7:
                        tr.then_inc(*inc("st"))
            SEC("tensor", a1_tr)
            st_tr[b] = cnt["st"]

            def a1_ev(eng, b=b):
                eng.wait_ge(sems["st"], st_tr[b])
                eng.activation(out=lnT[:, :, sl(b, 128)],
                               in_=psA[:, 0:1024].rearrange("p (c t) -> p c t", c=8),
                               func=ACT.Copy).then_inc(*inc("ss"))
            SEC("scalar", a1_ev)
            ss_ev[b] = cnt["ss"]
        SS_LNT = cnt["ss"]

        SKIP_A2 = PHASES < 1

        if not SKIP_A2:
            def a1_dbl(eng):
                for b in range(NB):
                    db = eng.tensor_scalar(blk2(stream, b), blk2(stream, b), 2.0, None, ALU.mult)
                    if b == NB - 1:
                        db.then_inc(*inc("sv"))
                    eng.drain()
            SEC("vector", a1_dbl)

        last_psA_ss = SS_LNT
        last_psA_sv = 0
        last_psC_ss = 0
        last_psB_sv = cnt["sv"]
        prev_ot_sv = 0
        prev_av_st = 0
        prev_d16_sd = 0
        prev_rrow_sd = 0
        prev_rb_st = 0
        prev_evA_ss = 0
        qg_ot_sv = {}
        sv_op_last = cnt["sv"]

        for qg in range(4 if not SKIP_A2 else 0):
            for hh in range(4):
                h = qg * 4 + hh
                par = h % 2
                sA, sB = 2 * h, 2 * h + 1
                OT_PREV = prev_ot_sv
                AV_PREV = prev_av_st
                D16_PREV = prev_d16_sd
                RROW_PREV = prev_rrow_sd
                RB_PREV = prev_rb_st

                def a2_nb(eng, sA=sA, w=prev_evA_ss):
                    eng.wait_ge(sems["sd"], LOADS)
                    if w:
                        eng.wait_ge(sems["ss"], w)
                    eng.tensor_scalar(smallv[:, 8:9], c_bin[:, sA:sA + 1], -2.0, None, ALU.mult).then_inc(*inc("sv"))
                    eng.drain()
                SEC("vector", a2_nb)
                SV_NB = cnt["sv"]

                st_pa = {}
                SS_EA_SNAP = cnt["ss"]

                def a2_pa(eng, h=h, par=par, psafree_ss=last_psA_ss, psafree_sv=last_psA_sv, opfree=sv_op_last):
                    eng.wait_ge(sems["ss"], SS_LNT)
                    eng.wait_ge(sems["sg"], wload[h])
                    if psafree_ss:
                        eng.wait_ge(sems["ss"], psafree_ss)
                    if psafree_sv:
                        eng.wait_ge(sems["sv"], psafree_sv)
                    eng.wait_ge(sems["sv"], opfree)
                    eng.wait_ge(sems["sv"], SV_NB)
                    for half in range(2):
                        if half == 1:
                            eng.wait_ge(sems["ss"], SS_EA_SNAP + 1)
                        for sp in range(2):
                            for c in range(8):
                                mm = eng.matmul(psA[:, sl(sp, 512)], wslA[par][:, c:c + 1, :].squeeze(),
                                           lnT[:, c:c + 1, sl(half * 2 + sp, 512)].squeeze(),
                                           start=(c == 0), stop=(c == 7))
                                if sp == 1 and c == 7:
                                    mm.then_inc(*inc("st"))
                        st_pa[half] = cnt["st"]
                SEC("tensor", a2_pa)

                def a2_ea(eng, h=h, sA=sA):
                    eng.wait_ge(sems["sv"], SV_NB)
                    for half in range(2):
                        eng.wait_ge(sems["st"], st_pa[half])
                        dst = sl(half, 1024)
                        eng.activation(out=TA[:, dst], in_=psA[:], func=ACT.Identity,
                                       bias=c_bin[:, sA:sA + 1])
                        eng.activation(out=SQ2[64:128, dst], in_=psA[64:128, :], func=ACT.Identity,
                                       bias=smallv[64:128, 8:9], scale=-2.0)
                        eng.activation(out=KQSQ[64:128, dst], in_=psA[64:128, :], func=ACT.Square,
                                       bias=c_bin[64:128, sA:sA + 1]).then_inc(*inc("ss"))
                SEC("scalar", a2_ea)
                SS_EA = cnt["ss"]
                prev_evA_ss = SS_EA

                def a2_q2(eng, h=h, pcfree=last_psC_ss):
                    eng.wait_ge(sems["ss"], SS_EA)
                    if pcfree:
                        eng.wait_ge(sems["ss"], pcfree)
                    for sp in range(4):
                        mm = eng.matmul(psC[0:1, sl(sp, 512)], c_sel[64:128, 3:4],
                                   KQSQ[64:128, sl(sp, 512)], start=True, stop=True,
                                   skip_group_check=True)
                        if sp == 3:
                            mm.then_inc(*inc("st"))
                SEC("tensor", a2_q2)
                ST_Q2 = cnt["st"]

                def a2_q2e(eng, h=h):
                    eng.wait_ge(sems["st"], ST_Q2)
                    eng.activation(out=augR0[:], in_=psC[0:1, :], func=ACT.Copy).then_inc(*inc("ss"))
                SEC("scalar", a2_q2e)

                st_vt = {}

                def a2_vt(eng, h=h, pbfree=last_psB_sv):
                    eng.wait_ge(sems["ss"], SS_EA)
                    eng.wait_ge(sems["sv"], pbfree)
                    for j in range(16):
                        tr = eng.transpose(psBbf[:, sl(j, 64)], TA[0:64, sl(j, 128)],
                                      c_identbf[0:64, 0:64])
                        if j == 15:
                            tr.then_inc(*inc("st"))
                    st_vt[0] = cnt["st"]
                SEC("tensor", a2_vt)

                def a2_ve(eng, h=h):
                    eng.wait_ge(sems["st"], st_vt[0])
                    eng.tensor_copy(vaug[:, :, 0:64],
                                    psBbf[:, 0:1024].rearrange("p (b j) -> p b j", b=16)).then_inc(*inc("sv"))
                    eng.drain()
                SEC("vector", a2_ve)
                SV_VAUG = cnt["sv"]
                last_psB_sv = SV_VAUG

                st_pb = {}
                SS_EB_SNAP = cnt["ss"]

                def a2_pb(eng, h=h, par=par):
                    eng.wait_ge(sems["ss"], SS_EA)
                    for half in range(2):
                        if half == 1:
                            eng.wait_ge(sems["ss"], SS_EB_SNAP + 1)
                        for sp in range(2):
                            for c in range(8):
                                mm = eng.matmul(psA[:, sl(sp, 512)], wslB[par][:, c:c + 1, :].squeeze(),
                                           lnT[:, c:c + 1, sl(half * 2 + sp, 512)].squeeze(),
                                           start=(c == 0), stop=(c == 7))
                                if sp == 1 and c == 7:
                                    mm.then_inc(*inc("st"))
                        st_pb[half] = cnt["st"]
                SEC("tensor", a2_pb)

                def a2_eb(eng, h=h, sB=sB, otprev=OT_PREV):
                    if otprev:
                        eng.wait_ge(sems["sv"], otprev)
                    for half in range(2):
                        eng.wait_ge(sems["st"], st_pb[half])
                        dst = sl(half, 1024)
                        eng.activation(out=TB[:, dst], in_=psA[:], func=ACT.Identity,
                                       bias=c_bin[:, sB:sB + 1])
                        eng.activation(out=KQSQ[64:128, dst], in_=psA[64:128, :], func=ACT.Square,
                                       bias=c_bin[64:128, sB:sB + 1]).then_inc(*inc("ss"))
                SEC("scalar", a2_eb)
                SS_EB = cnt["ss"]

                def a2_k2(eng, h=h):
                    eng.wait_ge(sems["ss"], SS_EB)
                    eng.wait_ge(sems["ss"], cnt["ss"])
                    for sp in range(4):
                        mm = eng.matmul(psC[0:1, sl(sp, 512)], c_sel[64:128, 0:1],
                                   KQSQ[64:128, sl(sp, 512)], start=True, stop=True,
                                   skip_group_check=True)
                        if sp == 3:
                            mm.then_inc(*inc("st"))
                SEC("tensor", a2_k2)
                ST_K2 = cnt["st"]

                def a2_aug(eng, h=h):
                    eng.wait_ge(sems["st"], ST_K2)
                    eng.activation(out=augK0[:], in_=psC[0:1, :], func=ACT.Copy).then_inc(*inc("ss"))
                SEC("scalar", a2_aug)
                SS_AUG = cnt["ss"]

                sv_cl = {}
                ss_w = {}
                ss_p1 = {}
                wchx = [(c_tidg[:, 0:512], 0, 512),
                        (KIDX[:, :].bitcast(F32)[:, 0:384], 512, 384),
                        (c_selblk[:, 0:128], 896, 128)]
                st_sc = {}
                st_av = {}
                def mk_av(kb):
                    def a2_av(eng, h=h, kb=kb):
                        eng.wait_ge(sems["ss"], ss_w[kb])
                        for sp in range(4):
                            mm = eng.matmul(psC[:, sl(sp, 512)], vaug[:, kb:kb + 1, :].squeeze(),
                                       wsc[:, sl(sp, 512)],
                                       start=(kb == 0), stop=(kb == NB - 1),
                                       skip_group_check=True)
                            if sp == 3:
                                mm.then_inc(*inc("st"))
                    SEC("tensor", a2_av)
                    st_av[kb] = cnt["st"]

                for kb in range(NB):
                    snap_sv = cnt["sv"]

                    def a2_sc(eng, h=h, kb=kb, snap=snap_sv):
                        if kb == 0:
                            eng.wait_ge(sems["ss"], SS_AUG)
                            eng.wait_ge(sems["sv"], SV_VAUG)
                        for half in range(2):
                            if half == 1 or kb >= 1:
                                eng.wait_ge(sems["sv"], snap + half)
                            for sp in range(2):
                                eng.matmul(psA[:, sl(sp, 512)],
                                           TB[64:128, sl(kb, 128)],
                                           SQ2[64:128, sl(half * 2 + sp, 512)],
                                           start=True, stop=False, skip_group_check=True)
                                eng.matmul(psA[:, sl(sp, 512)],
                                           augK0[0:1, sl(kb, 128)],
                                           c_onesbf[0:1, 0:512],
                                           start=False, stop=False, skip_group_check=True)
                                mm = eng.matmul(psA[:, sl(sp, 512)],
                                           c_onesbf[0:1, 0:128],
                                           augR0[0:1, sl(half * 2 + sp, 512)],
                                           start=False, stop=True, skip_group_check=True)
                                if sp == 1:
                                    mm.then_inc(*inc("st"))
                            st_sc[(kb, half)] = cnt["st"]
                    SEC("tensor", a2_sc)

                    if kb >= 1:
                        mk_av(kb - 1)

                    def a2_cl(eng, h=h, kb=kb, otprev=OT_PREV):
                        if kb >= 1:
                            eng.wait_ge(sems["ss"], ss_p1[kb - 1])
                        elif otprev:
                            eng.wait_ge(sems["sv"], otprev)
                        for half in range(2):
                            eng.wait_ge(sems["st"], st_sc[(kb, half)])
                            eng.tensor_scalar(UB[:, sl(half, 1024)], psA[:], 0.0, None, ALU.max).then_inc(*inc("sv"))
                    SEC("vector", a2_cl)
                    sv_cl[kb] = cnt["sv"]

                    def a2_wch(eng, h=h, kb=kb, avprev=AV_PREV):
                        eng.wait_ge(sems["sv"], sv_cl[kb])
                        if kb >= 1:
                            eng.wait_ge(sems["st"], st_av[kb - 1])
                        elif avprev:
                            eng.wait_ge(sems["st"], avprev)
                        eng.activation(out=ZB[:, 0:1024], in_=UB[:, 0:1024], func=ACT.Sqrt)
                        eng.activation(out=ZB[:, 1024:2048], in_=UB[:, 1024:2048],
                                       func=ACT.Sqrt).then_inc(*inc("ss"))
                        eng.drain()
                        eng.activation(out=c_gidxp1[:, :], in_=ZB[:, 0:1024], func=ACT.Ln, bias=c_cv[:, 1:2])
                        for xp, off, w in wchx:
                            eng.activation(out=xp, in_=ZB[:, 1024 + off:1024 + off + w],
                                           func=ACT.Ln, bias=c_cv[:, 1:2])
                        eng.drain()
                        eng.activation(out=wsc[:, 0:1024], in_=c_gidxp1[:, :], func=ACT.Exp, scale=-2.0)
                        for j, (xp, off, w) in enumerate(wchx):
                            ee = eng.activation(out=wsc[:, 1024 + off:1024 + off + w], in_=xp,
                                                func=ACT.Exp, scale=-2.0)
                        ee.then_inc(*inc("ss"))
                    SEC("scalar", a2_wch)
                    ss_w[kb] = cnt["ss"]
                    ss_p1[kb] = ss_w[kb] - 1
                mk_av(NB - 1)

                def a2_ae(eng, h=h, otprev=OT_PREV, d16prev=D16_PREV):
                    eng.wait_ge(sems["st"], st_av[NB - 1])
                    if otprev:
                        eng.wait_ge(sems["sv"], otprev)
                    if d16prev:
                        eng.wait_ge(sems["sd"], d16prev)
                    eng.activation(out=abuf[:], in_=psC[:], func=ACT.Copy)
                    eng.activation(out=scrA[64:65, :], in_=psC[64:65, :], func=ACT.Copy).then_inc(*inc("ss"))
                SEC("scalar", a2_ae)
                SS_AE = cnt["ss"]
                last_psC_ss = SS_AE
                last_psA_sv = sv_cl[NB - 1]

                def a2_d1(eng, h=h, rrowprev=RROW_PREV):
                    eng.wait_ge(sems["ss"], SS_AE)
                    if rrowprev:
                        eng.wait_ge(sems["sd"], rrowprev)
                    DMA(eng, "sd", out=scr_row[0:1, 0:T], in_=scrA[64:65, :])
                    eng.wait_ge(sems["sd"], cnt["sd"])
                    DMA(eng, "sd", out=d16, in_=scr_row[0:1, 0:T].rearrange("o (r f) -> (o r) f", r=16))
                SEC("sync", a2_d1)
                SD_D16 = cnt["sd"]

                def a2_rc(eng, h=h):
                    eng.wait_ge(sems["sd"], SD_D16)
                    eng.tensor_scalar(d16, d16, float(SHEP_EPS), None, ALU.add)
                    eng.drain()
                    eng.reciprocal(d16, d16).then_inc(*inc("sv"))
                    eng.drain()
                SEC("vector", a2_rc)
                SV_RC = cnt["sv"]

                def a2_d2(eng, h=h, rbprev=RB_PREV):
                    eng.wait_ge(sems["sv"], SV_RC)
                    if rbprev:
                        eng.wait_ge(sems["st"], rbprev)
                    DMA(eng, "sd", out=scr_row[0:1, 0:T].rearrange("o (r f) -> (o r) f", r=16),
                                  in_=d16)
                    eng.wait_ge(sems["sd"], cnt["sd"])
                    DMA(eng, "sd", out=rrow, in_=scr_row[0:1, 0:T])
                SEC("sync", a2_d2)
                SD_RROW = cnt["sd"]

                st_rb = {}
                SV_OT_BASE = cnt["sv"]

                def a2_rb(eng, h=h):
                    eng.wait_ge(sems["sd"], SD_RROW)
                    eng.wait_ge(sems["ss"], SS_AE)
                    eng.wait_ge(sems["sv"], SV_VAUG)
                    for sp in range(4):
                        if sp >= 1:
                            eng.wait_ge(sems["sv"], SV_OT_BASE + 2 * sp - 1)
                        eng.matmul(psB[0:64, :], c_ones1f[0:1, 0:64], rrow[0:1, sl(sp, 512)],
                                   start=True, stop=True, skip_group_check=True).then_inc(*inc("st"))
                        st_rb[sp] = cnt["st"]
                SEC("tensor", a2_rb)

                def a2_ot(eng, h=h, hh=hh):
                    for sp in range(4):
                        eng.wait_ge(sems["st"], st_rb[sp])
                        eng.tensor_tensor(UB[0:64, sl(sp, 512)], abuf[0:64, sl(sp, 512)],
                                          psB[0:64, :], ALU.mult).then_inc(*inc("sv"))
                        eng.drain()
                        eng.tensor_tensor(otb[:, hh:hh + 1, sl(sp, 512)].squeeze(),
                                          UB[0:64, sl(sp, 512)],
                                          TB[0:64, sl(sp, 512)], ALU.mult).then_inc(*inc("sv"))
                        eng.drain()
                SEC("vector", a2_ot)
                SV_OT = cnt["sv"]
                last_psB_sv = SV_OT
                prev_ot_sv = SV_OT
                prev_av_st = st_av[NB - 1]
                prev_d16_sd = SD_D16
                prev_rrow_sd = SD_RROW
                prev_rb_st = st_rb[3]

                if h + 2 < H:
                    def a2_wl(eng, h=h, par=par, pbdone=st_pb[1]):
                        eng.wait_ge(sems["st"], pbdone)
                        DMA(eng, "sg", out=wslA[par][:],
                                      in_=winT[:, sl(2 * (h + 2), 128)].rearrange("(c p) e -> p c e", p=128))
                        DMA(eng, "sg", out=wslB[par][:],
                                      in_=winT[:, sl(2 * (h + 2) + 1, 128)].rearrange("(c p) e -> p c e", p=128))
                        eng.wait_ge(sems["sg"], cnt["sg"])
                    SEC("gpsimd", a2_wl)
                    wload[h + 2] = cnt["sg"]

                if PHASES >= 2:
                    def a2_pc(eng, h=h):
                        DMA(eng, "sp", out=w1b[h * 64:(h + 1) * 64, :],
                                      in_=w1[h * 64:(h + 1) * 64, :])
                        DMA(eng, "sp", out=w2b[h * 256:(h + 1) * 256, :],
                                      in_=w2[h * 256:(h + 1) * 256, :])
                        if PHASES >= 3:
                            src = [wkT, wvT, wrT, woT][h // 4]
                            r0 = (h % 4) * 256
                            DMA(eng, "sp", out=rwkvb[(h // 4) * D + r0:(h // 4) * D + r0 + 256, :],
                                          in_=src[r0:r0 + 256, :])
                    SEC("gpsimd", a2_pc)

            qg_ot_sv[qg] = prev_ot_sv

            def a2_whq(eng, qg=qg, opfree=sv_op_last):
                eng.wait_ge(sems["sv"], opfree)
                DMA(eng, "sg", out=whq[:],
                              in_=woutT[sl(qg, 4), :, :].rearrange("h o c -> o h c"))
            SEC("gpsimd", a2_whq)
            SD_WHQ = cnt["sg"]

            st_op = {}
            sv_op = {}
            for tt in range(NB):
                def a2_op(eng, qg=qg, tt=tt, psafree_sv=last_psA_sv):
                    eng.wait_ge(sems["sg"], SD_WHQ)
                    eng.wait_ge(sems["sv"], qg_ot_sv[qg])
                    if tt == 0 and psafree_sv:
                        eng.wait_ge(sems["sv"], psafree_sv)
                    else:
                        eng.wait_ge(sems["sv"], sv_op[tt - 1])
                    for sp in range(2):
                        for hh4 in range(4):
                            mm = eng.matmul(psA[:, sl(sp, 512)],
                                       otb[:, hh4:hh4 + 1, sl(tt, 128)].squeeze(),
                                       whq[:, hh4:hh4 + 1, sl(sp, 512)].squeeze(),
                                       start=(hh4 == 0), stop=(hh4 == 3), skip_group_check=True)
                            if sp == 1 and hh4 == 3:
                                mm.then_inc(*inc("st"))
                SEC("tensor", a2_op)
                st_op[tt] = cnt["st"]

                def a2_opv(eng, qg=qg, tt=tt):
                    eng.wait_ge(sems["st"], st_op[tt])
                    eng.tensor_tensor(blk2(stream, tt), psA[:, 0:1024], blk2(stream, tt), ALU.add).then_inc(*inc("sv"))
                    eng.drain()
                SEC("vector", a2_opv)
                sv_op[tt] = cnt["sv"]
            sv_op_last = sv_op[NB - 1]
            last_psA_ss = 0
            last_psA_sv = sv_op[NB - 1]

        SP_RWKV = cnt["sp"]

        if SKIP_A2:
            SV_DONE0 = cnt["sv"]

            def a_out0(eng):
                eng.wait_ge(sems["sv"], SV_DONE0)
                eng.wait_ge(sems["ss"], cnt["ss"])
                DMA(eng, "so", out=out_d.rearrange("(b p) d -> p b d", p=128), in_=stream[:])
                eng.wait_ge(sems["so"], cnt["so"])
            SEC("sync", a_out0)
            es.pop_all().close() if False else None
            blk.__exit__(None, None, None) if False else None
        if not SKIP_A2:
            _dummy = 0

            def a3_bias(eng):
                for b in range(NB):
                    eng.tensor_tensor(blk2(stream, b), c_bout[:], blk2(stream, b), ALU.add)
                eng.drain()
                for b in range(0, NB, 2):
                    eng.bn_stats(smallv[:, 0:6], blk2(stream, b)[:, 0:512])
                    eng.bn_stats(smallv[:, 6:12], blk2(stream, b)[:, 512:1024])
                    eng.bn_stats(smallv[:, 80:86], blk2(stream, b + 1)[:, 0:512])
                    eng.bn_stats(smallv[:, 86:92], blk2(stream, b + 1)[:, 512:1024])
                    eng.drain()
                    eng.bn_aggr(smallv[:, 2 * b + 16:2 * b + 18], smallv[:, 0:12])
                    ag = eng.bn_aggr(smallv[:, 2 * b + 18:2 * b + 20], smallv[:, 80:92])
                    if b == NB - 2:
                        ag.then_inc(*inc("sv"))
                    eng.drain()
            SEC("vector", a3_bias)
            SV_ST3 = cnt["sv"]

            def a3_sqrt(eng):
                eng.wait_ge(sems["sv"], SV_ST3)
                for b in range(NB):
                    aa = eng.activation(out=smallv[:, 48 + b:49 + b], in_=smallv[:, 2 * b + 17:2 * b + 18],
                                   func=ACT.Sqrt, bias=c_cv[:, 0:1])
                    if b == NB - 1:
                        aa.then_inc(*inc("ss"))
                eng.drain()
            SEC("scalar", a3_sqrt)
            SS_SQ3 = cnt["ss"]

            def a3_norm(eng):
                eng.wait_ge(sems["ss"], SS_SQ3)
                eng.reciprocal(smallv[:, 64:80], smallv[:, 48:64])
                eng.drain()
                for b in range(NB):
                    tsn = eng.tensor_scalar(blk2(stream, b), blk2(stream, b),
                                      smallv[:, 2 * b + 16:2 * b + 17], smallv[:, 64 + b:65 + b],
                                      ALU.subtract, ALU.mult)
                    if b == NB - 1:
                        tsn.then_inc(*inc("sv"))
                eng.drain()
            SEC("vector", a3_norm)
            SV_DONEA = cnt["sv"]


            _ph = {}

            def emit_phase_b():
                if PHASES < 2:
                    return
                esA.close()
                psD2 = es.enter_context(nc.psum_tensor("psD2", [128, 2048], F32))
                _ph["psD2"] = psD2
                psBf32 = psBbf[:, :].bitcast(F32)

                S16g = TA[:, :].bitcast(F32)
                SQf = SQ2[:, :].bitcast(F32)
                LWl = SQf[:, 0:128]
                PWR = SQf[:, 128:256]
                m_l = SQf[:, 256:384]
                probs = wsc[:, :].bitcast(F32)
                PW = scrA[0:8, :]
                hbv = lnT[:, :, :].rearrange("p c t -> p (c t)")
                UBbf = UB[:, :].bitcast(BF16)
                yv = ZB[:, :].bitcast(BF16)
                KQi = KQSQ[:, :].bitcast(I16)
                KQu = KQSQ[:, :].bitcast(U32)
                xgst = vaug[:, :, :].rearrange("p a b -> p (a b)")[:, 0:1024]

                def b0(eng):
                    eng.wait_ge(sems["sv"], SV_DONEA)
                    for s2, dst in [(ind8f, c_ind8f), (ind8Tf, c_ind8Tf), (selblk, c_selblk),
                                    (triblk, c_triblk), (gidxp1, c_gidxp1), (offv, c_offv),
                                    (colsel, c_colsel), (bsrow, c_bsrow),
                                    (ones8f, c_ones81), (ind8f16, c_ind8f16),
                                    (tidg, c_tidg), (myexp, c_myexp),
                                    (b1r, c_b1r), (b2_b, c_b2b)]:
                        DMA(eng, "sd", out=(dst if dst is c_colsel else dst[:]), in_=s2)
                    DMA(eng, "sd", out=c_wsr[:], in_=ws_r.rearrange("(c p) e -> p c e", p=128))
                SEC("sync", b0)
                B0L = cnt["sd"]

                if BSTOP < 1:
                    return
                sv_rte = {}
                ss_rx = {}
                for tt in range(NB):
                    def b1_tr(eng, tt=tt):
                        if tt == 0:
                            eng.wait_ge(sems["sv"], SV_DONEA)
                            eng.wait_ge(sems["sd"], B0L)
                        if tt >= 2:
                            eng.wait_ge(sems["sv"], sv_rte[tt - 2])
                        for c in range(8):
                            tr = eng.transpose(psA[:, sl(c, 128)], blk2(stream, tt)[:, sl(c, 128)], c_ident[:])
                            if c == 7:
                                tr.then_inc(*inc("st"))
                    SEC("tensor", b1_tr)
                    ST_RT = cnt["st"]

                    def b1_ev(eng, tt=tt):
                        eng.wait_ge(sems["st"], ST_RT)
                        eng.tensor_copy(UB[:, sl(tt % 2, 1024)], psA[:, 0:1024]).then_inc(*inc("sv"))
                        eng.drain()
                    SEC("vector", b1_ev)
                    sv_rte[tt] = cnt["sv"]

                    def b1_mm(eng, tt=tt):
                        eng.wait_ge(sems["sv"], sv_rte[tt])
                        if tt >= 1:
                            eng.wait_ge(sems["ss"], ss_rx[tt - 1])
                        for c in range(8):
                            eng.matmul(psB[:, 0:8], UB[:, sl(tt % 2, 1024)][:, sl(c, 128)],
                                       c_wsr[:, c:c + 1, :].squeeze(),
                                       start=(c == 0), stop=False, skip_group_check=True)
                        mm = eng.matmul(psB[:, 0:8], c_ones1f[0:1, 0:128], c_bsrow[0:1, :],
                                        start=False, stop=True, skip_group_check=True)
                        mm.then_inc(*inc("st"))
                    SEC("tensor", b1_mm)
                    ST_RMM = cnt["st"]

                    def b1_x(eng, tt=tt):
                        eng.wait_ge(sems["st"], ST_RMM)
                        eng.activation(out=probs[:, sl(tt, 8)], in_=psB[:, 0:8], func=ACT.Exp)
                        eng.drain()
                        eng.nop().then_inc(*inc("ss"))
                    SEC("scalar", b1_x)
                    ss_rx[tt] = cnt["ss"]

                def b2_sm(eng):
                    eng.wait_ge(sems["ss"], ss_rx[NB - 1])
                    pv = probs[:, 0:128].rearrange("p (b e) -> p b e", e=8)
                    eng.tensor_reduce(moesm[:, 0:16], pv, AX.X, ALU.add)
                    eng.drain()
                    eng.reciprocal(moesm[:, 16:32], moesm[:, 0:16])
                    eng.drain()
                    rv = moesm[:, 16:32].rearrange("p b -> p b 1" if False else "p (b o) -> p b o", o=1)
                    eng.tensor_tensor(pv, pv, rv.broadcast_to([128, 16, 8]), ALU.mult)
                    eng.drain()
                    eng.nop().then_inc(*inc("sv"))
                SEC("vector", b2_sm)
                SV_SM = cnt["sv"]

                if BSTOP < 2:
                    return
                st_pt = {}
                for rnd in range(4):
                    def b3_tr(eng, rnd=rnd):
                        eng.wait_ge(sems["sv"], SV_SM)
                        if rnd >= 1:
                            eng.wait_ge(sems["sv"], SV_SM + rnd)
                        for j in range(4):
                            b = rnd * 4 + j
                            tr = eng.transpose(psB[0:8, sl(j, 128)], probs[:, sl(b, 8)], c_ident[:])
                            if j == 3:
                                tr.then_inc(*inc("st"))
                    SEC("tensor", b3_tr)
                    st_pt[rnd] = cnt["st"]

                    def b3_ev(eng, rnd=rnd):
                        eng.wait_ge(sems["st"], st_pt[rnd])
                        eng.tensor_copy(PW[:, sl(rnd, 512)], psB[0:8, 0:512]).then_inc(*inc("sv"))
                        eng.drain()
                    SEC("vector", b3_ev)
                SV_PW = cnt["sv"]

                def b3_d(eng):
                    eng.wait_ge(sems["sv"], SV_PW)
                    DMA(eng, "sd", out=scr2, in_=PW[:, :])
                    eng.wait_ge(sems["sd"], cnt["sd"])
                    DMA(eng, "sd", out=PWR.rearrange("p (e r) -> p e r", r=16),
                                  in_=scr2.rearrange("e (f r) -> f e r", r=16))
                SEC("sync", b3_d)
                SD_PWR = cnt["sd"]

                def b3_t2(eng):
                    eng.wait_ge(sems["sd"], SD_PWR)
                    eng.transpose(psA[:, 0:128], PWR, c_ident[:]).then_inc(*inc("st"))
                SEC("tensor", b3_t2)
                ST_LW = cnt["st"]

                def b3_lw(eng):
                    eng.wait_ge(sems["st"], ST_LW)
                    eng.tensor_copy(LWl, psA[:, 0:128]).then_inc(*inc("sv"))
                    eng.drain()
                SEC("vector", b3_lw)
                SV_LW = cnt["sv"]

                if BSTOP < 3:
                    return
                def b4_x(eng):
                    eng.wait_ge(sems["sv"], SV_DONEA)
                    DMA(eng, "sg", out=agx_in.rearrange("(b p) d -> p b d", p=128),
                                  in_=stream[:])
                    eng.wait_ge(sems["sg"], cnt["sg"])
                    eng.collective_compute("AllGather", ALU.bypass,
                                           replica_groups=replica_groups.LNC1_1x8,
                                           ins=[agx_in], outs=[agx_out]).then_inc(*inc("sq"))
                SEC("gpsimd", b4_x)
                SQ_AGX = cnt["sq"]

                def b4_p(eng):
                    eng.wait_ge(sems["sv"], SV_LW)
                    DMA(eng, "sd", out=agp_in, in_=LWl)
                SEC("sync", b4_p)
                SD_AGP = cnt["sd"]

                def b4_pc(eng):
                    eng.wait_ge(sems["sd"], SD_AGP)
                    eng.collective_compute("AllGather", ALU.bypass,
                                           replica_groups=replica_groups.LNC1_1x8,
                                           ins=[agp_in], outs=[agp_out]).then_inc(*inc("sq"))
                SEC("gpsimd", b4_pc)
                SQ_AGP = cnt["sq"]

                def b5_l(eng):
                    eng.wait_ge(sems["sq"], SQ_AGP)
                    DMA(eng, "sd", out=S16g.rearrange("p (c f) -> p c f", c=8),
                                  in_=agp_out.rearrange("(c p) f -> p c f", p=128))
                SEC("sync", b5_l)
                SD_S16 = cnt["sd"]

                if BSTOP < 4:
                    return
                def b6_init(eng):
                    eng.wait_ge(sems["sd"], SD_S16)
                    eng.memset(KQi[:, 0:1152], 0)
                    eng.drain()
                    eng.memset(moesm[:, 32:33], 0.0)
                    eng.memset(moesm[:, 33:34], 1.01)
                    eng.drain()
                    eng.nop().then_inc(*inc("sv"))
                SEC("vector", b6_init)

                selb = KQSQ[:, :].bitcast(I16)[:, 1200:1204]
                for it in range(32):
                    def b6_v1(eng, it=it):
                        eng.tensor_scalar(moesm[:, 34:35], moesm[:, 32:33], moesm[:, 33:34], 0.5,
                                          ALU.add, ALU.mult)
                        eng.drain()
                        eng.tensor_scalar(UB[:, 0:1024], S16g, moesm[:, 34:35], 0.0, ALU.is_ge,
                                          ALU.add, accum_out=moesm[:, 35:36]).then_inc(*inc("sv"))
                        eng.drain()
                    SEC("vector", b6_v1)
                    SV_B6 = cnt["sv"]

                    def b6_t(eng, it=it):
                        eng.wait_ge(sems["sv"], SV_B6)
                        eng.matmul(psB[:, 16:17], c_selblk[:, :], moesm[:, 35:36],
                                   start=True, stop=True, skip_group_check=True).then_inc(*inc("st"))
                    SEC("tensor", b6_t)
                    ST_B6 = cnt["st"]

                    def b6_v2(eng, it=it):
                        eng.wait_ge(sems["st"], ST_B6)
                        eng.tensor_scalar(selb[:, 0:1], psB[:, 16:17], 2047.5, None, ALU.is_ge)
                        eng.drain()
                        eng.select(moesm[:, 32:33], selb[:, 0:1], moesm[:, 34:35], moesm[:, 32:33], add_drain=True)
                        eng.drain()
                        eng.tensor_scalar(selb[:, 1:2], psB[:, 16:17], 2047.5, None, ALU.is_lt)
                        eng.drain()
                        eng.select(moesm[:, 33:34], selb[:, 1:2], moesm[:, 34:35], moesm[:, 33:34], add_drain=True).then_inc(*inc("sv"))
                        eng.drain()
                    SEC("vector", b6_v2)
                SV_TAU = cnt["sv"]

                if BSTOP < 5:
                    return
                def b7_v(eng):
                    eng.wait_ge(sems["sv"], SV_TAU)
                    eng.tensor_scalar(UB[:, 0:1024], S16g, moesm[:, 32:33], None, ALU.is_ge)
                    eng.drain()
                    eng.tensor_scalar(m_l, LWl, moesm[:, 32:33], None, ALU.is_ge).then_inc(*inc("sv"))
                    eng.drain()
                SEC("vector", b7_v)
                SV_MG = cnt["sv"]

                def b7_cs(eng):
                    eng.wait_ge(sems["sv"], SV_MG)
                    for half in range(2):
                        mm = eng.matmul(psA[0:8, sl(half, 512)], c_ind8f[:, :],
                                        UB[:, sl(half, 512)], start=True, stop=True,
                                        skip_group_check=True)
                        if half == 1:
                            mm.then_inc(*inc("st"))
                SEC("tensor", b7_cs)
                ST_CS = cnt["st"]

                def b7_scan(eng):
                    eng.wait_ge(sems["st"], ST_CS)
                    eng.tensor_copy(PW[:, 0:1024], psA[0:8, 0:1024])
                    eng.drain()
                    eng.tensor_tensor_scan(PW[:, 1024:2048], c_ones81[:, :].broadcast_to([8, 1024]), PW[:, 0:1024],
                                           0.0, ALU.mult, ALU.add)
                    eng.drain()
                    eng.tensor_tensor(PW[:, 1024:2048], PW[:, 1024:2048], PW[:, 0:1024], ALU.subtract)
                    eng.drain()
                    eng.tensor_tensor(PW[:, 0:1024], PW[:, 1024:2048], c_colsel, ALU.mult)
                    eng.drain()
                    eng.tensor_reduce(moesm[0:8, 40:41], PW[:, 0:1024], AX.X, ALU.add).then_inc(*inc("sv"))
                    eng.drain()
                SEC("vector", b7_scan)
                SV_C8 = cnt["sv"]

                def b7_cb(eng):
                    eng.wait_ge(sems["sv"], SV_C8)
                    eng.matmul(psB[:, 16:17], c_ind8Tf[:, :], moesm[0:8, 40:41],
                               start=True, stop=True, skip_group_check=True).then_inc(*inc("st"))
                SEC("tensor", b7_cb)
                ST_CB = cnt["st"]

                def b7_cv(eng):
                    eng.wait_ge(sems["st"], ST_CB)
                    eng.tensor_copy(moesm[:, 41:42], psB[:, 16:17])
                    eng.drain()
                    eng.nop().then_inc(*inc("sv"))
                SEC("vector", b7_cv)
                SV_CV = cnt["sv"]

                def b7_lcs(eng):
                    eng.wait_ge(sems["sv"], SV_CV)
                    eng.matmul(psB[0:8, 0:128], c_ind8f[:, :], m_l,
                               start=True, stop=True, skip_group_check=True).then_inc(*inc("st"))
                SEC("tensor", b7_lcs)
                ST_LCS = cnt["st"]

                def b7_lscan(eng):
                    eng.wait_ge(sems["st"], ST_LCS)
                    eng.tensor_copy(PW[:, 0:128], psB[0:8, 0:128])
                    eng.drain()
                    eng.tensor_tensor_scan(PW[:, 128:256], c_ones81[:, :].broadcast_to([8, 128]), PW[:, 0:128],
                                           0.0, ALU.mult, ALU.add)
                    eng.drain()
                    eng.tensor_tensor(PW[:, 128:256], PW[:, 128:256], PW[:, 0:128], ALU.subtract).then_inc(*inc("sv"))
                    eng.drain()
                SEC("vector", b7_lscan)
                SV_LSC = cnt["sv"]

                def b7_slot(eng):
                    eng.wait_ge(sems["sv"], SV_LSC)
                    eng.matmul(psA[:, 0:128], c_ind8Tf[:, :], PW[:, 128:256],
                               start=True, stop=False, skip_group_check=True)
                    eng.matmul(psA[:, 0:128], c_triblk[:, :], m_l,
                               start=False, stop=True, skip_group_check=True).then_inc(*inc("st"))
                SEC("tensor", b7_slot)
                ST_SLOT = cnt["st"]

                def b7_adj(eng):
                    eng.wait_ge(sems["st"], ST_SLOT)
                    eng.tensor_scalar(ZB[:, 1024:1152], psA[:, 0:128], moesm[:, 41:42], None, ALU.add)
                    eng.drain()
                    eng.tensor_scalar(ZB[:, 1024:1152], ZB[:, 1024:1152], 2047.0, c_offv[:, 0:1],
                                      ALU.min, ALU.add).then_inc(*inc("sv"))
                    eng.drain()
                SEC("vector", b7_adj)
                SV_ADJ = cnt["sv"]

                SV_IDX = cnt["sv"]

                def b8_sc(eng):
                    eng.wait_ge(sems["sv"], SV_ADJ)
                    eng.tensor_scalar(UB[:, 0:128], m_l, -60000.0, 60000.0, ALU.mult, ALU.add)
                    eng.drain()
                    eng.tensor_tensor(UB[:, 128:256], ZB[:, 1024:1152], UB[:, 0:128], ALU.add)
                    eng.drain()
                    eng.tensor_copy(KIDX[:, 0:128], UB[:, 128:256])
                    eng.drain()
                    eng.tensor_scalar(UB[:, 128:256], c_gidxp1[:, 0:128], c_offv[:, 0:1], -1.0,
                                      ALU.add, ALU.add)
                    eng.drain()
                    eng.tensor_copy(KIDX[:, 128:256], UB[:, 128:256])
                    eng.drain()
                    eng.memset(UB[:, 1536:2048], 0.0)
                    eng.drain()
                    for qq in range(4):
                        cp = eng.tensor_copy(
                            UB[:, 1024:1536].rearrange("p (a b) -> p a b", b=4)[:, :, qq:qq + 1].squeeze(),
                            ZB[:, 1024:1152])
                        if qq == 3:
                            cp.then_inc(*inc("sv"))
                        eng.drain()
                SEC("vector", b8_sc)
                SV_SC = cnt["sv"]

                def b8_z(eng):
                    eng.wait_ge(sems["sv"], SV_SC)
                    DMA(eng, "sd", out=idbufd.rearrange("(p j) o -> p (j o)", p=128),
                        in_=UB[:, 1536:2048])
                SEC("sync", b8_z)
                SD_Z = cnt["sd"]

                def b8_scat(eng):
                    eng.wait_ge(sems["sv"], SV_SC)
                    eng.wait_ge(sems["sd"], SD_Z)
                    breg = eng.alloc_register()
                    eng.reg_mov(breg, NTOK - 1)
                    for Fl in range(128):
                        eng.indirect_dma_start(
                            out=idbufd,
                            out_offset=IndirectOffsetOnAxis(ap=KIDX[:, Fl:Fl + 1], axis=0),
                            in_=c_tidg[:, 4 * Fl:4 * Fl + 4], in_offset=None,
                            bounds_check=breg, oob_is_err=False).then_inc(*inc("sgt", 16))
                    eng.wait_ge(sems["sgt"], cnt["sgt"])
                    for Fl in range(128):
                        eng.indirect_dma_start(
                            out=gslotd,
                            out_offset=IndirectOffsetOnAxis(ap=KIDX[:, 128 + Fl:129 + Fl], axis=0),
                            in_=UB[:, 1024 + 4 * Fl:1028 + 4 * Fl], in_offset=None).then_inc(*inc("sgt", 16))
                    eng.wait_ge(sems["sgt"], cnt["sgt"])
                    eng.collective_compute("AllReduce", ALU.add,
                                           replica_groups=replica_groups.LNC1_1x8,
                                           ins=[idbufd], outs=[idbufr]).then_inc(*inc("sq"))
                SEC("gpsimd", b8_scat)
                SQ_AR = cnt["sq"]

                def b8_ld(eng):
                    eng.wait_ge(sems["sq"], SQ_AR)
                    DMA(eng, "sd", out=UB[:, 1024:1536].rearrange("p (x o) -> p x o", o=4),
                        in_=idbufr.rearrange("(x p) o -> p x o", p=128))
                    DMA(eng, "sd", out=UB[:, 1536:2048].rearrange("p (x o) -> p x o", o=4),
                        in_=gslotd.rearrange("(x p) o -> p x o", p=128))
                SEC("sync", b8_ld)
                SD_IDL = cnt["sd"]

                def b8_mi(eng):
                    eng.wait_ge(sems["sd"], SD_IDL)
                    eng.tensor_copy(UB[:, 0:128],
                                    UB[:, 1024:1536].rearrange("p (x o) -> p x o", o=4)[:, :, 0:1].squeeze())
                    eng.drain()
                    eng.tensor_copy(UB[:, 256:384].rearrange("p (j e) -> p j e", e=8),
                                    UB[:, 0:128].rearrange("p (e j) -> p e j", j=16).transpose([0, 2, 1]))
                    eng.drain()
                    eng.tensor_copy(UB[:, 0:128],
                                    UB[:, 1536:2048].rearrange("p (x o) -> p x o", o=4)[:, :, 0:1].squeeze())
                    eng.drain()
                    eng.tensor_copy(UB[:, 384:512].rearrange("p (b e) -> p b e", e=8),
                                    UB[:, 0:128].rearrange("p (e b) -> p e b", b=16).transpose([0, 2, 1]))
                    eng.drain()
                    eng.tensor_tensor(UB[:, 0:128], UB[:, 256:384], c_myexp[:, :], ALU.mult)
                    eng.drain()
                    eng.tensor_reduce(UB[:, 128:144],
                                      UB[:, 0:128].rearrange("p (j e) -> p j e", e=8),
                                      AX.X, ALU.add)
                    eng.drain()
                    eng.tensor_copy(KIDX[:, 256:272], UB[:, 128:144])
                    eng.drain()
                    eng.tensor_copy(KIDX[:, 272:400], UB[:, 384:512]).then_inc(*inc("sv"))
                    eng.drain()
                SEC("vector", b8_mi)
                SV_MI = cnt["sv"]

                NBLK = 4
                sv_y = {}
                sq_g = {}
                sd_yd = {}
                last_mm1_marker = 0
                for blk in range(NBLK):
                    st_xt = {}
                    sv_xe = {}
                    for sb4 in range(4):
                        def b8_xg(eng, blk=blk, sb4=sb4):
                            if blk == 0 and sb4 == 0:
                                eng.wait_ge(sems["sq"], SQ_AGX)
                                eng.wait_ge(sems["sv"], SV_MI)
                            if blk > 0 and sb4 == 0:
                                eng.wait_ge(sems["st"], last_mm1_marker)
                            if sb4 >= 1:
                                eng.wait_ge(sems["st"], st_xt[sb4 - 1])
                            sbi = blk * 4 + sb4
                            eng.indirect_dma_start(
                                out=vaug[:, :, :].rearrange("p a b -> p (a b)")[:, 0:2048].bitcast(BF16)[:, 0:1024] if False else vaug[:, 0:8, 0:65].rearrange("p a b -> p (a b)")[:, 0:520].bitcast(BF16) if False else xgst,
                                out_offset=None,
                                in_=agx_out,
                                in_offset=IndirectOffsetOnAxis(ap=KIDX[:, 256 + sbi:257 + sbi], axis=0)).then_inc(*inc("sgt", 16))
                            eng.wait_ge(sems["sgt"], cnt["sgt"])
                            eng.nop().then_inc(*inc("sq"))
                        SEC("gpsimd", b8_xg)
                        sq_xg = cnt["sq"]

                        def b8_xt(eng, blk=blk, sb4=sb4, pvy=None):
                            eng.wait_ge(sems["sq"], sq_xg)
                            if sb4 >= 1:
                                eng.wait_ge(sems["sv"], sv_xe[sb4 - 1])
                            if sb4 == 0 and blk > 0:
                                eng.wait_ge(sems["sv"], sv_y[blk - 1])
                            for c in range(8):
                                tr = eng.transpose(psBbf[:, sl(c, 128)], xgst[:, sl(c, 128)],
                                                   c_identbf[:, :])
                                if c == 7:
                                    tr.then_inc(*inc("st"))
                        SEC("tensor", b8_xt)
                        st_xt[sb4] = cnt["st"]

                        def b8_xe(eng, blk=blk, sb4=sb4):
                            eng.wait_ge(sems["st"], st_xt[sb4])
                            eng.tensor_copy(
                                UBbf[:, :].rearrange("p (c t) -> p c t", c=8)[:, :, sl(sb4, 128)],
                                psBbf[:, 0:1024].rearrange("p (c t) -> p c t", c=8)).then_inc(*inc("sv"))
                            eng.drain()
                        SEC("vector", b8_xe)
                        sv_xe[sb4] = cnt["sv"]
                    sq_g[blk] = cnt["sv"]

                    st_mm1 = {}
                    ss_h = {}
                    for fs in range(32):
                        def b8_w1(eng, blk=blk, fs=fs, lastm=last_mm1_marker):
                            if fs >= 2:
                                eng.wait_ge(sems["st"], st_mm1[fs - 2])
                            elif lastm:
                                eng.wait_ge(sems["st"], lastm)
                            else:
                                eng.wait_ge(sems["sd"], B0L)
                                eng.wait_ge(sems["sp"], SP_RWKV)
                            DMA(eng, "sd", out=wslA[fs % 2][:],
                                          in_=w1b[:, sl(fs, 128)].rearrange("(c p) e -> p c e", p=128))
                        SEC("sync", b8_w1)
                        SD_W1 = cnt["sd"]

                        def b8_mm1(eng, blk=blk, fs=fs):
                            eng.wait_ge(sems["sd"], SD_W1)
                            eng.wait_ge(sems["sv"], sq_g[blk])
                            if fs >= 2:
                                eng.wait_ge(sems["ss"], ss_h[fs - 2])
                            if fs < 2 and blk > 0:
                                eng.wait_ge(sems["sv"], sv_y[blk - 1])
                            for c in range(8):
                                mm = eng.matmul(psB[:, 0:512] if fs % 2 == 0 else psBf32[:, 0:512],
                                                wslA[fs % 2][:, c:c + 1, :].squeeze(),
                                                UBbf[:, :].rearrange("p (c t) -> p c t", c=8)[:, c:c + 1, 0:512].squeeze(),
                                                start=(c == 0), stop=(c == 7), skip_group_check=True)
                                if c == 7:
                                    mm.then_inc(*inc("st"))
                        SEC("tensor", b8_mm1)
                        st_mm1[fs] = cnt["st"]

                        def b8_h(eng, blk=blk, fs=fs):
                            eng.wait_ge(sems["st"], st_mm1[fs])
                            eng.activation(out=hbv[:, sl(fs, 512)],
                                           in_=psB[:, 0:512] if fs % 2 == 0 else psBf32[:, 0:512],
                                           func=ACT.Gelu_apprx_tanh,
                                           bias=c_b1r[:, fs:fs + 1]).then_inc(*inc("ss"))
                        SEC("scalar", b8_h)
                        ss_h[fs] = cnt["ss"]
                    last_mm1_marker = st_mm1[31]

                    banks = [psD2[:, sl(0, 512)], psD2[:, sl(1, 512)], psD2[:, sl(2, 512)],
                             psD2[:, sl(3, 512)], psA[:, sl(0, 512)], psA[:, sl(1, 512)],
                             psB[:, 0:512], psBf32[:, 0:512]]
                    st_w2 = {}
                    sq_w2 = {}
                    for fc in range(32):
                        def b8_w2(eng, blk=blk, fc=fc):
                            if fc < 2:
                                eng.wait_ge(sems["ss"], ss_h[31])
                            if fc >= 2:
                                eng.wait_ge(sems["st"], st_w2[fc - 2])
                            DMA(eng, "sd", out=wslB[fc % 2][:],
                                          in_=w2b[sl(fc, 128), :].rearrange("p (c e) -> p c e", c=8))
                        SEC("sync", b8_w2)
                        sd_w2b = cnt["sd"]

                        def b8_mm2(eng, blk=blk, fc=fc):
                            eng.wait_ge(sems["sd"], sd_w2b)
                            if fc == 0:
                                eng.wait_ge(sems["sv"], sv_y.get(blk - 1, SV_IDX))
                            w2v = wslB[fc % 2][:, :, :].rearrange("p c e -> p (c e)")
                            for si in range(4):
                                for sp2 in range(2):
                                    mm = eng.matmul(banks[si * 2 + sp2],
                                                    hbv[:, sl(fc, 512)][:, sl(si, 128)],
                                                    w2v[:, sl(sp2, 512)],
                                                    start=(fc == 0), stop=(fc == 31),
                                                    skip_group_check=True)
                                    if si == 3 and sp2 == 1:
                                        mm.then_inc(*inc("st"))
                        SEC("tensor", b8_mm2)
                        st_w2[fc] = cnt["st"]

                    def b8_y(eng, blk=blk):
                        eng.wait_ge(sems["st"], st_w2[31])
                        if blk > 0:
                            eng.wait_ge(sems["sd"], sd_yd[blk - 1])
                        for si in range(4):
                            for sp2 in range(2):
                                tt2 = eng.tensor_tensor(yv[:, si * 1024 + sp2 * 512: si * 1024 + sp2 * 512 + 512],
                                                        banks[si * 2 + sp2], c_b2b[:, sl(sp2, 512)], ALU.add)
                                if si == 3 and sp2 == 1:
                                    tt2.then_inc(*inc("sv"))
                        eng.drain()
                    SEC("vector", b8_y)
                    sv_y[blk] = cnt["sv"]

                    def b8_yd(eng, blk=blk):
                        eng.wait_ge(sems["sv"], sv_y[blk])
                        DMA(eng, "sd", out=agy_in[sl(blk, 512), :].rearrange("(s p) d -> p s d", p=128),
                                      in_=yv[:, 0:4096].rearrange("p (s d) -> p s d", s=4))
                    SEC("sync", b8_yd)
                    sd_yd[blk] = cnt["sd"]
                SD_Y = cnt["sd"]

                def b8_ag(eng):
                    eng.wait_ge(sems["sd"], SD_Y)
                    eng.collective_compute("AllGather", ALU.bypass,
                                           replica_groups=replica_groups.LNC1_1x8,
                                           ins=[agy_in], outs=[agy_out]).then_inc(*inc("sq"))
                SEC("gpsimd", b8_ag)
                SQ_AGY = cnt["sq"]

                if BSTOP < 8:
                    return
                def b9_g(eng):
                    eng.wait_ge(sems["sv"], SV_TAU)
                    eng.wait_ge(sems["sv"], sv_y[NBLK - 1])
                    eng.matmul(psB[0:8, 20:21], c_ind8f16[:, :], moesm[:, 32:33],
                               start=True, stop=True, skip_group_check=True).then_inc(*inc("st"))
                SEC("tensor", b9_g)
                ST_T8 = cnt["st"]

                def b9_g2v(eng):
                    eng.wait_ge(sems["st"], ST_T8)
                    eng.tensor_copy(moesm[0:8, 42:43], psB[0:8, 20:21]).then_inc(*inc("sv"))
                    eng.drain()
                SEC("vector", b9_g2v)
                SV_T8a = cnt["sv"]

                def b9_g2(eng):
                    eng.wait_ge(sems["sv"], SV_T8a)
                    eng.transpose(psB[0:1, 24:32], moesm[0:8, 42:43], c_ident[0:8, 0:8]).then_inc(*inc("st"))
                SEC("tensor", b9_g2)
                ST_T8b = cnt["st"]

                def b9_g3(eng):
                    eng.wait_ge(sems["st"], ST_T8b)
                    eng.tensor_copy(moesm[0:1, 44:52], psB[0:1, 24:32]).then_inc(*inc("sv"))
                    eng.drain()
                SEC("vector", b9_g3)
                SV_T8 = cnt["sv"]

                def b9_g4(eng):
                    eng.wait_ge(sems["sv"], SV_T8)
                    eng.matmul(psB[:, 24:32], c_ones1f[0:1, 0:128], moesm[0:1, 44:52],
                               start=True, stop=True, skip_group_check=True).then_inc(*inc("st"))
                SEC("tensor", b9_g4)
                ST_T8c = cnt["st"]

                def b9_g5(eng):
                    eng.wait_ge(sems["st"], ST_T8c)
                    eng.tensor_copy(moesm[:, 52:60], psB[:, 24:32])
                    eng.drain()
                    pv = probs[:, 0:128].rearrange("p (b e) -> p b e", e=8)
                    tb = moesm[:, 52:60].rearrange("p (o e) -> p o e", o=1).broadcast_to([128, 16, 8])
                    eng.tensor_tensor(UB[:, 0:128].rearrange("p (b e) -> p b e", e=8), pv, tb, ALU.is_ge)
                    eng.drain()
                    eng.tensor_tensor(probs[:, 128:256], probs[:, 0:128], UB[:, 0:128], ALU.mult).then_inc(*inc("sv"))
                    eng.drain()
                SEC("vector", b9_g5)
                SV_GV = cnt["sv"]

                gv = probs[:, 128:256].rearrange("p (b e) -> p b e", e=8)
                hbv3 = hbv[:, :].rearrange("p (b d) -> p b d", b=16)
                sq_gt = {}
                sv_acch = {}
                for ee in range(E):
                    for hh in range(2):
                        def b9_gt(eng, ee=ee, hh=hh):
                            if ee == 0 and hh == 0:
                                eng.wait_ge(sems["sq"], SQ_AGY)
                            if ee >= 1:
                                eng.wait_ge(sems["sv"], sv_acch[(ee - 1, hh)])
                            for b in range(hh * 8, hh * 8 + 8):
                                eng.indirect_dma_start(
                                    out=hbv3[:, b:b + 1, :].squeeze(),
                                    out_offset=None,
                                    in_=agy_out,
                                    in_offset=IndirectOffsetOnAxis(ap=KIDX[:, 272 + b * 8 + ee:273 + b * 8 + ee], axis=0)).then_inc(*inc("sgt", 16))
                            eng.wait_ge(sems["sgt"], cnt["sgt"])
                            eng.nop().then_inc(*inc("sq"))
                        SEC("gpsimd", b9_gt)
                        sq_gt[(ee, hh)] = cnt["sq"]

                        def b9_acc(eng, ee=ee, hh=hh):
                            eng.wait_ge(sems["sq"], sq_gt[(ee, hh)])
                            if ee == 0 and hh == 0:
                                eng.wait_ge(sems["sv"], SV_GV)
                            for b in range(hh * 8, hh * 8 + 8):
                                stt = eng.scalar_tensor_tensor(blk2(stream, b),
                                                               hbv3[:, b:b + 1, :].squeeze(),
                                                               probs[:, 128 + b * 8 + ee:129 + b * 8 + ee],
                                                               blk2(stream, b), ALU.mult, ALU.add)
                            stt.then_inc(*inc("sv"))
                            eng.drain()
                        SEC("vector", b9_acc)
                        sv_acch[(ee, hh)] = cnt["sv"]
                sv_acc = {E - 1: sv_acch[(E - 1, 1)]}

                if BSTOP < 9:
                    return
                def b10_s(eng):
                    eng.wait_ge(sems["sv"], sv_acc[E - 1])
                    for b in range(0, NB, 2):
                        eng.bn_stats(smallv[:, 0:6], blk2(stream, b)[:, 0:512])
                        eng.bn_stats(smallv[:, 6:12], blk2(stream, b)[:, 512:1024])
                        eng.bn_stats(smallv[:, 80:86], blk2(stream, b + 1)[:, 0:512])
                        eng.bn_stats(smallv[:, 86:92], blk2(stream, b + 1)[:, 512:1024])
                        eng.drain()
                        eng.bn_aggr(smallv[:, 2 * b + 16:2 * b + 18], smallv[:, 0:12])
                        ag = eng.bn_aggr(smallv[:, 2 * b + 18:2 * b + 20], smallv[:, 80:92])
                        if b == NB - 2:
                            ag.then_inc(*inc("sv"))
                        eng.drain()
                SEC("vector", b10_s)
                SV_B10 = cnt["sv"]

                def b10_q(eng):
                    eng.wait_ge(sems["sv"], SV_B10)
                    for b in range(NB):
                        aa = eng.activation(out=smallv[:, 48 + b:49 + b], in_=smallv[:, 2 * b + 17:2 * b + 18],
                                            func=ACT.Sqrt, bias=c_cv[:, 0:1])
                        if b == NB - 1:
                            aa.then_inc(*inc("ss"))
                    eng.drain()
                SEC("scalar", b10_q)
                SS_B10 = cnt["ss"]

                def b10_n(eng):
                    eng.wait_ge(sems["ss"], SS_B10)
                    eng.reciprocal(smallv[:, 64:80], smallv[:, 48:64])
                    eng.drain()
                    for b in range(NB):
                        tsn = eng.tensor_scalar(blk2(stream, b), blk2(stream, b),
                                                smallv[:, 2 * b + 16:2 * b + 17], smallv[:, 64 + b:65 + b],
                                                ALU.subtract, ALU.mult)
                        if b == NB - 1:
                            tsn.then_inc(*inc("sv"))
                    eng.drain()
                SEC("vector", b10_n)


            emit_phase_b()

            def emit_postln(wsv, wsq=0):
                def pl_s(eng):
                    eng.wait_ge(sems["sv"], wsv)
                    if wsq:
                        eng.wait_ge(sems["sq"], wsq)
                    for b in range(0, NB, 2):
                        eng.bn_stats(smallv[:, 0:6], blk2(stream, b)[:, 0:512])
                        eng.bn_stats(smallv[:, 6:12], blk2(stream, b)[:, 512:1024])
                        eng.bn_stats(smallv[:, 80:86], blk2(stream, b + 1)[:, 0:512])
                        eng.bn_stats(smallv[:, 86:92], blk2(stream, b + 1)[:, 512:1024])
                        eng.drain()
                        eng.bn_aggr(smallv[:, 2 * b + 16:2 * b + 18], smallv[:, 0:12])
                        ag = eng.bn_aggr(smallv[:, 2 * b + 18:2 * b + 20], smallv[:, 80:92])
                        if b == NB - 2:
                            ag.then_inc(*inc("sv"))
                        eng.drain()
                SEC("vector", pl_s)
                svs = cnt["sv"]

                def pl_q(eng):
                    eng.wait_ge(sems["sv"], svs)
                    for b in range(NB):
                        aa = eng.activation(out=smallv[:, 48 + b:49 + b], in_=smallv[:, 2 * b + 17:2 * b + 18],
                                            func=ACT.Sqrt, bias=c_cv[:, 0:1])
                        if b == NB - 1:
                            aa.then_inc(*inc("ss"))
                    eng.drain()
                SEC("scalar", pl_q)
                sss = cnt["ss"]

                def pl_n(eng):
                    eng.wait_ge(sems["ss"], sss)
                    eng.reciprocal(smallv[:, 64:80], smallv[:, 48:64])
                    eng.drain()
                    for b in range(NB):
                        tsn = eng.tensor_scalar(blk2(stream, b), blk2(stream, b),
                                                smallv[:, 2 * b + 16:2 * b + 17], smallv[:, 64 + b:65 + b],
                                                ALU.subtract, ALU.mult)
                        if b == NB - 1:
                            tsn.then_inc(*inc("sv"))
                    eng.drain()
                SEC("vector", pl_n)
                return cnt["sv"]

            def emit_phase_cd():
                if PHASES < 3:
                    return
                psD2 = _ph.get("psD2")
                if psD2 is None:
                    return
                CSTOP = int(os.environ.get("MK_CSTOP", "99"))
                DSTOP = int(os.environ.get("MK_DSTOP", "99"))
                SVB = cnt["sv"]
                STB = cnt["st"]

                lnTf = lnT[:, :, :].rearrange("p c t -> p (c t)")
                vaugf = vaug[:, :, :].rearrange("p a b -> p (a b)")
                sA0, sA1 = scrA[:, 0:1024], scrA[:, 1024:2048]
                sU0, sU1 = UB[:, 0:1024], UB[:, 1024:2048]
                sZ0, sZ1 = ZB[:, 0:1024], ZB[:, 1024:2048]
                gidbf = c_gidxp1[:, :].bitcast(BF16)
                gkob = [c_tidg[:, :].bitcast(BF16), vaugf[:, 0:1024]]
                zcol = vaugf[:, 1024:1025]
                wob = c_bout[:, :].bitcast(BF16)
                psBo = psBbf2 = psBbf[:, :].bitcast(F32)
                mixK = [TA[:, 0:1024], TA[:, 1024:2048], TB[:, 0:1024], TB[:, 1024:2048]]
                mixV = [SQ2[:, 0:1024], SQ2[:, 1024:2048], KQSQ[:, 0:1024], KQSQ[:, 1024:2048]]
                mixR = [wsc[:, 0:1024], wsc[:, 1024:2048], gidbf[:, 0:1024], gidbf[:, 1024:2048]]

                def x2hi(dc, b2):
                    return lnT[:, dc:dc + 1, b2 * 1024:(b2 + 1) * 1024].squeeze()

                def c0_d(eng):
                    eng.wait_ge(sems["sv"], SVB)
                    DMA(eng, "sd", out=moesm[:, 16:24], in_=mixk_r)
                    DMA(eng, "sd", out=moesm[:, 24:32], in_=mixv_r)
                    DMA(eng, "sd", out=moesm[:, 32:40], in_=mixr_r)
                    DMA(eng, "sd", out=smallv[:, 80:88], in_=decay_r)
                    DMA(eng, "sd", out=smallv[:, 88:96], in_=first_r)
                SEC("sync", c0_d)
                SD_C0 = cnt["sd"]

                def c0_s(eng):
                    eng.wait_ge(sems["sd"], SD_C0)
                    eng.activation(out=moesm[:, 40:48], in_=smallv[:, 80:88],
                                   func=ACT.Exp, scale=1.0 / T).then_inc(*inc("ss"))
                    eng.drain()
                SEC("scalar", c0_s)
                SS_C0E = cnt["ss"]

                def c0_v(eng):
                    eng.wait_ge(sems["sd"], SD_C0)
                    eng.tensor_scalar(moesm[:, 48:56], smallv[:, 88:96], 1.0 / T, None, ALU.mult)
                    eng.drain()
                    eng.memset(moesm[:, 0:16], 0.0)
                    eng.drain()
                    eng.memset(zcol, 0.0)
                    eng.drain()
                    eng.nop().then_inc(*inc("sv"))
                SEC("vector", c0_v)
                SV_C0 = cnt["sv"]

                st_c1 = {}
                ss_c1 = {}
                for b in range(NB):
                    def c1_tr(eng, b=b):
                        if b == 0:
                            eng.wait_ge(sems["sv"], SVB)
                        else:
                            eng.wait_ge(sems["ss"], ss_c1[b - 1])
                        for c in range(8):
                            tr = eng.transpose(psA[:, sl(c, 128)], blk2(stream, b)[:, sl(c, 128)], c_ident[:])
                            if c == 7:
                                tr.then_inc(*inc("st"))
                    SEC("tensor", c1_tr)
                    st_c1[b] = cnt["st"]

                    def c1_ev(eng, b=b):
                        eng.wait_ge(sems["st"], st_c1[b])
                        eng.activation(out=lnT[:, :, sl(b, 128)],
                                       in_=psA[:, 0:1024].rearrange("p (c t) -> p c t", c=8),
                                       func=ACT.Copy).then_inc(*inc("ss"))
                        eng.drain()
                    SEC("scalar", c1_ev)
                    ss_c1[b] = cnt["ss"]
                SS_C1 = cnt["ss"]

                if CSTOP < 2:
                    return

                mix_sv = {}
                sd_wsl = {}
                st_k = {}
                st_v = {}
                st_r = {}
                st_o_ts = {}
                st_o_last = {}
                ss_ke = {}
                ekuv_sv = {}
                den_sv = {}
                sr_ss = {}
                gko_sv = {}

                def c_ld(eng, j):
                    if j == 0:
                        eng.wait_ge(sems["sp"], SP_RWKV)
                        eng.wait_ge(sems["st"], STB)
                        eng.wait_ge(sems["sv"], SVB)
                    else:
                        eng.wait_ge(sems["st"], st_r.get(j - 2, STB))
                        eng.wait_ge(sems["st"], st_o_last.get(j - 2, STB))
                    co = j % 8
                    DMA(eng, "sd", out=wslA[j % 2][:],
                        in_=rwkvb[0:D, sl(co, 128)].rearrange("(d p) e -> p d e", p=128))
                    DMA(eng, "sd", out=wslB[j % 2][:],
                        in_=rwkvb[D:2 * D, sl(co, 128)].rearrange("(d p) e -> p d e", p=128))
                    DMA(eng, "sd", out=wslC[j % 2][:],
                        in_=rwkvb[2 * D:3 * D, sl(co, 128)].rearrange("(d p) e -> p d e", p=128))
                    DMA(eng, "sd", out=wob[:, sl(j % 2, 1024)],
                        in_=rwkvb[3 * D + co * 128:3 * D + (co + 1) * 128, :])

                def c_ld0(eng):
                    c_ld(eng, 0)
                SEC("sync", c_ld0)
                sd_wsl[0] = cnt["sd"]

                for gi in range(17):
                    b2, co = gi // 8, gi % 8
                    sv0 = cnt["sv"]

                    if gi >= 1:
                        def c_o(eng, j=gi - 1, sv0=sv0):
                            eng.wait_ge(sems["sv"], gko_sv[j])
                            for ts in range(8):
                                if ts >= 1:
                                    eng.wait_ge(sems["sv"], sv0 + ts)
                                eng.matmul(psB[:, 0:512], gkob[j % 2][:, sl(ts, 128)],
                                           wob[:, (j % 2) * 1024:(j % 2) * 1024 + 512],
                                           start=True, stop=True, skip_group_check=True)
                                mm = eng.matmul(psBo[:, 0:512], gkob[j % 2][:, sl(ts, 128)],
                                                wob[:, (j % 2) * 1024 + 512:(j % 2) * 1024 + 1024],
                                                start=True, stop=True, skip_group_check=True)
                                mm.then_inc(*inc("st"))
                                st_o_ts[(j, ts)] = cnt["st"]
                        SEC("tensor", c_o)
                        st_o_last[gi - 1] = cnt["st"]

                        def c_oadd(eng, j=gi - 1):
                            for ts in range(8):
                                eng.wait_ge(sems["st"], st_o_ts[(j, ts)])
                                tb = (j // 8) * 8 + ts
                                eng.tensor_tensor(blk2(stream, tb)[:, 0:512], psB[:, 0:512],
                                                  blk2(stream, tb)[:, 0:512], ALU.add)
                                eng.tensor_tensor(blk2(stream, tb)[:, 512:1024], psBo[:, 0:512],
                                                  blk2(stream, tb)[:, 512:1024], ALU.add).then_inc(*inc("sv"))
                            eng.drain()
                        SEC("vector", c_oadd)

                    if gi == 16:
                        break

                    if co == 0:
                        def c_mx(eng, b2=b2, gi=gi):
                            if b2 == 0:
                                eng.wait_ge(sems["ss"], SS_C1)
                                eng.wait_ge(sems["sd"], SD_C0)
                                eng.wait_ge(sems["sv"], SV_C0)
                            else:
                                eng.wait_ge(sems["st"], st_o_last[gi - 1])
                            dtile = gkob[0]
                            for c in range(4):
                                x2c = lnT[:, c:c + 1, :].squeeze()
                                if c < 2:
                                    if b2 == 0:
                                        eng.tensor_tensor(dtile[:, 1:1024], x2c[:, 1:1024],
                                                          x2c[:, 0:1023], ALU.subtract)
                                        eng.drain()
                                        eng.tensor_copy(dtile[:, 0:1], x2c[:, 0:1])
                                        eng.drain()
                                        xxm = x2c[:, 0:1023]
                                        rng = (1, 1024)
                                        edge = 0
                                    else:
                                        eng.tensor_tensor(dtile[:, 0:1024], x2c[:, 1024:2048],
                                                          x2c[:, 1023:2047], ALU.subtract)
                                        eng.drain()
                                        xxm = x2c[:, 1023:2047]
                                        rng = (0, 1024)
                                        edge = None
                                else:
                                    if b2 == 0:
                                        eng.tensor_tensor(dtile[:, 0:1024], x2c[:, 0:1024],
                                                          x2c[:, 1:1025], ALU.subtract)
                                        eng.drain()
                                        xxm = x2c[:, 1:1025]
                                        rng = (0, 1024)
                                        edge = None
                                    else:
                                        eng.tensor_tensor(dtile[:, 0:1023], x2c[:, 1024:2047],
                                                          x2c[:, 1025:2048], ALU.subtract)
                                        eng.drain()
                                        eng.tensor_copy(dtile[:, 1023:1024], x2c[:, 2047:2048])
                                        eng.drain()
                                        xxm = x2c[:, 1025:2048]
                                        rng = (0, 1023)
                                        edge = 1023
                                a, bb = rng
                                for mi, mcol, mt in ((0, 16, mixK[c]), (1, 24, mixV[c]), (2, 32, mixR[c])):
                                    sc = moesm[:, mcol + c:mcol + c + 1]
                                    eng.scalar_tensor_tensor(mt[:, a:bb], dtile[:, a:bb], sc, xxm,
                                                             ALU.mult, ALU.add)
                                    eng.drain()
                                    if edge is not None:
                                        stt = eng.scalar_tensor_tensor(mt[:, edge:edge + 1],
                                                                       dtile[:, edge:edge + 1], sc, zcol,
                                                                       ALU.mult, ALU.add)
                                        eng.drain()
                                    else:
                                        stt = None
                            eng.nop().then_inc(*inc("sv"))
                        SEC("vector", c_mx)
                        mix_sv[b2] = cnt["sv"]

                    if gi + 1 < 16:
                        def c_ldn(eng, j=gi + 1):
                            c_ld(eng, j)
                        SEC("sync", c_ldn)
                        sd_wsl[gi + 1] = cnt["sd"]

                    if PHASES >= 4:
                        def c_dc(eng, gi=gi):
                            DMA(eng, "sp", out=wfcb[sl(gi, 64), :], in_=wfcT[sl(gi, 64), :])
                            DMA(eng, "sp", out=wprojb[sl(gi, 256), :], in_=wprojT[sl(gi, 256), :])
                        SEC("gpsimd", c_dc)

                    def c_proj(eng, gi=gi, b2=b2):
                        eng.wait_ge(sems["sd"], sd_wsl[gi])
                        eng.wait_ge(sems["sv"], mix_sv[b2])
                        if gi >= 1:
                            eng.wait_ge(sems["sv"], ekuv_sv[gi - 1])
                            eng.wait_ge(sems["ss"], sr_ss[gi - 1])
                        for wt, mixd, pdst, stn in ((wslA, mixK, psD2[:, 0:1024], st_k),
                                                    (wslB, mixV, psD2[:, 1024:2048], st_v),
                                                    (wslC, mixR, psA[:, 0:1024], st_r)):
                            for sp in range(2):
                                for dc in range(8):
                                    rhs = (mixd[dc] if dc < 4 else x2hi(dc, b2))[:, sl(sp, 512)]
                                    mm = eng.matmul(pdst[:, sl(sp, 512)], wt[gi % 2][:, dc:dc + 1, :].squeeze(),
                                                    rhs, start=(dc == 0), stop=(dc == 7),
                                                    skip_group_check=True)
                                    if sp == 1 and dc == 7:
                                        mm.then_inc(*inc("st"))
                            stn[gi] = cnt["st"]
                    SEC("tensor", c_proj)

                    def c_s1(eng, gi=gi, co=co):
                        eng.wait_ge(sems["st"], st_k[gi])
                        if gi == 0:
                            eng.wait_ge(sems["ss"], SS_C0E)
                        eng.activation(out=sA0, in_=psD2[:, 0:1024], func=ACT.Exp,
                                       bias=moesm[:, 48 + co:49 + co])
                        eng.drain()
                        eng.activation(out=sA1, in_=psD2[:, 0:1024], func=ACT.Exp).then_inc(*inc("ss"))
                        eng.drain()
                    SEC("scalar", c_s1)
                    ss_ke[gi] = cnt["ss"]

                    def c_el(eng, gi=gi, co=co):
                        eng.wait_ge(sems["ss"], ss_ke[gi])
                        eng.wait_ge(sems["st"], st_v[gi])
                        if gi == 0:
                            eng.wait_ge(sems["sv"], SV_C0)
                        eng.tensor_tensor(sU0, sA1, psD2[:, 1024:2048], ALU.mult)
                        eng.drain()
                        eng.tensor_tensor(sU1, sA0, psD2[:, 1024:2048], ALU.mult).then_inc(*inc("sv"))
                        eng.drain()
                        eng.tensor_copy(moesm[:, 56:57], moesm[:, co:co + 1])
                        eng.drain()
                        eng.tensor_copy(moesm[:, 57:58], moesm[:, 8 + co:9 + co])
                        eng.drain()
                        ewb = moesm[:, 40 + co:41 + co].broadcast_to([128, 1024])
                        eng.tensor_tensor_scan(sZ0, ewb, sU0, moesm[:, co:co + 1], ALU.mult, ALU.add)
                        eng.drain()
                        eng.tensor_tensor_scan(sZ1, ewb, sA1, moesm[:, 8 + co:9 + co], ALU.mult, ALU.add)
                        eng.drain()
                        eng.tensor_copy(moesm[:, co:co + 1], sZ0[:, 1023:1024])
                        eng.drain()
                        eng.tensor_copy(moesm[:, 8 + co:9 + co], sZ1[:, 1023:1024])
                        eng.drain()
                        eng.tensor_tensor(sA1[:, 1:1024], sU1[:, 1:1024], sZ0[:, 0:1023], ALU.add)
                        eng.drain()
                        eng.tensor_tensor(sA1[:, 0:1], sU1[:, 0:1], moesm[:, 56:57], ALU.add)
                        eng.drain()
                        eng.tensor_tensor(sU0[:, 1:1024], sA0[:, 1:1024], sZ1[:, 0:1023], ALU.add)
                        eng.drain()
                        eng.tensor_tensor(sU0[:, 0:1], sA0[:, 0:1], moesm[:, 57:58], ALU.add).then_inc(*inc("sv"))
                        eng.drain()
                    SEC("vector", c_el)
                    den_sv[gi] = cnt["sv"]
                    ekuv_sv[gi] = den_sv[gi] - 1

                    def c_s2(eng, gi=gi):
                        eng.wait_ge(sems["sv"], den_sv[gi])
                        eng.activation(out=sZ0, in_=sU0, func=ACT.Ln)
                        eng.drain()
                        eng.activation(out=sZ1, in_=sZ0, func=ACT.Exp, scale=-1.0)
                        eng.drain()
                        eng.wait_ge(sems["st"], st_r[gi])
                        eng.activation(out=sA0, in_=psA[:, 0:1024], func=ACT.Sigmoid).then_inc(*inc("ss"))
                        eng.drain()
                    SEC("scalar", c_s2)
                    sr_ss[gi] = cnt["ss"]

                    def c_gk(eng, gi=gi):
                        eng.wait_ge(sems["ss"], sr_ss[gi])
                        if gi >= 2:
                            eng.wait_ge(sems["st"], st_o_last[gi - 2])
                        eng.tensor_tensor(sU1, sA1, sZ1, ALU.mult)
                        eng.drain()
                        eng.tensor_tensor(gkob[gi % 2], sU1, sA0, ALU.mult).then_inc(*inc("sv"))
                        eng.drain()
                    SEC("vector", c_gk)
                    gko_sv[gi] = cnt["sv"]

                SP_ALL = cnt["sp"]
                SV_CL = cnt["sv"]
                LNC_sv = emit_postln(SV_CL)
                STC_LAST = cnt["st"]

                if PHASES < 4 or CSTOP < 3:
                    return

                x3Tv = UB[:, :].bitcast(BF16).rearrange("p (c t) -> p c t", t=512)
                wpb = [c_tidg[:, :].bitcast(BF16), vaugf[:, 0:1024]]

                pd_ts = [psA[:, 0:1024], psD2[:, 0:1024], psD2[:, 1024:2048], None]
                ppiece = [[(pd_ts[0], 0, 1024)], [(pd_ts[1], 0, 1024)], [(pd_ts[2], 0, 1024)],
                          [(psB[:, 0:512], 0, 512), (psBo[:, 0:512], 512, 512)]]
                KQf32 = KQSQ[:, :].bitcast(F32)
                dscanM = [scrA[:, 0:1024], scrA[:, 1024:2048], ZB[:, 0:1024], ZB[:, 1024:2048]]
                dscanO = [UB[:, 0:1024], UB[:, 1024:2048], c_gidxp1[:, 0:1024], c_bout[:, 0:1024]]
                yadjs = [TA[:, 0:1024], TA[:, 1024:2048], TB[:, 0:1024], TB[:, 1024:2048]]
                spks = [wsc[:, 0:1024], wsc[:, 1024:2048], SQ2[:, 0:1024], SQ2[:, 1024:2048]]

                def emit_scan_iters(dd):
                    Mset = list(dscanM)
                    Oset = list(dscanO)
                    sv_mem = {}
                    ss_sign = {}
                    for k in range(2, 6):
                        def d_vit(eng, dd=dd, k=k, Mset=Mset[:], Oset=Oset[:]):
                            if k == 2:
                                eng.wait_ge(sems["ss"], ss_init[dd])
                            else:
                                eng.wait_ge(sems["ss"], ss_sign[k - 1])
                            for i in range(4):
                                eng.scalar_tensor_tensor(blk2(stream, 4 * dd + i), spks[i], 0.1,
                                                         blk2(stream, 4 * dd + i), ALU.mult, ALU.add)
                            for i in range(4):
                                eng.scalar_tensor_tensor(Oset[i], spks[i], -0.5, yadjs[i],
                                                         ALU.mult, ALU.add)
                            eng.drain()
                            for i in range(4):
                                mmo = eng.scalar_tensor_tensor(Oset[i], Mset[i], 0.9, Oset[i],
                                                               ALU.mult, ALU.add)
                            mmo.then_inc(*inc("sv"))
                            eng.drain()
                        SEC("vector", d_vit)
                        sv_mem[k] = cnt["sv"]

                        def d_sg(eng, dd=dd, k=k, Oset=Oset[:]):
                            eng.wait_ge(sems["sv"], sv_mem[k])
                            for i in range(4):
                                sg = eng.activation(out=spks[i], in_=Oset[i], func=ACT.Sign,
                                                    bias=c_cv[:, 3:4])
                            sg.then_inc(*inc("ss"))
                        SEC("scalar", d_sg)
                        ss_sign[k] = cnt["ss"]

                        Mset[:], Oset[:] = Oset[:], Mset[:]

                    def d_res5(eng, dd=dd):
                        eng.wait_ge(sems["ss"], ss_sign[5])
                        for i in range(4):
                            rr = eng.scalar_tensor_tensor(blk2(stream, 4 * dd + i), spks[i], 0.1,
                                                          blk2(stream, 4 * dd + i), ALU.mult, ALU.add)
                        rr.then_inc(*inc("sv"))
                        eng.drain()
                    SEC("vector", d_res5)
                    scan_end[dd] = cnt["sv"]

                st_dtr = {}
                ss_x3 = {}
                sd_w1 = {}
                st_mm1 = {}
                ss_r1 = {}
                sv_h = {}
                sd_w2 = {}
                st_mm2 = {}
                ss_init = {}
                scan_end = {}

                for d in range(min(4, DSTOP)):
                    for bl in range(4):
                        trps = psA if bl % 2 == 0 else psD2
                        def d_tr(eng, d=d, bl=bl, trps=trps):
                            if bl == 0:
                                if d == 0:
                                    eng.wait_ge(sems["sv"], LNC_sv)
                                else:
                                    eng.wait_ge(sems["ss"], ss_init[d - 1])
                            elif bl >= 2:
                                eng.wait_ge(sems["ss"], ss_x3[(d, bl - 2)])
                            for c in range(8):
                                tr = eng.transpose(trps[:, sl(c, 128)],
                                                   blk2(stream, 4 * d + bl)[:, sl(c, 128)], c_ident[:])
                                if c == 7:
                                    tr.then_inc(*inc("st"))
                        SEC("tensor", d_tr)
                        st_dtr[(d, bl)] = cnt["st"]

                        def d_x3(eng, d=d, bl=bl, trps=trps):
                            eng.wait_ge(sems["st"], st_dtr[(d, bl)])
                            eng.activation(out=x3Tv[:, :, sl(bl, 128)],
                                           in_=trps[:, 0:1024].rearrange("p (c t) -> p c t", c=8),
                                           func=ACT.Copy).then_inc(*inc("ss"))
                            eng.drain()
                        SEC("scalar", d_x3)
                        ss_x3[(d, bl)] = cnt["ss"]

                    def d_w1f(eng, d=d):
                        for fs in range(2):
                            if d == 0:
                                eng.wait_ge(sems["sp"], SP_ALL)
                                eng.wait_ge(sems["st"], STC_LAST)
                            else:
                                eng.wait_ge(sems["st"], st_mm1[(d - 1, 30 + fs)])
                            DMA(eng, "sd", out=wslA[fs % 2][:],
                                in_=wfcb[:, sl(fs, 128)].rearrange("(d p) e -> p d e", p=128))
                            sd_w1[(d, fs)] = cnt["sd"]
                    SEC("sync", d_w1f)

                    for fs in range(32):
                        def d_mm1(eng, d=d, fs=fs):
                            eng.wait_ge(sems["sd"], sd_w1[(d, fs)])
                            eng.wait_ge(sems["ss"], ss_x3[(d, 3)])
                            if fs >= 4:
                                eng.wait_ge(sems["sv"], sv_h[(d, fs - 4)])
                            elif d >= 1:
                                eng.wait_ge(sems["ss"], ss_init[d - 1])
                            for dc in range(8):
                                mm = eng.matmul(psD2[:, sl(fs % 4, 512)], wslA[fs % 2][:, dc:dc + 1, :].squeeze(),
                                                x3Tv[:, dc:dc + 1, :].squeeze(), start=(dc == 0), stop=(dc == 7),
                                                skip_group_check=True)
                                if dc == 7:
                                    mm.then_inc(*inc("st"))
                        SEC("tensor", d_mm1)
                        st_mm1[(d, fs)] = cnt["st"]

                        if fs + 2 < 32:
                            def d_w1n(eng, d=d, fs=fs + 2):
                                eng.wait_ge(sems["st"], st_mm1[(d, fs - 2)])
                                DMA(eng, "sd", out=wslA[fs % 2][:],
                                    in_=wfcb[:, sl(fs, 128)].rearrange("(d p) e -> p d e", p=128))
                            SEC("sync", d_w1n)
                            sd_w1[(d, fs + 2)] = cnt["sd"]

                        def d_relu(eng, d=d, fs=fs):
                            eng.wait_ge(sems["st"], st_mm1[(d, fs)])
                            if fs >= 2:
                                eng.wait_ge(sems["sv"], sv_h[(d, fs - 2)])
                            eng.activation(out=KQf32[:, sl(fs % 2, 512)],
                                           in_=psD2[:, sl(fs % 4, 512)],
                                           func=ACT.Relu).then_inc(*inc("ss"))
                        SEC("scalar", d_relu)
                        ss_r1[(d, fs)] = cnt["ss"]

                        def d_h(eng, d=d, fs=fs):
                            eng.wait_ge(sems["ss"], ss_r1[(d, fs)])
                            eng.tensor_tensor(lnTf[:, sl(fs, 512)],
                                              KQf32[:, sl(fs % 2, 512)],
                                              psD2[:, sl(fs % 4, 512)], ALU.mult).then_inc(*inc("sv"))
                            eng.drain()
                        SEC("vector", d_h)
                        sv_h[(d, fs)] = cnt["sv"]

                    if d >= 1:
                        emit_scan_iters(d - 1)

                    def d_w2f(eng, d=d):
                        for fc in range(2):
                            if d == 0:
                                eng.wait_ge(sems["sv"], LNC_sv)
                            else:
                                eng.wait_ge(sems["st"], st_mm2[(d - 1, 30 + fc)])
                            DMA(eng, "sd", out=wpb[fc % 2], in_=wprojb[sl(fc, 128), :])
                            sd_w2[(d, fc)] = cnt["sd"]
                    SEC("sync", d_w2f)

                    pd_ts = [psA[:, 0:1024], psD2[:, 0:1024], psD2[:, 1024:2048], None]
                    for fc in range(32):
                        def d_mm2(eng, d=d, fc=fc):
                            eng.wait_ge(sems["sd"], sd_w2[(d, fc)])
                            if fc == 0:
                                eng.wait_ge(sems["sv"], sv_h[(d, 31)])
                                if d >= 1:
                                    eng.wait_ge(sems["ss"], ss_init[d - 1])
                            for ts in range(4):
                                for sp in range(2):
                                    if ts < 3:
                                        dst = pd_ts[ts][:, sl(sp, 512)]
                                    else:
                                        dst = psB[:, 0:512] if sp == 0 else psBo[:, 0:512]
                                    mm = eng.matmul(dst, lnTf[:, fc * 512 + ts * 128:fc * 512 + ts * 128 + 128],
                                                    wpb[fc % 2][:, sl(sp, 512)], start=(fc == 0), stop=(fc == 31),
                                                    skip_group_check=True)
                                    if ts == 3 and sp == 1:
                                        mm.then_inc(*inc("st"))
                        SEC("tensor", d_mm2)
                        st_mm2[(d, fc)] = cnt["st"]

                        if fc + 2 < 32:
                            def d_w2n(eng, d=d, fc=fc + 2):
                                eng.wait_ge(sems["st"], st_mm2[(d, fc - 2)])
                                DMA(eng, "sd", out=wpb[fc % 2], in_=wprojb[sl(fc, 128), :])
                            SEC("sync", d_w2n)
                            sd_w2[(d, fc + 2)] = cnt["sd"]

                    def d_init(eng, d=d):
                        eng.wait_ge(sems["st"], st_mm2[(d, 31)])
                        if d >= 1:
                            eng.wait_ge(sems["sv"], scan_end[d - 1])
                        for i in range(4):
                            for pap, off, w in ppiece[i]:
                                eng.activation(out=dscanM[i][:, off:off + w], in_=pap, func=ACT.Copy)
                                eng.activation(out=yadjs[i][:, off:off + w], in_=pap,
                                               func=ACT.Identity, bias=c_cv[:, 4:5])
                                sg = eng.activation(out=spks[i][:, off:off + w], in_=pap,
                                                    func=ACT.Sign, bias=c_cv[:, 3:4])
                        sg.then_inc(*inc("ss"))
                    SEC("scalar", d_init)
                    ss_init[d] = cnt["ss"]

                if DSTOP >= 1:
                    emit_scan_iters(min(4, DSTOP) - 1)
                if DSTOP >= 4:
                    emit_postln(scan_end[3])

            emit_phase_cd()

            SV_FIN = cnt["sv"]
            SS_FIN = cnt["ss"]
            SQ_FIN = cnt["sq"]

            def a_out(eng):
                eng.wait_ge(sems["sv"], SV_FIN)
                eng.wait_ge(sems["ss"], SS_FIN)
                eng.wait_ge(sems["sq"], SQ_FIN)
                DMA(eng, "so", out=out_d.rearrange("(b p) d -> p b d", p=128), in_=stream[:])
                eng.wait_ge(sems["so"], cnt["so"])
            SEC("sync", a_out)


    return nc


def _prep_inputs(inputs):
    f32 = np.float32
    import ml_dtypes
    bf = ml_dtypes.bfloat16
    X = np.ascontiguousarray(inputs["L0_attn_X"], dtype=f32)
    W_in = np.ascontiguousarray(inputs["L0_attn_W_in"], dtype=f32)
    b_in = np.ascontiguousarray(inputs["L0_attn_b_in"], dtype=f32)
    W_out = np.ascontiguousarray(inputs["L0_attn_W_out"], dtype=f32)
    b_out = np.ascontiguousarray(inputs["L0_attn_b_out"], dtype=f32)

    order = []
    for h in range(H):
        base = h * 4 * DH
        order += list(range(base + 2 * DH, base + 3 * DH))
        order += list(range(base + 0 * DH, base + 1 * DH))
        order += list(range(base + 3 * DH, base + 4 * DH))
        order += list(range(base + 1 * DH, base + 2 * DH))
    order = np.array(order)
    winT = np.ascontiguousarray(W_in[order].T)
    binr = np.ascontiguousarray(b_in[order].reshape(32, 128).T)
    woutT = np.ascontiguousarray(W_out.T.reshape(H, DH, D))
    bout_b = np.ascontiguousarray(np.broadcast_to(b_out, (128, D)))

    ident = np.eye(128, dtype=f32)
    ident_bf = ident.astype(bf)
    csel = np.zeros((128, 4), dtype=bf)
    csel[64:128, 0] = 1.0
    csel[64:128, 3] = 1.0
    ones1f = np.ones((1, 128), dtype=f32)
    cvals = np.zeros((128, 8), dtype=f32)
    cvals[:, 0] = LN_EPS
    cvals[:, 1] = SHEP_EPS
    cvals[:, 2] = 1.0
    cvals[:, 3] = -1.0
    cvals[:, 4] = -0.5
    onesrow = np.ones((1, 512), dtype=bf)

    Ws = np.ascontiguousarray(inputs["L0_ffn_Ws"], dtype=f32)
    bs = np.ascontiguousarray(inputs["L0_ffn_bs"], dtype=f32)
    W1 = np.ascontiguousarray(inputs["L0_ffn_W1"], dtype=f32)
    b1 = np.ascontiguousarray(inputs["L0_ffn_b1"], dtype=f32)
    W2 = np.ascontiguousarray(inputs["L0_ffn_W2"], dtype=f32)
    b2 = np.ascontiguousarray(inputs["L0_ffn_b2"], dtype=f32)
    pidx = np.arange(128)
    ind8f_h = (pidx[:, None] // 16 == np.arange(8)[None, :]).astype(f32)
    ind8Tf_h = np.ascontiguousarray(ind8f_h.T)
    selblk_h = (pidx[:, None] // 16 == pidx[None, :] // 16).astype(f32)
    triblk_h = ((pidx[:, None] // 16 == pidx[None, :] // 16) &
                (pidx[:, None] % 16 < pidx[None, :] % 16)).astype(f32)
    gidxp1_h = (np.arange(1024)[None, :] * 16 + (pidx % 16)[:, None] + 1).astype(f32)
    offv_h = ((pidx // 16) * 2048).astype(f32).reshape(128, 1)
    def r128(v):
        return np.ascontiguousarray(np.asarray(v, dtype=f32).reshape(8, 128).T)

    common = dict(winT=winT, binr=binr, woutT=woutT, bout_b=bout_b,
                  ident=ident, ident_bf=ident_bf, csel=csel, ones1f=ones1f, cvals=cvals,
                  onesrow=onesrow, ws_r=Ws, bsrow=bs.reshape(1, 8),
                  ind8f=ind8f_h, ind8Tf=ind8Tf_h, ind8f16=(ind8f_h / 16.0).astype(f32),
                  selblk=selblk_h, triblk=triblk_h, gidxp1=gidxp1_h, offv=offv_h,
                  ones8f=np.ones((8, 1), f32),
                  wkT=np.ascontiguousarray(np.asarray(inputs["L1_attn_Wk"], f32).T),
                  wvT=np.ascontiguousarray(np.asarray(inputs["L1_attn_Wv"], f32).T),
                  wrT=np.ascontiguousarray(np.asarray(inputs["L1_attn_Wr"], f32).T),
                  woT=np.ascontiguousarray(np.asarray(inputs["L1_attn_Wo"], f32).T),
                  wfcT=np.ascontiguousarray(np.asarray(inputs["L1_ffn_W_fc"], f32).T),
                  wprojT=np.ascontiguousarray(np.asarray(inputs["L1_ffn_W_proj"], f32).T),
                  decay_r=r128(inputs["L1_attn_spatial_decay"]),
                  first_r=r128(inputs["L1_attn_spatial_first"]),
                  mixk_r=r128(inputs["L1_attn_mix_k"]),
                  mixv_r=r128(inputs["L1_attn_mix_v"]),
                  mixr_r=r128(inputs["L1_attn_mix_r"]))
    in_maps = []
    for c in range(8):
        m = dict(common)
        m["x"] = np.ascontiguousarray(X[c])
        m["w1"] = W1[c]
        m["b1r"] = np.ascontiguousarray(b1[c].reshape(32, 128).T)
        m["w2"] = W2[c]
        m["b2_b"] = np.ascontiguousarray(np.broadcast_to(b2[c], (128, D)))
        cs = np.zeros((8, 1024), bf)
        cs[:, c * 128] = 1.0
        m["colsel"] = cs
        tg = (c * 2048 + np.arange(128)[None, :] * 16 + (pidx % 16)[:, None]).astype(f32)
        m["tidg"] = np.ascontiguousarray(np.repeat(tg, 4, axis=1))
        me = np.zeros((128, 128), f32)
        me[:, np.arange(16) * 8 + c] = 1.0
        m["myexp"] = me
        in_maps.append(m)
    return in_maps


def kernel(**inputs):
    nc = build_program()
    in_maps = _prep_inputs(inputs)
    res = run_bass_kernel_spmd(nc, in_maps, core_ids=list(range(8)))
    out = np.stack([res.results[c]["out"] for c in range(8)], axis=0)
    return np.ascontiguousarray(out.astype(np.float32))


# version 12 (final)
# speedup vs baseline: 1.0031x; 1.0031x over previous
import os
import numpy as np
import concourse.bass as bass
import concourse.mybir as mybir
from concourse.bass_utils import run_bass_kernel_spmd
import concourse.replica_groups as replica_groups
from concourse import library_config
from concourse.bass import IndirectOffsetOnAxis

F32 = mybir.dt.float32
BF16 = mybir.dt.bfloat16
I16 = mybir.dt.int16
U32 = mybir.dt.uint32
ALU = mybir.AluOpType
ACT = mybir.ActivationFunctionType
AX = mybir.AxisListType

T = 2048
D = 1024
NB = 16
H = 16
DH = 64
E = 8
NTOK = 8 * T
KCAP = 2048
FF = 4096
LN_EPS = 1e-5
SHEP_EPS = 1e-4

PHASES = int(os.environ.get("MK_PHASES", "4"))
BSTOP = int(os.environ.get("MK_BSTOP", "99"))


def sl(i, n):
    return slice(i * n, (i + 1) * n)


def build_program():
    from contextlib import ExitStack
    nc = bass.Bass("TRN2", target_bir_lowering=False)
    es = ExitStack()

    def din(name, shape, dt=F32):
        return nc.dram_tensor(name, shape, dt, kind="ExternalInput").ap()

    x_in = din("x", [T, D])
    winT = din("winT", [D, 4 * D])
    binr = din("binr", [128, 32])
    woutT = din("woutT", [H, DH, D])
    bout_b = din("bout_b", [128, D])
    ident = din("ident", [128, 128])
    ident_bf = din("ident_bf", [128, 128], BF16)
    csel = din("csel", [128, 4], BF16)
    ones1f = din("ones1f", [1, 128])
    cvals = din("cvals", [128, 8])
    onesrow = din("onesrow", [1, 512], BF16)
    ws_r = din("ws_r", [D, E])
    w1 = din("w1", [D, FF])
    b1r = din("b1r", [128, 32])
    w2 = din("w2", [FF, D])
    b2_b = din("b2_b", [128, D])
    ind8f = din("ind8f", [128, 8])
    ind8Tf = din("ind8Tf", [8, 128])
    selblk = din("selblk", [128, 128])
    triblk = din("triblk", [128, 128])
    gidxp1 = din("gidxp1", [128, 1024])
    offv = din("offv", [128, 1])
    colsel = din("colsel", [8, 1024], BF16)
    bsrow = din("bsrow", [1, E])
    ones8f = din("ones8f", [8, 1])
    ind8f16 = din("ind8f16", [128, 8])
    tidg = din("tidg", [128, 512])
    myexp = din("myexp", [128, 128])
    wkT = din("wkT", [D, D])
    wvT = din("wvT", [D, D])
    wrT = din("wrT", [D, D])
    woT = din("woT", [D, D])
    wfcT = din("wfcT", [D, FF])
    wprojT = din("wprojT", [FF, D])
    decay_r = din("decay_r", [128, 8])
    first_r = din("first_r", [128, 8])
    mixk_r = din("mixk_r", [128, 8])
    mixv_r = din("mixv_r", [128, 8])
    mixr_r = din("mixr_r", [128, 8])
    out_d = nc.dram_tensor("out", [T, D], F32, kind="ExternalOutput").ap()
    rwkvb = nc.dram_tensor("rwkvb", [4 * D, D], BF16, kind="Internal").ap()
    w1b = nc.dram_tensor("w1b", [D, FF], BF16, kind="Internal").ap()
    w2b = nc.dram_tensor("w2b", [FF, D], BF16, kind="Internal").ap()
    wfcb = nc.dram_tensor("wfcb", [D, FF], BF16, kind="Internal").ap()
    wprojb = nc.dram_tensor("wprojb", [FF, D], BF16, kind="Internal").ap()
    scr_row = nc.dram_tensor("scr_row", [1, 8192], F32, kind="Internal").ap()
    scr2 = nc.dram_tensor("scr2", [8, T], F32, kind="Internal").ap()
    agx_in = nc.dram_tensor("agx_in", [T, D], BF16, kind="Internal").ap()
    agx_out = nc.dram_tensor("agx_out", [NTOK, D], BF16, kind="Internal").ap()
    agp_in = nc.dram_tensor("agp_in", [128, 128], F32, kind="Internal").ap()
    agp_out = nc.dram_tensor("agp_out", [8 * 128, 128], F32, kind="Internal").ap()
    agy_in = nc.dram_tensor("agy_in", [KCAP, D], BF16, kind="Internal").ap()
    agy_out = nc.dram_tensor("agy_out", [NTOK, D], BF16, kind="Internal").ap()
    idbufd = nc.dram_tensor("idbufd", [NTOK, 4], F32, kind="Internal").ap()
    idbufr = nc.dram_tensor("idbufr", [NTOK, 4], F32, kind="Internal").ap()
    gslotd = nc.dram_tensor("gslotd", [NTOK, 4], F32, kind="Internal").ap()

    cnt = {}
    sems = {}

    with es:
        def sb(name, shape, dt=F32):
            return es.enter_context(nc.sbuf_tensor(name, shape, dt))

        def pst(name, shape, dt=F32):
            return es.enter_context(nc.psum_tensor(name, shape, dt))

        def new_sem(name):
            sems[name] = es.enter_context(nc.semaphore(name))
            cnt[name] = 0

        def inc(name, n=1):
            cnt[name] += n
            return sems[name], n

        def _ndma():
            return sum(1 for i in nc.all_instructions()
                       if type(i).__name__ in ("InstDMACopy", "InstDMATranspose"))

        def DMA(eng, semname, **kw):
            n0 = _ndma()
            inst = eng.dma_start(**kw)
            n = _ndma() - n0
            inst.then_inc(sems[semname], 16)
            cnt[semname] += 16 * n
            eng.wait_ge(sems[semname], cnt[semname])

        stream = sb("stream", [128, NB, D])
        lnT = sb("lnT", [128, 8, T], BF16)
        TA = sb("TA", [128, T], BF16)
        TB = sb("TB", [128, T], BF16)
        SQ2 = sb("SQ2", [128, T], BF16)
        KQSQ = sb("KQSQ", [128, T], BF16)
        vaug = sb("vaug", [128, NB, 65], BF16)
        scrA = sb("scrA", [128, T])
        rrow = scrA[0:1, :]
        UB = sb("UB", [128, T])
        ZB = sb("ZB", [128, T])
        wsc = sb("wsc", [128, T], BF16)
        abuf = sb("abuf", [65, T], BF16)
        otb = sb("otb", [64, 4, T], BF16)
        whq = sb("whq", [64, 4, D], BF16)
        wslA = [sb(f"wslA{i}", [128, 8, 128], BF16) for i in range(2)]
        wslB = [sb(f"wslB{i}", [128, 8, 128], BF16) for i in range(2)]
        wslC = [sb(f"wslC{i}", [128, 8, 128], BF16) for i in range(2)]
        augK0 = sb("augK0", [1, T], BF16)
        augR0 = sb("augR0", [1, T], BF16)
        c_onesbf = sb("c_onesbf", [1, 512], BF16)
        smallv = sb("smallv", [128, 96])
        c_ident = sb("c_ident", [128, 128])
        c_identbf = sb("c_identbf", [128, 128], BF16)
        c_sel = sb("c_sel", [128, 4], BF16)
        c_ones1f = sb("c_ones1f", [1, 128])
        c_bout = sb("c_bout", [128, D])
        c_b2b = c_bout
        c_bin = sb("c_bin", [128, 32])
        c_cv = sb("c_cv", [128, 8])
        c_ind8f = sb("c_ind8f", [128, 8])
        c_ind8Tf = sb("c_ind8Tf", [8, 128])
        c_selblk = sb("c_selblk", [128, 128])
        c_triblk = sb("c_triblk", [128, 128])
        c_gidxp1 = sb("c_gidxp1", [128, 1024])
        d16 = None
        c_offv = sb("c_offv", [128, 1])
        c_colsel = abuf[0:8, 0:1024]
        c_wsr = sb("c_wsr", [128, 8, 8])
        c_bsb = sb("c_bsb", [128, 8])
        c_b1r = sb("c_b1r", [128, 32])
        moesm = sb("moesm", [128, 64])
        c_bsrow = sb("c_bsrow", [1, 8])
        c_ones81 = sb("c_ones81", [8, 1])
        c_ind8f16 = sb("c_ind8f16", [128, 8])
        c_tidg = sb("c_tidg", [128, 512])
        c_myexp = sb("c_myexp", [128, 128])
        d16 = c_myexp[0:16, 0:128]
        KIDX = sb("KIDX", [128, 400], U32)

        psA = pst("psA", [128, 1024])
        psB = pst("psB", [128, 512])
        psBbf = pst("psBbf", [128, 1024], BF16)
        from contextlib import ExitStack as _ES2
        esA = _ES2()
        psC = esA.enter_context(nc.psum_tensor("psC", [65, 2048], F32))

        for s in ["sd", "sg", "sv", "ss", "st", "sq", "so", "sgt", "sp"]:
            new_sem(s)

        blk = es.enter_context(nc.Block())

        def SEC(engname, fn):
            getattr(blk, engname)(fn)

        def blk2(tile, b):
            return tile[:, b:b + 1, :].squeeze()

        def a0(eng):
            DMA(eng, "sd", out=stream[:], in_=x_in.rearrange("(b p) d -> p b d", p=128))
            for src, dst in [(ident, c_ident), (ident_bf, c_identbf), (csel, c_sel),
                             (ones1f, c_ones1f), (bout_b, c_bout), (binr, c_bin), (cvals, c_cv),
                             (onesrow, c_onesbf)]:
                DMA(eng, "sd", out=dst[:], in_=src)
        SEC("sync", a0)
        LOADS = cnt["sd"]

        def a0g(eng):
            for hh in range(2):
                DMA(eng, "sg", out=wslA[hh][:],
                              in_=winT[:, sl(2 * hh, 128)].rearrange("(c p) e -> p c e", p=128))
                DMA(eng, "sg", out=wslB[hh][:],
                              in_=winT[:, sl(2 * hh + 1, 128)].rearrange("(c p) e -> p c e", p=128))
        SEC("gpsimd", a0g)
        wload = {0: cnt["sg"], 1: cnt["sg"]}

        def a0v(eng):
            eng.memset(vaug[:, :, 64:65], 1.0).then_inc(*inc("sv"))
            eng.drain()
        SEC("vector", a0v)

        def a1_stats(eng):
            eng.wait_ge(sems["sd"], LOADS)
            for b in range(0, NB, 2):
                eng.bn_stats(smallv[:, 0:6], blk2(stream, b)[:, 0:512])
                eng.bn_stats(smallv[:, 6:12], blk2(stream, b)[:, 512:1024])
                eng.bn_stats(smallv[:, 80:86], blk2(stream, b + 1)[:, 0:512])
                eng.bn_stats(smallv[:, 86:92], blk2(stream, b + 1)[:, 512:1024])
                eng.drain()
                eng.bn_aggr(smallv[:, 2 * b + 16:2 * b + 18], smallv[:, 0:12])
                ag = eng.bn_aggr(smallv[:, 2 * b + 18:2 * b + 20], smallv[:, 80:92])
                if b == NB - 2:
                    ag.then_inc(*inc("sv"))
                eng.drain()
        SEC("vector", a1_stats)
        SV_STATS = cnt["sv"]

        def a1_sqrt(eng):
            eng.wait_ge(sems["sv"], SV_STATS)
            for b in range(NB):
                aa = eng.activation(out=smallv[:, 48 + b:49 + b], in_=smallv[:, 2 * b + 17:2 * b + 18],
                               func=ACT.Sqrt, bias=c_cv[:, 0:1])
                if b == NB - 1:
                    aa.then_inc(*inc("ss"))
            eng.drain()
        SEC("scalar", a1_sqrt)
        SS_SQRT = cnt["ss"]

        def a1_recip(eng):
            eng.wait_ge(sems["ss"], SS_SQRT)
            eng.reciprocal(smallv[:, 64:80], smallv[:, 48:64]).then_inc(*inc("sv"))
            eng.drain()
        SEC("vector", a1_recip)
        SV_RSTD = cnt["sv"]

        st_tr = {}
        sv_norm = {}
        ss_ev = {}
        for b in range(NB):
            def a1_norm(eng, b=b):
                eng.wait_ge(sems["sv"], SV_RSTD)
                if b >= 2:
                    eng.wait_ge(sems["st"], st_tr[b - 2])
                eng.tensor_scalar(scrA[:, sl(b % 2, D)], blk2(stream, b),
                                  smallv[:, 2 * b + 16:2 * b + 17], smallv[:, 64 + b:65 + b],
                                  ALU.subtract, ALU.mult).then_inc(*inc("sv"))
                eng.drain()
            SEC("vector", a1_norm)
            sv_norm[b] = cnt["sv"]

            def a1_tr(eng, b=b):
                eng.wait_ge(sems["sv"], sv_norm[b])
                if b >= 1:
                    eng.wait_ge(sems["ss"], ss_ev[b - 1])
                for c in range(8):
                    tr = eng.transpose(psA[:, sl(c, 128)], scrA[:, sl(b % 2, D)][:, sl(c, 128)], c_ident[:])
                    if c == 7:
                        tr.then_inc(*inc("st"))
            SEC("tensor", a1_tr)
            st_tr[b] = cnt["st"]

            def a1_ev(eng, b=b):
                eng.wait_ge(sems["st"], st_tr[b])
                eng.activation(out=lnT[:, :, sl(b, 128)],
                               in_=psA[:, 0:1024].rearrange("p (c t) -> p c t", c=8),
                               func=ACT.Copy).then_inc(*inc("ss"))
            SEC("scalar", a1_ev)
            ss_ev[b] = cnt["ss"]
        SS_LNT = cnt["ss"]

        SKIP_A2 = PHASES < 1

        if not SKIP_A2:
            def a1_dbl(eng):
                for b in range(NB):
                    db = eng.tensor_scalar(blk2(stream, b), blk2(stream, b), 2.0, None, ALU.mult)
                    if b == NB - 1:
                        db.then_inc(*inc("sv"))
                    eng.drain()
            SEC("vector", a1_dbl)

        last_psA_ss = SS_LNT
        last_psA_sv = 0
        last_psC_ss = 0
        last_psB_sv = cnt["sv"]
        prev_ot_sv = 0
        prev_av_st = 0
        prev_d16_sd = 0
        prev_rrow_sd = 0
        prev_rb_st = 0
        prev_evA_ss = 0
        qg_ot_sv = {}
        sv_op_last = cnt["sv"]

        for qg in range(4 if not SKIP_A2 else 0):
            for hh in range(4):
                h = qg * 4 + hh
                par = h % 2
                sA, sB = 2 * h, 2 * h + 1
                OT_PREV = prev_ot_sv
                AV_PREV = prev_av_st
                D16_PREV = prev_d16_sd
                RROW_PREV = prev_rrow_sd
                RB_PREV = prev_rb_st

                def a2_nb(eng, sA=sA, w=prev_evA_ss):
                    eng.wait_ge(sems["sd"], LOADS)
                    if w:
                        eng.wait_ge(sems["ss"], w)
                    eng.tensor_scalar(smallv[:, 8:9], c_bin[:, sA:sA + 1], -2.0, None, ALU.mult).then_inc(*inc("sv"))
                    eng.drain()
                SEC("vector", a2_nb)
                SV_NB = cnt["sv"]

                st_pa = {}
                SS_EA_SNAP = cnt["ss"]

                def a2_pa(eng, h=h, par=par, psafree_ss=last_psA_ss, psafree_sv=last_psA_sv, opfree=sv_op_last):
                    eng.wait_ge(sems["ss"], SS_LNT)
                    eng.wait_ge(sems["sg"], wload[h])
                    if psafree_ss:
                        eng.wait_ge(sems["ss"], psafree_ss)
                    if psafree_sv:
                        eng.wait_ge(sems["sv"], psafree_sv)
                    eng.wait_ge(sems["sv"], opfree)
                    eng.wait_ge(sems["sv"], SV_NB)
                    for half in range(2):
                        if half == 1:
                            eng.wait_ge(sems["ss"], SS_EA_SNAP + 1)
                        for sp in range(2):
                            for c in range(8):
                                mm = eng.matmul(psA[:, sl(sp, 512)], wslA[par][:, c:c + 1, :].squeeze(),
                                           lnT[:, c:c + 1, sl(half * 2 + sp, 512)].squeeze(),
                                           start=(c == 0), stop=(c == 7))
                                if sp == 1 and c == 7:
                                    mm.then_inc(*inc("st"))
                        st_pa[half] = cnt["st"]
                SEC("tensor", a2_pa)

                def a2_ea(eng, h=h, sA=sA):
                    eng.wait_ge(sems["sv"], SV_NB)
                    for half in range(2):
                        eng.wait_ge(sems["st"], st_pa[half])
                        dst = sl(half, 1024)
                        eng.activation(out=TA[:, dst], in_=psA[:], func=ACT.Identity,
                                       bias=c_bin[:, sA:sA + 1])
                        eng.activation(out=SQ2[64:128, dst], in_=psA[64:128, :], func=ACT.Identity,
                                       bias=smallv[64:128, 8:9], scale=-2.0)
                        eng.activation(out=KQSQ[64:128, dst], in_=psA[64:128, :], func=ACT.Square,
                                       bias=c_bin[64:128, sA:sA + 1]).then_inc(*inc("ss"))
                SEC("scalar", a2_ea)
                SS_EA = cnt["ss"]
                prev_evA_ss = SS_EA

                def a2_q2(eng, h=h, pcfree=last_psC_ss):
                    eng.wait_ge(sems["ss"], SS_EA)
                    if pcfree:
                        eng.wait_ge(sems["ss"], pcfree)
                    for sp in range(4):
                        mm = eng.matmul(psC[0:1, sl(sp, 512)], c_sel[64:128, 3:4],
                                   KQSQ[64:128, sl(sp, 512)], start=True, stop=True,
                                   skip_group_check=True)
                        if sp == 3:
                            mm.then_inc(*inc("st"))
                SEC("tensor", a2_q2)
                ST_Q2 = cnt["st"]

                def a2_q2e(eng, h=h):
                    eng.wait_ge(sems["st"], ST_Q2)
                    eng.activation(out=augR0[:], in_=psC[0:1, :], func=ACT.Copy).then_inc(*inc("ss"))
                SEC("scalar", a2_q2e)

                st_vt = {}

                def a2_vt(eng, h=h, pbfree=last_psB_sv):
                    eng.wait_ge(sems["ss"], SS_EA)
                    eng.wait_ge(sems["sv"], pbfree)
                    for j in range(16):
                        tr = eng.transpose(psBbf[:, sl(j, 64)], TA[0:64, sl(j, 128)],
                                      c_identbf[0:64, 0:64])
                        if j == 15:
                            tr.then_inc(*inc("st"))
                    st_vt[0] = cnt["st"]
                SEC("tensor", a2_vt)

                def a2_ve(eng, h=h):
                    eng.wait_ge(sems["st"], st_vt[0])
                    eng.tensor_copy(vaug[:, :, 0:64],
                                    psBbf[:, 0:1024].rearrange("p (b j) -> p b j", b=16)).then_inc(*inc("sv"))
                    eng.drain()
                SEC("vector", a2_ve)
                SV_VAUG = cnt["sv"]
                last_psB_sv = SV_VAUG

                st_pb = {}
                SS_EB_SNAP = cnt["ss"]

                def a2_pb(eng, h=h, par=par):
                    eng.wait_ge(sems["ss"], SS_EA)
                    for half in range(2):
                        if half == 1:
                            eng.wait_ge(sems["ss"], SS_EB_SNAP + 1)
                        for sp in range(2):
                            for c in range(8):
                                mm = eng.matmul(psA[:, sl(sp, 512)], wslB[par][:, c:c + 1, :].squeeze(),
                                           lnT[:, c:c + 1, sl(half * 2 + sp, 512)].squeeze(),
                                           start=(c == 0), stop=(c == 7))
                                if sp == 1 and c == 7:
                                    mm.then_inc(*inc("st"))
                        st_pb[half] = cnt["st"]
                SEC("tensor", a2_pb)

                def a2_eb(eng, h=h, sB=sB, otprev=OT_PREV):
                    if otprev:
                        eng.wait_ge(sems["sv"], otprev)
                    for half in range(2):
                        eng.wait_ge(sems["st"], st_pb[half])
                        dst = sl(half, 1024)
                        eng.activation(out=TB[:, dst], in_=psA[:], func=ACT.Identity,
                                       bias=c_bin[:, sB:sB + 1])
                        eng.activation(out=KQSQ[64:128, dst], in_=psA[64:128, :], func=ACT.Square,
                                       bias=c_bin[64:128, sB:sB + 1]).then_inc(*inc("ss"))
                SEC("scalar", a2_eb)
                SS_EB = cnt["ss"]

                def a2_k2(eng, h=h):
                    eng.wait_ge(sems["ss"], SS_EB)
                    eng.wait_ge(sems["ss"], cnt["ss"])
                    for sp in range(4):
                        mm = eng.matmul(psC[0:1, sl(sp, 512)], c_sel[64:128, 0:1],
                                   KQSQ[64:128, sl(sp, 512)], start=True, stop=True,
                                   skip_group_check=True)
                        if sp == 3:
                            mm.then_inc(*inc("st"))
                SEC("tensor", a2_k2)
                ST_K2 = cnt["st"]

                def a2_aug(eng, h=h):
                    eng.wait_ge(sems["st"], ST_K2)
                    eng.activation(out=augK0[:], in_=psC[0:1, :], func=ACT.Copy).then_inc(*inc("ss"))
                SEC("scalar", a2_aug)
                SS_AUG = cnt["ss"]

                sv_cl = {}
                ss_w = {}
                ss_p1 = {}
                wchx = [(c_tidg[:, 0:512], 0, 512),
                        (KIDX[:, :].bitcast(F32)[:, 0:384], 512, 384),
                        (c_selblk[:, 0:128], 896, 128)]
                st_sc = {}
                st_av = {}
                def mk_av(kb):
                    def a2_av(eng, h=h, kb=kb):
                        eng.wait_ge(sems["ss"], ss_w[kb])
                        for sp in range(4):
                            mm = eng.matmul(psC[:, sl(sp, 512)], vaug[:, kb:kb + 1, :].squeeze(),
                                       wsc[:, sl(sp, 512)],
                                       start=(kb == 0), stop=(kb == NB - 1),
                                       skip_group_check=True)
                            if sp == 3:
                                mm.then_inc(*inc("st"))
                    SEC("tensor", a2_av)
                    st_av[kb] = cnt["st"]

                for kb in range(NB):
                    snap_sv = cnt["sv"]

                    def a2_sc(eng, h=h, kb=kb, snap=snap_sv):
                        if kb == 0:
                            eng.wait_ge(sems["ss"], SS_AUG)
                            eng.wait_ge(sems["sv"], SV_VAUG)
                        for half in range(2):
                            if half == 1 or kb >= 1:
                                eng.wait_ge(sems["sv"], snap + half)
                            for sp in range(2):
                                eng.matmul(psA[:, sl(sp, 512)],
                                           TB[64:128, sl(kb, 128)],
                                           SQ2[64:128, sl(half * 2 + sp, 512)],
                                           start=True, stop=False, skip_group_check=True)
                                eng.matmul(psA[:, sl(sp, 512)],
                                           augK0[0:1, sl(kb, 128)],
                                           c_onesbf[0:1, 0:512],
                                           start=False, stop=False, skip_group_check=True)
                                mm = eng.matmul(psA[:, sl(sp, 512)],
                                           c_onesbf[0:1, 0:128],
                                           augR0[0:1, sl(half * 2 + sp, 512)],
                                           start=False, stop=True, skip_group_check=True)
                                if sp == 1:
                                    mm.then_inc(*inc("st"))
                            st_sc[(kb, half)] = cnt["st"]
                    SEC("tensor", a2_sc)

                    if kb >= 1:
                        mk_av(kb - 1)

                    def a2_cl(eng, h=h, kb=kb, otprev=OT_PREV):
                        if kb >= 1:
                            eng.wait_ge(sems["ss"], ss_p1[kb - 1])
                        elif otprev:
                            eng.wait_ge(sems["sv"], otprev)
                        for half in range(2):
                            eng.wait_ge(sems["st"], st_sc[(kb, half)])
                            eng.tensor_scalar(UB[:, sl(half, 1024)], psA[:], 0.0, None, ALU.max).then_inc(*inc("sv"))
                    SEC("vector", a2_cl)
                    sv_cl[kb] = cnt["sv"]

                    def a2_wch(eng, h=h, kb=kb, avprev=AV_PREV):
                        eng.wait_ge(sems["sv"], sv_cl[kb])
                        if kb >= 1:
                            eng.wait_ge(sems["st"], st_av[kb - 1])
                        elif avprev:
                            eng.wait_ge(sems["st"], avprev)
                        eng.activation(out=ZB[:, 0:1024], in_=UB[:, 0:1024], func=ACT.Sqrt)
                        eng.activation(out=ZB[:, 1024:2048], in_=UB[:, 1024:2048],
                                       func=ACT.Sqrt).then_inc(*inc("ss"))
                        eng.drain()
                        eng.activation(out=c_gidxp1[:, :], in_=ZB[:, 0:1024], func=ACT.Ln, bias=c_cv[:, 1:2])
                        for xp, off, w in wchx:
                            eng.activation(out=xp, in_=ZB[:, 1024 + off:1024 + off + w],
                                           func=ACT.Ln, bias=c_cv[:, 1:2])
                        eng.drain()
                        eng.activation(out=wsc[:, 0:1024], in_=c_gidxp1[:, :], func=ACT.Exp, scale=-2.0)
                        for j, (xp, off, w) in enumerate(wchx):
                            ee = eng.activation(out=wsc[:, 1024 + off:1024 + off + w], in_=xp,
                                                func=ACT.Exp, scale=-2.0)
                        ee.then_inc(*inc("ss"))
                    SEC("scalar", a2_wch)
                    ss_w[kb] = cnt["ss"]
                    ss_p1[kb] = ss_w[kb] - 1
                mk_av(NB - 1)

                def a2_ae(eng, h=h, otprev=OT_PREV, d16prev=D16_PREV):
                    eng.wait_ge(sems["st"], st_av[NB - 1])
                    if otprev:
                        eng.wait_ge(sems["sv"], otprev)
                    if d16prev:
                        eng.wait_ge(sems["sd"], d16prev)
                    eng.activation(out=abuf[:], in_=psC[:], func=ACT.Copy)
                    eng.activation(out=scrA[64:65, :], in_=psC[64:65, :], func=ACT.Copy).then_inc(*inc("ss"))
                SEC("scalar", a2_ae)
                SS_AE = cnt["ss"]
                last_psC_ss = SS_AE
                last_psA_sv = sv_cl[NB - 1]

                def a2_d1(eng, h=h, rrowprev=RROW_PREV):
                    eng.wait_ge(sems["ss"], SS_AE)
                    if rrowprev:
                        eng.wait_ge(sems["sd"], rrowprev)
                    DMA(eng, "sd", out=scr_row[0:1, 0:T], in_=scrA[64:65, :])
                    eng.wait_ge(sems["sd"], cnt["sd"])
                    DMA(eng, "sd", out=d16, in_=scr_row[0:1, 0:T].rearrange("o (r f) -> (o r) f", r=16))
                SEC("sync", a2_d1)
                SD_D16 = cnt["sd"]

                def a2_rc(eng, h=h):
                    eng.wait_ge(sems["sd"], SD_D16)
                    eng.tensor_scalar(d16, d16, float(SHEP_EPS), None, ALU.add)
                    eng.drain()
                    eng.reciprocal(d16, d16).then_inc(*inc("sv"))
                    eng.drain()
                SEC("vector", a2_rc)
                SV_RC = cnt["sv"]

                def a2_d2(eng, h=h, rbprev=RB_PREV):
                    eng.wait_ge(sems["sv"], SV_RC)
                    if rbprev:
                        eng.wait_ge(sems["st"], rbprev)
                    DMA(eng, "sd", out=scr_row[0:1, 0:T].rearrange("o (r f) -> (o r) f", r=16),
                                  in_=d16)
                    eng.wait_ge(sems["sd"], cnt["sd"])
                    DMA(eng, "sd", out=rrow, in_=scr_row[0:1, 0:T])
                SEC("sync", a2_d2)
                SD_RROW = cnt["sd"]

                st_rb = {}
                SV_OT_BASE = cnt["sv"]

                def a2_rb(eng, h=h):
                    eng.wait_ge(sems["sd"], SD_RROW)
                    eng.wait_ge(sems["ss"], SS_AE)
                    eng.wait_ge(sems["sv"], SV_VAUG)
                    for sp in range(4):
                        if sp >= 1:
                            eng.wait_ge(sems["sv"], SV_OT_BASE + 2 * sp - 1)
                        eng.matmul(psB[0:64, :], c_ones1f[0:1, 0:64], rrow[0:1, sl(sp, 512)],
                                   start=True, stop=True, skip_group_check=True).then_inc(*inc("st"))
                        st_rb[sp] = cnt["st"]
                SEC("tensor", a2_rb)

                def a2_ot(eng, h=h, hh=hh):
                    for sp in range(4):
                        eng.wait_ge(sems["st"], st_rb[sp])
                        eng.tensor_tensor(UB[0:64, sl(sp, 512)], abuf[0:64, sl(sp, 512)],
                                          psB[0:64, :], ALU.mult).then_inc(*inc("sv"))
                        eng.drain()
                        eng.tensor_tensor(otb[:, hh:hh + 1, sl(sp, 512)].squeeze(),
                                          UB[0:64, sl(sp, 512)],
                                          TB[0:64, sl(sp, 512)], ALU.mult).then_inc(*inc("sv"))
                        eng.drain()
                SEC("vector", a2_ot)
                SV_OT = cnt["sv"]
                last_psB_sv = SV_OT
                prev_ot_sv = SV_OT
                prev_av_st = st_av[NB - 1]
                prev_d16_sd = SD_D16
                prev_rrow_sd = SD_RROW
                prev_rb_st = st_rb[3]

                if h + 2 < H:
                    def a2_wl(eng, h=h, par=par, pbdone=st_pb[1]):
                        eng.wait_ge(sems["st"], pbdone)
                        DMA(eng, "sg", out=wslA[par][:],
                                      in_=winT[:, sl(2 * (h + 2), 128)].rearrange("(c p) e -> p c e", p=128))
                        DMA(eng, "sg", out=wslB[par][:],
                                      in_=winT[:, sl(2 * (h + 2) + 1, 128)].rearrange("(c p) e -> p c e", p=128))
                        eng.wait_ge(sems["sg"], cnt["sg"])
                    SEC("gpsimd", a2_wl)
                    wload[h + 2] = cnt["sg"]

                if PHASES >= 2:
                    def a2_pc(eng, h=h):
                        DMA(eng, "sp", out=w1b[h * 64:(h + 1) * 64, :],
                                      in_=w1[h * 64:(h + 1) * 64, :])
                        DMA(eng, "sp", out=w2b[h * 256:(h + 1) * 256, :],
                                      in_=w2[h * 256:(h + 1) * 256, :])
                        if PHASES >= 3:
                            src = [wkT, wvT, wrT, woT][h // 4]
                            r0 = (h % 4) * 256
                            DMA(eng, "sp", out=rwkvb[(h // 4) * D + r0:(h // 4) * D + r0 + 256, :],
                                          in_=src[r0:r0 + 256, :])
                    SEC("gpsimd", a2_pc)

            qg_ot_sv[qg] = prev_ot_sv

            def a2_whq(eng, qg=qg, opfree=sv_op_last):
                eng.wait_ge(sems["sv"], opfree)
                DMA(eng, "sg", out=whq[:],
                              in_=woutT[sl(qg, 4), :, :].rearrange("h o c -> o h c"))
            SEC("gpsimd", a2_whq)
            SD_WHQ = cnt["sg"]

            st_op = {}
            sv_op = {}
            for tt in range(NB):
                def a2_op(eng, qg=qg, tt=tt, psafree_sv=last_psA_sv):
                    eng.wait_ge(sems["sg"], SD_WHQ)
                    eng.wait_ge(sems["sv"], qg_ot_sv[qg])
                    if tt == 0 and psafree_sv:
                        eng.wait_ge(sems["sv"], psafree_sv)
                    else:
                        eng.wait_ge(sems["sv"], sv_op[tt - 1])
                    for sp in range(2):
                        for hh4 in range(4):
                            mm = eng.matmul(psA[:, sl(sp, 512)],
                                       otb[:, hh4:hh4 + 1, sl(tt, 128)].squeeze(),
                                       whq[:, hh4:hh4 + 1, sl(sp, 512)].squeeze(),
                                       start=(hh4 == 0), stop=(hh4 == 3), skip_group_check=True)
                            if sp == 1 and hh4 == 3:
                                mm.then_inc(*inc("st"))
                SEC("tensor", a2_op)
                st_op[tt] = cnt["st"]

                def a2_opv(eng, qg=qg, tt=tt):
                    eng.wait_ge(sems["st"], st_op[tt])
                    eng.tensor_tensor(blk2(stream, tt), psA[:, 0:1024], blk2(stream, tt), ALU.add).then_inc(*inc("sv"))
                    eng.drain()
                SEC("vector", a2_opv)
                sv_op[tt] = cnt["sv"]
            sv_op_last = sv_op[NB - 1]
            last_psA_ss = 0
            last_psA_sv = sv_op[NB - 1]

        SP_RWKV = cnt["sp"]

        if SKIP_A2:
            SV_DONE0 = cnt["sv"]

            def a_out0(eng):
                eng.wait_ge(sems["sv"], SV_DONE0)
                eng.wait_ge(sems["ss"], cnt["ss"])
                DMA(eng, "so", out=out_d.rearrange("(b p) d -> p b d", p=128), in_=stream[:])
                eng.wait_ge(sems["so"], cnt["so"])
            SEC("sync", a_out0)
            es.pop_all().close() if False else None
            blk.__exit__(None, None, None) if False else None
        if not SKIP_A2:
            _dummy = 0

            def a3_bias(eng):
                for b in range(NB):
                    eng.tensor_tensor(blk2(stream, b), c_bout[:], blk2(stream, b), ALU.add)
                eng.drain()
                for b in range(0, NB, 2):
                    eng.bn_stats(smallv[:, 0:6], blk2(stream, b)[:, 0:512])
                    eng.bn_stats(smallv[:, 6:12], blk2(stream, b)[:, 512:1024])
                    eng.bn_stats(smallv[:, 80:86], blk2(stream, b + 1)[:, 0:512])
                    eng.bn_stats(smallv[:, 86:92], blk2(stream, b + 1)[:, 512:1024])
                    eng.drain()
                    eng.bn_aggr(smallv[:, 2 * b + 16:2 * b + 18], smallv[:, 0:12])
                    ag = eng.bn_aggr(smallv[:, 2 * b + 18:2 * b + 20], smallv[:, 80:92])
                    if b == NB - 2:
                        ag.then_inc(*inc("sv"))
                    eng.drain()
            SEC("vector", a3_bias)
            SV_ST3 = cnt["sv"]

            def a3_sqrt(eng):
                eng.wait_ge(sems["sv"], SV_ST3)
                for b in range(NB):
                    aa = eng.activation(out=smallv[:, 48 + b:49 + b], in_=smallv[:, 2 * b + 17:2 * b + 18],
                                   func=ACT.Sqrt, bias=c_cv[:, 0:1])
                    if b == NB - 1:
                        aa.then_inc(*inc("ss"))
                eng.drain()
            SEC("scalar", a3_sqrt)
            SS_SQ3 = cnt["ss"]

            def a3_norm(eng):
                eng.wait_ge(sems["ss"], SS_SQ3)
                eng.reciprocal(smallv[:, 64:80], smallv[:, 48:64])
                eng.drain()
                for b in range(NB):
                    tsn = eng.tensor_scalar(blk2(stream, b), blk2(stream, b),
                                      smallv[:, 2 * b + 16:2 * b + 17], smallv[:, 64 + b:65 + b],
                                      ALU.subtract, ALU.mult)
                    if b == NB - 1:
                        tsn.then_inc(*inc("sv"))
                eng.drain()
            SEC("vector", a3_norm)
            SV_DONEA = cnt["sv"]


            _ph = {}

            def emit_phase_b():
                if PHASES < 2:
                    return
                esA.close()
                psD2 = es.enter_context(nc.psum_tensor("psD2", [128, 2048], F32))
                _ph["psD2"] = psD2
                psBf32 = psBbf[:, :].bitcast(F32)

                S16g = TA[:, :].bitcast(F32)
                SQf = SQ2[:, :].bitcast(F32)
                LWl = SQf[:, 0:128]
                PWR = SQf[:, 128:256]
                m_l = SQf[:, 256:384]
                probs = wsc[:, :].bitcast(F32)
                PW = scrA[0:8, :]
                hbv = lnT[:, :, :].rearrange("p c t -> p (c t)")
                UBbf = UB[:, :].bitcast(BF16)
                yv = ZB[:, :].bitcast(BF16)
                KQi = KQSQ[:, :].bitcast(I16)
                KQu = KQSQ[:, :].bitcast(U32)
                xgst = vaug[:, :, :].rearrange("p a b -> p (a b)")[:, 0:1024]

                def b0(eng):
                    eng.wait_ge(sems["sv"], SV_DONEA)
                    for s2, dst in [(ind8f, c_ind8f), (ind8Tf, c_ind8Tf), (selblk, c_selblk),
                                    (triblk, c_triblk), (gidxp1, c_gidxp1), (offv, c_offv),
                                    (colsel, c_colsel), (bsrow, c_bsrow),
                                    (ones8f, c_ones81), (ind8f16, c_ind8f16),
                                    (tidg, c_tidg), (myexp, c_myexp),
                                    (b1r, c_b1r), (b2_b, c_b2b)]:
                        DMA(eng, "sd", out=(dst if dst is c_colsel else dst[:]), in_=s2)
                    DMA(eng, "sd", out=c_wsr[:], in_=ws_r.rearrange("(c p) e -> p c e", p=128))
                SEC("sync", b0)
                B0L = cnt["sd"]

                if BSTOP < 1:
                    return
                sv_rte = {}
                ss_rx = {}
                for tt in range(NB):
                    def b1_tr(eng, tt=tt):
                        if tt == 0:
                            eng.wait_ge(sems["sv"], SV_DONEA)
                            eng.wait_ge(sems["sd"], B0L)
                        if tt >= 2:
                            eng.wait_ge(sems["sv"], sv_rte[tt - 2])
                        for c in range(8):
                            tr = eng.transpose(psA[:, sl(c, 128)], blk2(stream, tt)[:, sl(c, 128)], c_ident[:])
                            if c == 7:
                                tr.then_inc(*inc("st"))
                    SEC("tensor", b1_tr)
                    ST_RT = cnt["st"]

                    def b1_ev(eng, tt=tt):
                        eng.wait_ge(sems["st"], ST_RT)
                        eng.tensor_copy(UB[:, sl(tt % 2, 1024)], psA[:, 0:1024]).then_inc(*inc("sv"))
                        eng.drain()
                    SEC("vector", b1_ev)
                    sv_rte[tt] = cnt["sv"]

                    def b1_mm(eng, tt=tt):
                        eng.wait_ge(sems["sv"], sv_rte[tt])
                        if tt >= 1:
                            eng.wait_ge(sems["ss"], ss_rx[tt - 1])
                        for c in range(8):
                            eng.matmul(psB[:, 0:8], UB[:, sl(tt % 2, 1024)][:, sl(c, 128)],
                                       c_wsr[:, c:c + 1, :].squeeze(),
                                       start=(c == 0), stop=False, skip_group_check=True)
                        mm = eng.matmul(psB[:, 0:8], c_ones1f[0:1, 0:128], c_bsrow[0:1, :],
                                        start=False, stop=True, skip_group_check=True)
                        mm.then_inc(*inc("st"))
                    SEC("tensor", b1_mm)
                    ST_RMM = cnt["st"]

                    def b1_x(eng, tt=tt):
                        eng.wait_ge(sems["st"], ST_RMM)
                        eng.activation(out=probs[:, sl(tt, 8)], in_=psB[:, 0:8], func=ACT.Exp)
                        eng.drain()
                        eng.nop().then_inc(*inc("ss"))
                    SEC("scalar", b1_x)
                    ss_rx[tt] = cnt["ss"]

                def b2_sm(eng):
                    eng.wait_ge(sems["ss"], ss_rx[NB - 1])
                    pv = probs[:, 0:128].rearrange("p (b e) -> p b e", e=8)
                    eng.tensor_reduce(moesm[:, 0:16], pv, AX.X, ALU.add)
                    eng.drain()
                    eng.reciprocal(moesm[:, 16:32], moesm[:, 0:16])
                    eng.drain()
                    rv = moesm[:, 16:32].rearrange("p b -> p b 1" if False else "p (b o) -> p b o", o=1)
                    eng.tensor_tensor(pv, pv, rv.broadcast_to([128, 16, 8]), ALU.mult)
                    eng.drain()
                    eng.nop().then_inc(*inc("sv"))
                SEC("vector", b2_sm)
                SV_SM = cnt["sv"]

                if BSTOP < 2:
                    return
                st_pt = {}
                for rnd in range(4):
                    def b3_tr(eng, rnd=rnd):
                        eng.wait_ge(sems["sv"], SV_SM)
                        if rnd >= 1:
                            eng.wait_ge(sems["sv"], SV_SM + rnd)
                        for j in range(4):
                            b = rnd * 4 + j
                            tr = eng.transpose(psB[0:8, sl(j, 128)], probs[:, sl(b, 8)], c_ident[:])
                            if j == 3:
                                tr.then_inc(*inc("st"))
                    SEC("tensor", b3_tr)
                    st_pt[rnd] = cnt["st"]

                    def b3_ev(eng, rnd=rnd):
                        eng.wait_ge(sems["st"], st_pt[rnd])
                        eng.tensor_copy(PW[:, sl(rnd, 512)], psB[0:8, 0:512]).then_inc(*inc("sv"))
                        eng.drain()
                    SEC("vector", b3_ev)
                SV_PW = cnt["sv"]

                def b3_d(eng):
                    eng.wait_ge(sems["sv"], SV_PW)
                    DMA(eng, "sd", out=scr2, in_=PW[:, :])
                    eng.wait_ge(sems["sd"], cnt["sd"])
                    DMA(eng, "sd", out=PWR.rearrange("p (e r) -> p e r", r=16),
                                  in_=scr2.rearrange("e (f r) -> f e r", r=16))
                SEC("sync", b3_d)
                SD_PWR = cnt["sd"]

                def b3_t2(eng):
                    eng.wait_ge(sems["sd"], SD_PWR)
                    eng.transpose(psA[:, 0:128], PWR, c_ident[:]).then_inc(*inc("st"))
                SEC("tensor", b3_t2)
                ST_LW = cnt["st"]

                def b3_lw(eng):
                    eng.wait_ge(sems["st"], ST_LW)
                    eng.tensor_copy(LWl, psA[:, 0:128]).then_inc(*inc("sv"))
                    eng.drain()
                SEC("vector", b3_lw)
                SV_LW = cnt["sv"]

                if BSTOP < 3:
                    return
                def b4_x(eng):
                    eng.wait_ge(sems["sv"], SV_DONEA)
                    DMA(eng, "sg", out=agx_in.rearrange("(b p) d -> p b d", p=128),
                                  in_=stream[:])
                    eng.wait_ge(sems["sg"], cnt["sg"])
                    eng.collective_compute("AllGather", ALU.bypass,
                                           replica_groups=replica_groups.LNC1_1x8,
                                           ins=[agx_in], outs=[agx_out]).then_inc(*inc("sq"))
                SEC("gpsimd", b4_x)
                SQ_AGX = cnt["sq"]

                def b4_p(eng):
                    eng.wait_ge(sems["sv"], SV_LW)
                    DMA(eng, "sd", out=agp_in, in_=LWl)
                SEC("sync", b4_p)
                SD_AGP = cnt["sd"]

                def b4_pc(eng):
                    eng.wait_ge(sems["sd"], SD_AGP)
                    eng.collective_compute("AllGather", ALU.bypass,
                                           replica_groups=replica_groups.LNC1_1x8,
                                           ins=[agp_in], outs=[agp_out]).then_inc(*inc("sq"))
                SEC("gpsimd", b4_pc)
                SQ_AGP = cnt["sq"]

                def b5_l(eng):
                    eng.wait_ge(sems["sq"], SQ_AGP)
                    DMA(eng, "sd", out=S16g.rearrange("p (c f) -> p c f", c=8),
                                  in_=agp_out.rearrange("(c p) f -> p c f", p=128))
                SEC("sync", b5_l)
                SD_S16 = cnt["sd"]

                if BSTOP < 4:
                    return
                def b6_init(eng):
                    eng.wait_ge(sems["sd"], SD_S16)
                    eng.memset(KQi[:, 0:1152], 0)
                    eng.drain()
                    eng.memset(moesm[:, 32:33], 0.0)
                    eng.memset(moesm[:, 33:34], 1.01)
                    eng.drain()
                    eng.nop().then_inc(*inc("sv"))
                SEC("vector", b6_init)

                selb = KQSQ[:, :].bitcast(I16)[:, 1200:1204]
                for it in range(32):
                    def b6_v1(eng, it=it):
                        eng.tensor_scalar(moesm[:, 34:35], moesm[:, 32:33], moesm[:, 33:34], 0.5,
                                          ALU.add, ALU.mult)
                        eng.drain()
                        eng.tensor_scalar(UB[:, 0:1024], S16g, moesm[:, 34:35], 0.0, ALU.is_ge,
                                          ALU.add, accum_out=moesm[:, 35:36]).then_inc(*inc("sv"))
                        eng.drain()
                    SEC("vector", b6_v1)
                    SV_B6 = cnt["sv"]

                    def b6_t(eng, it=it):
                        eng.wait_ge(sems["sv"], SV_B6)
                        eng.matmul(psB[:, 16:17], c_selblk[:, :], moesm[:, 35:36],
                                   start=True, stop=True, skip_group_check=True).then_inc(*inc("st"))
                    SEC("tensor", b6_t)
                    ST_B6 = cnt["st"]

                    def b6_v2(eng, it=it):
                        eng.wait_ge(sems["st"], ST_B6)
                        eng.tensor_scalar(selb[:, 0:1], psB[:, 16:17], 2047.5, None, ALU.is_ge)
                        eng.drain()
                        eng.select(moesm[:, 32:33], selb[:, 0:1], moesm[:, 34:35], moesm[:, 32:33], add_drain=True)
                        eng.drain()
                        eng.tensor_scalar(selb[:, 1:2], psB[:, 16:17], 2047.5, None, ALU.is_lt)
                        eng.drain()
                        eng.select(moesm[:, 33:34], selb[:, 1:2], moesm[:, 34:35], moesm[:, 33:34], add_drain=True).then_inc(*inc("sv"))
                        eng.drain()
                    SEC("vector", b6_v2)
                SV_TAU = cnt["sv"]

                if BSTOP < 5:
                    return
                def b7_v(eng):
                    eng.wait_ge(sems["sv"], SV_TAU)
                    eng.tensor_scalar(UB[:, 0:1024], S16g, moesm[:, 32:33], None, ALU.is_ge)
                    eng.drain()
                    eng.tensor_scalar(m_l, LWl, moesm[:, 32:33], None, ALU.is_ge).then_inc(*inc("sv"))
                    eng.drain()
                SEC("vector", b7_v)
                SV_MG = cnt["sv"]

                def b7_cs(eng):
                    eng.wait_ge(sems["sv"], SV_MG)
                    for half in range(2):
                        mm = eng.matmul(psA[0:8, sl(half, 512)], c_ind8f[:, :],
                                        UB[:, sl(half, 512)], start=True, stop=True,
                                        skip_group_check=True)
                        if half == 1:
                            mm.then_inc(*inc("st"))
                SEC("tensor", b7_cs)
                ST_CS = cnt["st"]

                def b7_scan(eng):
                    eng.wait_ge(sems["st"], ST_CS)
                    eng.tensor_copy(PW[:, 0:1024], psA[0:8, 0:1024])
                    eng.drain()
                    eng.tensor_tensor_scan(PW[:, 1024:2048], c_ones81[:, :].broadcast_to([8, 1024]), PW[:, 0:1024],
                                           0.0, ALU.mult, ALU.add)
                    eng.drain()
                    eng.tensor_tensor(PW[:, 1024:2048], PW[:, 1024:2048], PW[:, 0:1024], ALU.subtract)
                    eng.drain()
                    eng.tensor_tensor(PW[:, 0:1024], PW[:, 1024:2048], c_colsel, ALU.mult)
                    eng.drain()
                    eng.tensor_reduce(moesm[0:8, 40:41], PW[:, 0:1024], AX.X, ALU.add).then_inc(*inc("sv"))
                    eng.drain()
                SEC("vector", b7_scan)
                SV_C8 = cnt["sv"]

                def b7_cb(eng):
                    eng.wait_ge(sems["sv"], SV_C8)
                    eng.matmul(psB[:, 16:17], c_ind8Tf[:, :], moesm[0:8, 40:41],
                               start=True, stop=True, skip_group_check=True).then_inc(*inc("st"))
                SEC("tensor", b7_cb)
                ST_CB = cnt["st"]

                def b7_cv(eng):
                    eng.wait_ge(sems["st"], ST_CB)
                    eng.tensor_copy(moesm[:, 41:42], psB[:, 16:17])
                    eng.drain()
                    eng.nop().then_inc(*inc("sv"))
                SEC("vector", b7_cv)
                SV_CV = cnt["sv"]

                def b7_lcs(eng):
                    eng.wait_ge(sems["sv"], SV_CV)
                    eng.matmul(psB[0:8, 0:128], c_ind8f[:, :], m_l,
                               start=True, stop=True, skip_group_check=True).then_inc(*inc("st"))
                SEC("tensor", b7_lcs)
                ST_LCS = cnt["st"]

                def b7_lscan(eng):
                    eng.wait_ge(sems["st"], ST_LCS)
                    eng.tensor_copy(PW[:, 0:128], psB[0:8, 0:128])
                    eng.drain()
                    eng.tensor_tensor_scan(PW[:, 128:256], c_ones81[:, :].broadcast_to([8, 128]), PW[:, 0:128],
                                           0.0, ALU.mult, ALU.add)
                    eng.drain()
                    eng.tensor_tensor(PW[:, 128:256], PW[:, 128:256], PW[:, 0:128], ALU.subtract).then_inc(*inc("sv"))
                    eng.drain()
                SEC("vector", b7_lscan)
                SV_LSC = cnt["sv"]

                def b7_slot(eng):
                    eng.wait_ge(sems["sv"], SV_LSC)
                    eng.matmul(psA[:, 0:128], c_ind8Tf[:, :], PW[:, 128:256],
                               start=True, stop=False, skip_group_check=True)
                    eng.matmul(psA[:, 0:128], c_triblk[:, :], m_l,
                               start=False, stop=True, skip_group_check=True).then_inc(*inc("st"))
                SEC("tensor", b7_slot)
                ST_SLOT = cnt["st"]

                def b7_adj(eng):
                    eng.wait_ge(sems["st"], ST_SLOT)
                    eng.tensor_scalar(ZB[:, 1024:1152], psA[:, 0:128], moesm[:, 41:42], None, ALU.add)
                    eng.drain()
                    eng.tensor_scalar(ZB[:, 1024:1152], ZB[:, 1024:1152], 2047.0, c_offv[:, 0:1],
                                      ALU.min, ALU.add).then_inc(*inc("sv"))
                    eng.drain()
                SEC("vector", b7_adj)
                SV_ADJ = cnt["sv"]

                SV_IDX = cnt["sv"]

                def b8_sc(eng):
                    eng.wait_ge(sems["sv"], SV_ADJ)
                    eng.tensor_scalar(UB[:, 0:128], m_l, -60000.0, 60000.0, ALU.mult, ALU.add)
                    eng.drain()
                    eng.tensor_tensor(UB[:, 128:256], ZB[:, 1024:1152], UB[:, 0:128], ALU.add)
                    eng.drain()
                    eng.tensor_copy(KIDX[:, 0:128], UB[:, 128:256])
                    eng.drain()
                    eng.tensor_scalar(UB[:, 128:256], c_gidxp1[:, 0:128], c_offv[:, 0:1], -1.0,
                                      ALU.add, ALU.add)
                    eng.drain()
                    eng.tensor_copy(KIDX[:, 128:256], UB[:, 128:256])
                    eng.drain()
                    eng.memset(UB[:, 1536:2048], 0.0)
                    eng.drain()
                    for qq in range(4):
                        cp = eng.tensor_copy(
                            UB[:, 1024:1536].rearrange("p (a b) -> p a b", b=4)[:, :, qq:qq + 1].squeeze(),
                            ZB[:, 1024:1152])
                        if qq == 3:
                            cp.then_inc(*inc("sv"))
                        eng.drain()
                SEC("vector", b8_sc)
                SV_SC = cnt["sv"]

                def b8_z(eng):
                    eng.wait_ge(sems["sv"], SV_SC)
                    DMA(eng, "sd", out=idbufd.rearrange("(p j) o -> p (j o)", p=128),
                        in_=UB[:, 1536:2048])
                SEC("sync", b8_z)
                SD_Z = cnt["sd"]

                def b8_scat(eng):
                    eng.wait_ge(sems["sv"], SV_SC)
                    eng.wait_ge(sems["sd"], SD_Z)
                    breg = eng.alloc_register()
                    eng.reg_mov(breg, NTOK - 1)
                    for Fl in range(128):
                        eng.indirect_dma_start(
                            out=idbufd,
                            out_offset=IndirectOffsetOnAxis(ap=KIDX[:, Fl:Fl + 1], axis=0),
                            in_=c_tidg[:, 4 * Fl:4 * Fl + 4], in_offset=None,
                            bounds_check=breg, oob_is_err=False).then_inc(*inc("sgt", 16))
                    eng.wait_ge(sems["sgt"], cnt["sgt"])
                    for Fl in range(128):
                        eng.indirect_dma_start(
                            out=gslotd,
                            out_offset=IndirectOffsetOnAxis(ap=KIDX[:, 128 + Fl:129 + Fl], axis=0),
                            in_=UB[:, 1024 + 4 * Fl:1028 + 4 * Fl], in_offset=None).then_inc(*inc("sgt", 16))
                    eng.wait_ge(sems["sgt"], cnt["sgt"])
                    eng.collective_compute("AllReduce", ALU.add,
                                           replica_groups=replica_groups.LNC1_1x8,
                                           ins=[idbufd], outs=[idbufr]).then_inc(*inc("sq"))
                SEC("gpsimd", b8_scat)
                SQ_AR = cnt["sq"]

                def b8_ld(eng):
                    eng.wait_ge(sems["sq"], SQ_AR)
                    DMA(eng, "sd", out=UB[:, 1024:1536].rearrange("p (x o) -> p x o", o=4),
                        in_=idbufr.rearrange("(x p) o -> p x o", p=128))
                    DMA(eng, "sd", out=UB[:, 1536:2048].rearrange("p (x o) -> p x o", o=4),
                        in_=gslotd.rearrange("(x p) o -> p x o", p=128))
                SEC("sync", b8_ld)
                SD_IDL = cnt["sd"]

                def b8_mi(eng):
                    eng.wait_ge(sems["sd"], SD_IDL)
                    eng.tensor_copy(UB[:, 0:128],
                                    UB[:, 1024:1536].rearrange("p (x o) -> p x o", o=4)[:, :, 0:1].squeeze())
                    eng.drain()
                    eng.tensor_copy(UB[:, 256:384].rearrange("p (j e) -> p j e", e=8),
                                    UB[:, 0:128].rearrange("p (e j) -> p e j", j=16).transpose([0, 2, 1]))
                    eng.drain()
                    eng.tensor_copy(UB[:, 0:128],
                                    UB[:, 1536:2048].rearrange("p (x o) -> p x o", o=4)[:, :, 0:1].squeeze())
                    eng.drain()
                    eng.tensor_copy(UB[:, 384:512].rearrange("p (b e) -> p b e", e=8),
                                    UB[:, 0:128].rearrange("p (e b) -> p e b", b=16).transpose([0, 2, 1]))
                    eng.drain()
                    eng.tensor_tensor(UB[:, 0:128], UB[:, 256:384], c_myexp[:, :], ALU.mult)
                    eng.drain()
                    eng.tensor_reduce(UB[:, 128:144],
                                      UB[:, 0:128].rearrange("p (j e) -> p j e", e=8),
                                      AX.X, ALU.add)
                    eng.drain()
                    eng.tensor_copy(KIDX[:, 256:272], UB[:, 128:144])
                    eng.drain()
                    eng.tensor_copy(KIDX[:, 272:400], UB[:, 384:512]).then_inc(*inc("sv"))
                    eng.drain()
                SEC("vector", b8_mi)
                SV_MI = cnt["sv"]

                NBLK = 4
                sv_y = {}
                sq_g = {}
                sd_yd = {}
                last_mm1_marker = 0
                for blk in range(NBLK):
                    st_xt = {}
                    sv_xe = {}
                    for sb4 in range(4):
                        def b8_xg(eng, blk=blk, sb4=sb4):
                            if blk == 0 and sb4 == 0:
                                eng.wait_ge(sems["sq"], SQ_AGX)
                                eng.wait_ge(sems["sv"], SV_MI)
                            if blk > 0 and sb4 == 0:
                                eng.wait_ge(sems["st"], last_mm1_marker)
                            if sb4 >= 1:
                                eng.wait_ge(sems["st"], st_xt[sb4 - 1])
                            sbi = blk * 4 + sb4
                            eng.indirect_dma_start(
                                out=vaug[:, :, :].rearrange("p a b -> p (a b)")[:, 0:2048].bitcast(BF16)[:, 0:1024] if False else vaug[:, 0:8, 0:65].rearrange("p a b -> p (a b)")[:, 0:520].bitcast(BF16) if False else xgst,
                                out_offset=None,
                                in_=agx_out,
                                in_offset=IndirectOffsetOnAxis(ap=KIDX[:, 256 + sbi:257 + sbi], axis=0)).then_inc(*inc("sgt", 16))
                            eng.wait_ge(sems["sgt"], cnt["sgt"])
                            eng.nop().then_inc(*inc("sq"))
                        SEC("gpsimd", b8_xg)
                        sq_xg = cnt["sq"]

                        def b8_xt(eng, blk=blk, sb4=sb4, pvy=None):
                            eng.wait_ge(sems["sq"], sq_xg)
                            if sb4 >= 1:
                                eng.wait_ge(sems["sv"], sv_xe[sb4 - 1])
                            if sb4 == 0 and blk > 0:
                                eng.wait_ge(sems["sv"], sv_y[blk - 1])
                            for c in range(8):
                                tr = eng.transpose(psBbf[:, sl(c, 128)], xgst[:, sl(c, 128)],
                                                   c_identbf[:, :])
                                if c == 7:
                                    tr.then_inc(*inc("st"))
                        SEC("tensor", b8_xt)
                        st_xt[sb4] = cnt["st"]

                        def b8_xe(eng, blk=blk, sb4=sb4):
                            eng.wait_ge(sems["st"], st_xt[sb4])
                            eng.tensor_copy(
                                UBbf[:, :].rearrange("p (c t) -> p c t", c=8)[:, :, sl(sb4, 128)],
                                psBbf[:, 0:1024].rearrange("p (c t) -> p c t", c=8)).then_inc(*inc("sv"))
                            eng.drain()
                        SEC("vector", b8_xe)
                        sv_xe[sb4] = cnt["sv"]
                    sq_g[blk] = cnt["sv"]

                    st_mm1 = {}
                    ss_h = {}
                    for fs in range(32):
                        def b8_w1(eng, blk=blk, fs=fs, lastm=last_mm1_marker):
                            if fs >= 2:
                                eng.wait_ge(sems["st"], st_mm1[fs - 2])
                            elif lastm:
                                eng.wait_ge(sems["st"], lastm)
                            else:
                                eng.wait_ge(sems["sd"], B0L)
                                eng.wait_ge(sems["sp"], SP_RWKV)
                            DMA(eng, "sd", out=wslA[fs % 2][:],
                                          in_=w1b[:, sl(fs, 128)].rearrange("(c p) e -> p c e", p=128))
                        SEC("sync", b8_w1)
                        SD_W1 = cnt["sd"]

                        def b8_mm1(eng, blk=blk, fs=fs):
                            eng.wait_ge(sems["sd"], SD_W1)
                            eng.wait_ge(sems["sv"], sq_g[blk])
                            if fs >= 2:
                                eng.wait_ge(sems["ss"], ss_h[fs - 2])
                            if fs < 2 and blk > 0:
                                eng.wait_ge(sems["sv"], sv_y[blk - 1])
                            for c in range(8):
                                mm = eng.matmul(psB[:, 0:512] if fs % 2 == 0 else psBf32[:, 0:512],
                                                wslA[fs % 2][:, c:c + 1, :].squeeze(),
                                                UBbf[:, :].rearrange("p (c t) -> p c t", c=8)[:, c:c + 1, 0:512].squeeze(),
                                                start=(c == 0), stop=(c == 7), skip_group_check=True)
                                if c == 7:
                                    mm.then_inc(*inc("st"))
                        SEC("tensor", b8_mm1)
                        st_mm1[fs] = cnt["st"]

                        def b8_h(eng, blk=blk, fs=fs):
                            eng.wait_ge(sems["st"], st_mm1[fs])
                            eng.activation(out=hbv[:, sl(fs, 512)],
                                           in_=psB[:, 0:512] if fs % 2 == 0 else psBf32[:, 0:512],
                                           func=ACT.Gelu_apprx_tanh,
                                           bias=c_b1r[:, fs:fs + 1]).then_inc(*inc("ss"))
                        SEC("scalar", b8_h)
                        ss_h[fs] = cnt["ss"]
                    last_mm1_marker = st_mm1[31]

                    banks = [psD2[:, sl(0, 512)], psD2[:, sl(1, 512)], psD2[:, sl(2, 512)],
                             psD2[:, sl(3, 512)], psA[:, sl(0, 512)], psA[:, sl(1, 512)],
                             psB[:, 0:512], psBf32[:, 0:512]]
                    st_w2 = {}
                    sq_w2 = {}
                    for fc in range(32):
                        def b8_w2(eng, blk=blk, fc=fc):
                            if fc < 2:
                                eng.wait_ge(sems["ss"], ss_h[31])
                            if fc >= 2:
                                eng.wait_ge(sems["st"], st_w2[fc - 2])
                            DMA(eng, "sd", out=wslB[fc % 2][:],
                                          in_=w2b[sl(fc, 128), :].rearrange("p (c e) -> p c e", c=8))
                        SEC("sync", b8_w2)
                        sd_w2b = cnt["sd"]

                        def b8_mm2(eng, blk=blk, fc=fc):
                            eng.wait_ge(sems["sd"], sd_w2b)
                            if fc == 0:
                                eng.wait_ge(sems["sv"], sv_y.get(blk - 1, SV_IDX))
                            w2v = wslB[fc % 2][:, :, :].rearrange("p c e -> p (c e)")
                            for si in range(4):
                                for sp2 in range(2):
                                    mm = eng.matmul(banks[si * 2 + sp2],
                                                    hbv[:, sl(fc, 512)][:, sl(si, 128)],
                                                    w2v[:, sl(sp2, 512)],
                                                    start=(fc == 0), stop=(fc == 31),
                                                    skip_group_check=True)
                                    if si == 3 and sp2 == 1:
                                        mm.then_inc(*inc("st"))
                        SEC("tensor", b8_mm2)
                        st_w2[fc] = cnt["st"]

                    def b8_y(eng, blk=blk):
                        eng.wait_ge(sems["st"], st_w2[31])
                        if blk > 0:
                            eng.wait_ge(sems["sd"], sd_yd[blk - 1])
                        for si in range(4):
                            for sp2 in range(2):
                                tt2 = eng.tensor_tensor(yv[:, si * 1024 + sp2 * 512: si * 1024 + sp2 * 512 + 512],
                                                        banks[si * 2 + sp2], c_b2b[:, sl(sp2, 512)], ALU.add)
                                if si == 3 and sp2 == 1:
                                    tt2.then_inc(*inc("sv"))
                        eng.drain()
                    SEC("vector", b8_y)
                    sv_y[blk] = cnt["sv"]

                    def b8_yd(eng, blk=blk):
                        eng.wait_ge(sems["sv"], sv_y[blk])
                        DMA(eng, "sd", out=agy_in[sl(blk, 512), :].rearrange("(s p) d -> p s d", p=128),
                                      in_=yv[:, 0:4096].rearrange("p (s d) -> p s d", s=4))
                    SEC("sync", b8_yd)
                    sd_yd[blk] = cnt["sd"]
                SD_Y = cnt["sd"]

                def b8_ag(eng):
                    eng.wait_ge(sems["sd"], SD_Y)
                    eng.collective_compute("AllGather", ALU.bypass,
                                           replica_groups=replica_groups.LNC1_1x8,
                                           ins=[agy_in], outs=[agy_out]).then_inc(*inc("sq"))
                SEC("gpsimd", b8_ag)
                SQ_AGY = cnt["sq"]

                if BSTOP < 8:
                    return
                def b9_g(eng):
                    eng.wait_ge(sems["sv"], SV_TAU)
                    eng.wait_ge(sems["sv"], sv_y[NBLK - 1])
                    eng.matmul(psB[0:8, 20:21], c_ind8f16[:, :], moesm[:, 32:33],
                               start=True, stop=True, skip_group_check=True).then_inc(*inc("st"))
                SEC("tensor", b9_g)
                ST_T8 = cnt["st"]

                def b9_g2v(eng):
                    eng.wait_ge(sems["st"], ST_T8)
                    eng.tensor_copy(moesm[0:8, 42:43], psB[0:8, 20:21]).then_inc(*inc("sv"))
                    eng.drain()
                SEC("vector", b9_g2v)
                SV_T8a = cnt["sv"]

                def b9_g2(eng):
                    eng.wait_ge(sems["sv"], SV_T8a)
                    eng.transpose(psB[0:1, 24:32], moesm[0:8, 42:43], c_ident[0:8, 0:8]).then_inc(*inc("st"))
                SEC("tensor", b9_g2)
                ST_T8b = cnt["st"]

                def b9_g3(eng):
                    eng.wait_ge(sems["st"], ST_T8b)
                    eng.tensor_copy(moesm[0:1, 44:52], psB[0:1, 24:32]).then_inc(*inc("sv"))
                    eng.drain()
                SEC("vector", b9_g3)
                SV_T8 = cnt["sv"]

                def b9_g4(eng):
                    eng.wait_ge(sems["sv"], SV_T8)
                    eng.matmul(psB[:, 24:32], c_ones1f[0:1, 0:128], moesm[0:1, 44:52],
                               start=True, stop=True, skip_group_check=True).then_inc(*inc("st"))
                SEC("tensor", b9_g4)
                ST_T8c = cnt["st"]

                def b9_g5(eng):
                    eng.wait_ge(sems["st"], ST_T8c)
                    eng.tensor_copy(moesm[:, 52:60], psB[:, 24:32])
                    eng.drain()
                    pv = probs[:, 0:128].rearrange("p (b e) -> p b e", e=8)
                    tb = moesm[:, 52:60].rearrange("p (o e) -> p o e", o=1).broadcast_to([128, 16, 8])
                    eng.tensor_tensor(UB[:, 0:128].rearrange("p (b e) -> p b e", e=8), pv, tb, ALU.is_ge)
                    eng.drain()
                    eng.tensor_tensor(probs[:, 128:256], probs[:, 0:128], UB[:, 0:128], ALU.mult).then_inc(*inc("sv"))
                    eng.drain()
                SEC("vector", b9_g5)
                SV_GV = cnt["sv"]

                gv = probs[:, 128:256].rearrange("p (b e) -> p b e", e=8)
                hbv3 = hbv[:, :].rearrange("p (b d) -> p b d", b=16)
                sq_gt = {}
                sv_acch = {}
                for ee in range(E):
                    for hh in range(2):
                        def b9_gt(eng, ee=ee, hh=hh):
                            if ee == 0 and hh == 0:
                                eng.wait_ge(sems["sq"], SQ_AGY)
                            if ee >= 1:
                                eng.wait_ge(sems["sv"], sv_acch[(ee - 1, hh)])
                            for b in range(hh * 8, hh * 8 + 8):
                                eng.indirect_dma_start(
                                    out=hbv3[:, b:b + 1, :].squeeze(),
                                    out_offset=None,
                                    in_=agy_out,
                                    in_offset=IndirectOffsetOnAxis(ap=KIDX[:, 272 + b * 8 + ee:273 + b * 8 + ee], axis=0)).then_inc(*inc("sgt", 16))
                            eng.wait_ge(sems["sgt"], cnt["sgt"])
                            eng.nop().then_inc(*inc("sq"))
                        SEC("gpsimd", b9_gt)
                        sq_gt[(ee, hh)] = cnt["sq"]

                        def b9_acc(eng, ee=ee, hh=hh):
                            eng.wait_ge(sems["sq"], sq_gt[(ee, hh)])
                            if ee == 0 and hh == 0:
                                eng.wait_ge(sems["sv"], SV_GV)
                            for b in range(hh * 8, hh * 8 + 8):
                                stt = eng.scalar_tensor_tensor(blk2(stream, b),
                                                               hbv3[:, b:b + 1, :].squeeze(),
                                                               probs[:, 128 + b * 8 + ee:129 + b * 8 + ee],
                                                               blk2(stream, b), ALU.mult, ALU.add)
                            stt.then_inc(*inc("sv"))
                            eng.drain()
                        SEC("vector", b9_acc)
                        sv_acch[(ee, hh)] = cnt["sv"]
                sv_acc = {E - 1: sv_acch[(E - 1, 1)]}

                if BSTOP < 9:
                    return
                def b10_s(eng):
                    eng.wait_ge(sems["sv"], sv_acc[E - 1])
                    for b in range(0, NB, 2):
                        eng.bn_stats(smallv[:, 0:6], blk2(stream, b)[:, 0:512])
                        eng.bn_stats(smallv[:, 6:12], blk2(stream, b)[:, 512:1024])
                        eng.bn_stats(smallv[:, 80:86], blk2(stream, b + 1)[:, 0:512])
                        eng.bn_stats(smallv[:, 86:92], blk2(stream, b + 1)[:, 512:1024])
                        eng.drain()
                        eng.bn_aggr(smallv[:, 2 * b + 16:2 * b + 18], smallv[:, 0:12])
                        ag = eng.bn_aggr(smallv[:, 2 * b + 18:2 * b + 20], smallv[:, 80:92])
                        if b == NB - 2:
                            ag.then_inc(*inc("sv"))
                        eng.drain()
                SEC("vector", b10_s)
                SV_B10 = cnt["sv"]

                def b10_q(eng):
                    eng.wait_ge(sems["sv"], SV_B10)
                    for b in range(NB):
                        aa = eng.activation(out=smallv[:, 48 + b:49 + b], in_=smallv[:, 2 * b + 17:2 * b + 18],
                                            func=ACT.Sqrt, bias=c_cv[:, 0:1])
                        if b == NB - 1:
                            aa.then_inc(*inc("ss"))
                    eng.drain()
                SEC("scalar", b10_q)
                SS_B10 = cnt["ss"]

                def b10_n(eng):
                    eng.wait_ge(sems["ss"], SS_B10)
                    eng.reciprocal(smallv[:, 64:80], smallv[:, 48:64])
                    eng.drain()
                    for b in range(NB):
                        tsn = eng.tensor_scalar(blk2(stream, b), blk2(stream, b),
                                                smallv[:, 2 * b + 16:2 * b + 17], smallv[:, 64 + b:65 + b],
                                                ALU.subtract, ALU.mult)
                        if b == NB - 1:
                            tsn.then_inc(*inc("sv"))
                    eng.drain()
                SEC("vector", b10_n)


            emit_phase_b()

            def emit_postln(wsv, wsq=0):
                def pl_s(eng):
                    eng.wait_ge(sems["sv"], wsv)
                    if wsq:
                        eng.wait_ge(sems["sq"], wsq)
                    for b in range(0, NB, 2):
                        eng.bn_stats(smallv[:, 0:6], blk2(stream, b)[:, 0:512])
                        eng.bn_stats(smallv[:, 6:12], blk2(stream, b)[:, 512:1024])
                        eng.bn_stats(smallv[:, 80:86], blk2(stream, b + 1)[:, 0:512])
                        eng.bn_stats(smallv[:, 86:92], blk2(stream, b + 1)[:, 512:1024])
                        eng.drain()
                        eng.bn_aggr(smallv[:, 2 * b + 16:2 * b + 18], smallv[:, 0:12])
                        ag = eng.bn_aggr(smallv[:, 2 * b + 18:2 * b + 20], smallv[:, 80:92])
                        if b == NB - 2:
                            ag.then_inc(*inc("sv"))
                        eng.drain()
                SEC("vector", pl_s)
                svs = cnt["sv"]

                def pl_q(eng):
                    eng.wait_ge(sems["sv"], svs)
                    for b in range(NB):
                        aa = eng.activation(out=smallv[:, 48 + b:49 + b], in_=smallv[:, 2 * b + 17:2 * b + 18],
                                            func=ACT.Sqrt, bias=c_cv[:, 0:1])
                        if b == NB - 1:
                            aa.then_inc(*inc("ss"))
                    eng.drain()
                SEC("scalar", pl_q)
                sss = cnt["ss"]

                def pl_n(eng):
                    eng.wait_ge(sems["ss"], sss)
                    eng.reciprocal(smallv[:, 64:80], smallv[:, 48:64])
                    eng.drain()
                    for b in range(NB):
                        tsn = eng.tensor_scalar(blk2(stream, b), blk2(stream, b),
                                                smallv[:, 2 * b + 16:2 * b + 17], smallv[:, 64 + b:65 + b],
                                                ALU.subtract, ALU.mult)
                        if b == NB - 1:
                            tsn.then_inc(*inc("sv"))
                    eng.drain()
                SEC("vector", pl_n)
                return cnt["sv"]

            def emit_phase_cd():
                if PHASES < 3:
                    return
                psD2 = _ph.get("psD2")
                if psD2 is None:
                    return
                CSTOP = int(os.environ.get("MK_CSTOP", "99"))
                DSTOP = int(os.environ.get("MK_DSTOP", "99"))
                SVB = cnt["sv"]
                STB = cnt["st"]

                lnTf = lnT[:, :, :].rearrange("p c t -> p (c t)")
                vaugf = vaug[:, :, :].rearrange("p a b -> p (a b)")
                sA0, sA1 = scrA[:, 0:1024], scrA[:, 1024:2048]
                sU0, sU1 = UB[:, 0:1024], UB[:, 1024:2048]
                sZ0, sZ1 = ZB[:, 0:1024], ZB[:, 1024:2048]
                gidbf = c_gidxp1[:, :].bitcast(BF16)
                gkob = [c_tidg[:, :].bitcast(BF16), vaugf[:, 0:1024]]
                zcol = vaugf[:, 1024:1025]
                wob = c_bout[:, :].bitcast(BF16)
                psBo = psBbf2 = psBbf[:, :].bitcast(F32)
                mixK = [TA[:, 0:1024], TA[:, 1024:2048], TB[:, 0:1024], TB[:, 1024:2048]]
                mixV = [SQ2[:, 0:1024], SQ2[:, 1024:2048], KQSQ[:, 0:1024], KQSQ[:, 1024:2048]]
                mixR = [wsc[:, 0:1024], wsc[:, 1024:2048], gidbf[:, 0:1024], gidbf[:, 1024:2048]]

                def x2hi(dc, b2):
                    return lnT[:, dc:dc + 1, b2 * 1024:(b2 + 1) * 1024].squeeze()

                def c0_d(eng):
                    eng.wait_ge(sems["sv"], SVB)
                    DMA(eng, "sd", out=moesm[:, 16:24], in_=mixk_r)
                    DMA(eng, "sd", out=moesm[:, 24:32], in_=mixv_r)
                    DMA(eng, "sd", out=moesm[:, 32:40], in_=mixr_r)
                    DMA(eng, "sd", out=smallv[:, 80:88], in_=decay_r)
                    DMA(eng, "sd", out=smallv[:, 88:96], in_=first_r)
                SEC("sync", c0_d)
                SD_C0 = cnt["sd"]

                def c0_s(eng):
                    eng.wait_ge(sems["sd"], SD_C0)
                    eng.activation(out=moesm[:, 40:48], in_=smallv[:, 80:88],
                                   func=ACT.Exp, scale=1.0 / T).then_inc(*inc("ss"))
                    eng.drain()
                SEC("scalar", c0_s)
                SS_C0E = cnt["ss"]

                def c0_v(eng):
                    eng.wait_ge(sems["sd"], SD_C0)
                    eng.tensor_scalar(moesm[:, 48:56], smallv[:, 88:96], 1.0 / T, None, ALU.mult)
                    eng.drain()
                    eng.memset(moesm[:, 0:16], 0.0)
                    eng.drain()
                    eng.memset(zcol, 0.0)
                    eng.drain()
                    eng.nop().then_inc(*inc("sv"))
                SEC("vector", c0_v)
                SV_C0 = cnt["sv"]

                st_c1 = {}
                ss_c1 = {}
                for b in range(NB):
                    def c1_tr(eng, b=b):
                        if b == 0:
                            eng.wait_ge(sems["sv"], SVB)
                        else:
                            eng.wait_ge(sems["ss"], ss_c1[b - 1])
                        for c in range(8):
                            tr = eng.transpose(psA[:, sl(c, 128)], blk2(stream, b)[:, sl(c, 128)], c_ident[:])
                            if c == 7:
                                tr.then_inc(*inc("st"))
                    SEC("tensor", c1_tr)
                    st_c1[b] = cnt["st"]

                    def c1_ev(eng, b=b):
                        eng.wait_ge(sems["st"], st_c1[b])
                        eng.activation(out=lnT[:, :, sl(b, 128)],
                                       in_=psA[:, 0:1024].rearrange("p (c t) -> p c t", c=8),
                                       func=ACT.Copy).then_inc(*inc("ss"))
                        eng.drain()
                    SEC("scalar", c1_ev)
                    ss_c1[b] = cnt["ss"]
                SS_C1 = cnt["ss"]

                if CSTOP < 2:
                    return

                mix_sv = {}
                sd_wsl = {}
                st_k = {}
                st_v = {}
                st_r = {}
                st_o_ts = {}
                st_o_last = {}
                ss_ke = {}
                ekuv_sv = {}
                den_sv = {}
                sr_ss = {}
                gko_sv = {}

                def c_ld(eng, j):
                    if j == 0:
                        eng.wait_ge(sems["sp"], SP_RWKV)
                        eng.wait_ge(sems["st"], STB)
                        eng.wait_ge(sems["sv"], SVB)
                    else:
                        eng.wait_ge(sems["st"], st_r.get(j - 2, STB))
                        eng.wait_ge(sems["st"], st_o_last.get(j - 2, STB))
                    co = j % 8
                    DMA(eng, "sd", out=wslA[j % 2][:],
                        in_=rwkvb[0:D, sl(co, 128)].rearrange("(d p) e -> p d e", p=128))
                    DMA(eng, "sd", out=wslB[j % 2][:],
                        in_=rwkvb[D:2 * D, sl(co, 128)].rearrange("(d p) e -> p d e", p=128))
                    DMA(eng, "sd", out=wslC[j % 2][:],
                        in_=rwkvb[2 * D:3 * D, sl(co, 128)].rearrange("(d p) e -> p d e", p=128))
                    DMA(eng, "sd", out=wob[:, sl(j % 2, 1024)],
                        in_=rwkvb[3 * D + co * 128:3 * D + (co + 1) * 128, :])

                def c_ld0(eng):
                    c_ld(eng, 0)
                SEC("sync", c_ld0)
                sd_wsl[0] = cnt["sd"]

                for gi in range(17):
                    b2, co = gi // 8, gi % 8
                    sv0 = cnt["sv"]

                    if gi >= 1:
                        def c_o(eng, j=gi - 1, sv0=sv0):
                            eng.wait_ge(sems["sv"], gko_sv[j])
                            for ts in range(8):
                                if ts >= 1:
                                    eng.wait_ge(sems["sv"], sv0 + ts)
                                eng.matmul(psB[:, 0:512], gkob[j % 2][:, sl(ts, 128)],
                                           wob[:, (j % 2) * 1024:(j % 2) * 1024 + 512],
                                           start=True, stop=True, skip_group_check=True)
                                mm = eng.matmul(psBo[:, 0:512], gkob[j % 2][:, sl(ts, 128)],
                                                wob[:, (j % 2) * 1024 + 512:(j % 2) * 1024 + 1024],
                                                start=True, stop=True, skip_group_check=True)
                                mm.then_inc(*inc("st"))
                                st_o_ts[(j, ts)] = cnt["st"]
                        SEC("tensor", c_o)
                        st_o_last[gi - 1] = cnt["st"]

                        def c_oadd(eng, j=gi - 1):
                            for ts in range(8):
                                eng.wait_ge(sems["st"], st_o_ts[(j, ts)])
                                tb = (j // 8) * 8 + ts
                                eng.tensor_tensor(blk2(stream, tb)[:, 0:512], psB[:, 0:512],
                                                  blk2(stream, tb)[:, 0:512], ALU.add)
                                eng.tensor_tensor(blk2(stream, tb)[:, 512:1024], psBo[:, 0:512],
                                                  blk2(stream, tb)[:, 512:1024], ALU.add).then_inc(*inc("sv"))
                            eng.drain()
                        SEC("vector", c_oadd)

                    if gi == 16:
                        break

                    if co == 0:
                        def c_mx(eng, b2=b2, gi=gi):
                            if b2 == 0:
                                eng.wait_ge(sems["ss"], SS_C1)
                                eng.wait_ge(sems["sd"], SD_C0)
                                eng.wait_ge(sems["sv"], SV_C0)
                            else:
                                eng.wait_ge(sems["st"], st_o_last[gi - 1])
                            dtile = gkob[0]
                            for c in range(4):
                                x2c = lnT[:, c:c + 1, :].squeeze()
                                if c < 2:
                                    if b2 == 0:
                                        eng.tensor_tensor(dtile[:, 1:1024], x2c[:, 1:1024],
                                                          x2c[:, 0:1023], ALU.subtract)
                                        eng.drain()
                                        eng.tensor_copy(dtile[:, 0:1], x2c[:, 0:1])
                                        eng.drain()
                                        xxm = x2c[:, 0:1023]
                                        rng = (1, 1024)
                                        edge = 0
                                    else:
                                        eng.tensor_tensor(dtile[:, 0:1024], x2c[:, 1024:2048],
                                                          x2c[:, 1023:2047], ALU.subtract)
                                        eng.drain()
                                        xxm = x2c[:, 1023:2047]
                                        rng = (0, 1024)
                                        edge = None
                                else:
                                    if b2 == 0:
                                        eng.tensor_tensor(dtile[:, 0:1024], x2c[:, 0:1024],
                                                          x2c[:, 1:1025], ALU.subtract)
                                        eng.drain()
                                        xxm = x2c[:, 1:1025]
                                        rng = (0, 1024)
                                        edge = None
                                    else:
                                        eng.tensor_tensor(dtile[:, 0:1023], x2c[:, 1024:2047],
                                                          x2c[:, 1025:2048], ALU.subtract)
                                        eng.drain()
                                        eng.tensor_copy(dtile[:, 1023:1024], x2c[:, 2047:2048])
                                        eng.drain()
                                        xxm = x2c[:, 1025:2048]
                                        rng = (0, 1023)
                                        edge = 1023
                                a, bb = rng
                                for mi, mcol, mt in ((0, 16, mixK[c]), (1, 24, mixV[c]), (2, 32, mixR[c])):
                                    sc = moesm[:, mcol + c:mcol + c + 1]
                                    eng.scalar_tensor_tensor(mt[:, a:bb], dtile[:, a:bb], sc, xxm,
                                                             ALU.mult, ALU.add)
                                    eng.drain()
                                    if edge is not None:
                                        stt = eng.scalar_tensor_tensor(mt[:, edge:edge + 1],
                                                                       dtile[:, edge:edge + 1], sc, zcol,
                                                                       ALU.mult, ALU.add)
                                        eng.drain()
                                    else:
                                        stt = None
                            eng.nop().then_inc(*inc("sv"))
                        SEC("vector", c_mx)
                        mix_sv[b2] = cnt["sv"]

                    if gi + 1 < 16:
                        def c_ldn(eng, j=gi + 1):
                            c_ld(eng, j)
                        SEC("sync", c_ldn)
                        sd_wsl[gi + 1] = cnt["sd"]

                    if PHASES >= 4:
                        def c_dc(eng, gi=gi):
                            DMA(eng, "sp", out=wfcb[sl(gi, 64), :], in_=wfcT[sl(gi, 64), :])
                            DMA(eng, "sp", out=wprojb[sl(gi, 256), :], in_=wprojT[sl(gi, 256), :])
                        SEC("gpsimd", c_dc)

                    def c_proj(eng, gi=gi, b2=b2):
                        eng.wait_ge(sems["sd"], sd_wsl[gi])
                        eng.wait_ge(sems["sv"], mix_sv[b2])
                        if gi >= 1:
                            eng.wait_ge(sems["sv"], ekuv_sv[gi - 1])
                            eng.wait_ge(sems["ss"], sr_ss[gi - 1])
                        for wt, mixd, pdst, stn in ((wslA, mixK, psD2[:, 0:1024], st_k),
                                                    (wslB, mixV, psD2[:, 1024:2048], st_v),
                                                    (wslC, mixR, psA[:, 0:1024], st_r)):
                            for sp in range(2):
                                for dc in range(8):
                                    rhs = (mixd[dc] if dc < 4 else x2hi(dc, b2))[:, sl(sp, 512)]
                                    mm = eng.matmul(pdst[:, sl(sp, 512)], wt[gi % 2][:, dc:dc + 1, :].squeeze(),
                                                    rhs, start=(dc == 0), stop=(dc == 7),
                                                    skip_group_check=True)
                                    if sp == 1 and dc == 7:
                                        mm.then_inc(*inc("st"))
                            stn[gi] = cnt["st"]
                    SEC("tensor", c_proj)

                    def c_s1(eng, gi=gi, co=co):
                        eng.wait_ge(sems["st"], st_k[gi])
                        if gi == 0:
                            eng.wait_ge(sems["ss"], SS_C0E)
                        eng.activation(out=sA0, in_=psD2[:, 0:1024], func=ACT.Exp,
                                       bias=moesm[:, 48 + co:49 + co])
                        eng.drain()
                        eng.activation(out=sA1, in_=psD2[:, 0:1024], func=ACT.Exp).then_inc(*inc("ss"))
                        eng.drain()
                    SEC("scalar", c_s1)
                    ss_ke[gi] = cnt["ss"]

                    def c_el(eng, gi=gi, co=co):
                        eng.wait_ge(sems["ss"], ss_ke[gi])
                        eng.wait_ge(sems["st"], st_v[gi])
                        if gi == 0:
                            eng.wait_ge(sems["sv"], SV_C0)
                        eng.tensor_tensor(sU0, sA1, psD2[:, 1024:2048], ALU.mult)
                        eng.drain()
                        eng.tensor_tensor(sU1, sA0, psD2[:, 1024:2048], ALU.mult).then_inc(*inc("sv"))
                        eng.drain()
                        eng.tensor_copy(moesm[:, 56:57], moesm[:, co:co + 1])
                        eng.drain()
                        eng.tensor_copy(moesm[:, 57:58], moesm[:, 8 + co:9 + co])
                        eng.drain()
                        ewb = moesm[:, 40 + co:41 + co].broadcast_to([128, 1024])
                        eng.tensor_tensor_scan(sZ0, ewb, sU0, moesm[:, co:co + 1], ALU.mult, ALU.add)
                        eng.drain()
                        eng.tensor_tensor_scan(sZ1, ewb, sA1, moesm[:, 8 + co:9 + co], ALU.mult, ALU.add)
                        eng.drain()
                        eng.tensor_copy(moesm[:, co:co + 1], sZ0[:, 1023:1024])
                        eng.drain()
                        eng.tensor_copy(moesm[:, 8 + co:9 + co], sZ1[:, 1023:1024])
                        eng.drain()
                        eng.tensor_tensor(sA1[:, 1:1024], sU1[:, 1:1024], sZ0[:, 0:1023], ALU.add)
                        eng.drain()
                        eng.tensor_tensor(sA1[:, 0:1], sU1[:, 0:1], moesm[:, 56:57], ALU.add)
                        eng.drain()
                        eng.tensor_tensor(sU0[:, 1:1024], sA0[:, 1:1024], sZ1[:, 0:1023], ALU.add)
                        eng.drain()
                        eng.tensor_tensor(sU0[:, 0:1], sA0[:, 0:1], moesm[:, 57:58], ALU.add).then_inc(*inc("sv"))
                        eng.drain()
                    SEC("vector", c_el)
                    den_sv[gi] = cnt["sv"]
                    ekuv_sv[gi] = den_sv[gi] - 1

                    def c_s2(eng, gi=gi):
                        eng.wait_ge(sems["sv"], den_sv[gi])
                        eng.activation(out=sZ0, in_=sU0, func=ACT.Ln)
                        eng.drain()
                        eng.activation(out=sZ1, in_=sZ0, func=ACT.Exp, scale=-1.0)
                        eng.drain()
                        eng.wait_ge(sems["st"], st_r[gi])
                        eng.activation(out=sA0, in_=psA[:, 0:1024], func=ACT.Sigmoid).then_inc(*inc("ss"))
                        eng.drain()
                    SEC("scalar", c_s2)
                    sr_ss[gi] = cnt["ss"]

                    def c_gk(eng, gi=gi):
                        eng.wait_ge(sems["ss"], sr_ss[gi])
                        if gi >= 2:
                            eng.wait_ge(sems["st"], st_o_last[gi - 2])
                        eng.tensor_tensor(sU1, sA1, sZ1, ALU.mult)
                        eng.drain()
                        eng.tensor_tensor(gkob[gi % 2], sU1, sA0, ALU.mult).then_inc(*inc("sv"))
                        eng.drain()
                    SEC("vector", c_gk)
                    gko_sv[gi] = cnt["sv"]

                SP_ALL = cnt["sp"]
                SV_CL = cnt["sv"]
                LNC_sv = emit_postln(SV_CL)
                STC_LAST = cnt["st"]

                if PHASES < 4 or CSTOP < 3:
                    return

                x3Tv = UB[:, :].bitcast(BF16).rearrange("p (c t) -> p c t", t=512)
                wpb = [c_tidg[:, :].bitcast(BF16), vaugf[:, 0:1024]]

                pd_ts = [psA[:, 0:1024], psD2[:, 0:1024], psD2[:, 1024:2048], None]
                ppiece = [[(pd_ts[0], 0, 1024)], [(pd_ts[1], 0, 1024)], [(pd_ts[2], 0, 1024)],
                          [(psB[:, 0:512], 0, 512), (psBo[:, 0:512], 512, 512)]]
                KQf32 = KQSQ[:, :].bitcast(F32)
                dscanM = [scrA[:, 0:1024], scrA[:, 1024:2048], ZB[:, 0:1024], ZB[:, 1024:2048]]
                dscanO = [UB[:, 0:1024], UB[:, 1024:2048], c_gidxp1[:, 0:1024], c_bout[:, 0:1024]]
                yadjs = [TA[:, 0:1024], TA[:, 1024:2048], TB[:, 0:1024], TB[:, 1024:2048]]
                spks = [wsc[:, 0:1024], wsc[:, 1024:2048], SQ2[:, 0:1024], SQ2[:, 1024:2048]]

                def emit_scan_iters(dd):
                    Mset = list(dscanM)
                    Oset = list(dscanO)
                    sv_mem = {}
                    ss_sign = {}
                    for k in range(2, 6):
                        def d_vit(eng, dd=dd, k=k, Mset=Mset[:], Oset=Oset[:]):
                            if k == 2:
                                eng.wait_ge(sems["ss"], ss_init[dd])
                            else:
                                eng.wait_ge(sems["ss"], ss_sign[k - 1])
                            for i in range(4):
                                eng.scalar_tensor_tensor(blk2(stream, 4 * dd + i), spks[i], 0.1,
                                                         blk2(stream, 4 * dd + i), ALU.mult, ALU.add)
                            for i in range(4):
                                eng.scalar_tensor_tensor(Oset[i], spks[i], -0.5, yadjs[i],
                                                         ALU.mult, ALU.add)
                            eng.drain()
                            for i in range(4):
                                mmo = eng.scalar_tensor_tensor(Oset[i], Mset[i], 0.9, Oset[i],
                                                               ALU.mult, ALU.add)
                            mmo.then_inc(*inc("sv"))
                            eng.drain()
                        SEC("vector", d_vit)
                        sv_mem[k] = cnt["sv"]

                        def d_sg(eng, dd=dd, k=k, Oset=Oset[:]):
                            eng.wait_ge(sems["sv"], sv_mem[k])
                            for i in range(4):
                                sg = eng.activation(out=spks[i], in_=Oset[i], func=ACT.Sign,
                                                    bias=c_cv[:, 3:4])
                            sg.then_inc(*inc("ss"))
                        SEC("scalar", d_sg)
                        ss_sign[k] = cnt["ss"]

                        Mset[:], Oset[:] = Oset[:], Mset[:]

                    def d_res5(eng, dd=dd):
                        eng.wait_ge(sems["ss"], ss_sign[5])
                        for i in range(4):
                            rr = eng.scalar_tensor_tensor(blk2(stream, 4 * dd + i), spks[i], 0.1,
                                                          blk2(stream, 4 * dd + i), ALU.mult, ALU.add)
                        rr.then_inc(*inc("sv"))
                        eng.drain()
                    SEC("vector", d_res5)
                    scan_end[dd] = cnt["sv"]

                st_dtr = {}
                ss_x3 = {}
                sd_w1 = {}
                st_mm1 = {}
                ss_r1 = {}
                sv_h = {}
                sd_w2 = {}
                st_mm2 = {}
                ss_init = {}
                scan_end = {}

                for d in range(min(4, DSTOP)):
                    for bl in range(4):
                        trps = psA if bl % 2 == 0 else psD2
                        def d_tr(eng, d=d, bl=bl, trps=trps):
                            if bl == 0:
                                if d == 0:
                                    eng.wait_ge(sems["sv"], LNC_sv)
                                else:
                                    eng.wait_ge(sems["ss"], ss_init[d - 1])
                            elif bl >= 2:
                                eng.wait_ge(sems["ss"], ss_x3[(d, bl - 2)])
                            for c in range(8):
                                tr = eng.transpose(trps[:, sl(c, 128)],
                                                   blk2(stream, 4 * d + bl)[:, sl(c, 128)], c_ident[:])
                                if c == 7:
                                    tr.then_inc(*inc("st"))
                        SEC("tensor", d_tr)
                        st_dtr[(d, bl)] = cnt["st"]

                        def d_x3(eng, d=d, bl=bl, trps=trps):
                            eng.wait_ge(sems["st"], st_dtr[(d, bl)])
                            eng.activation(out=x3Tv[:, :, sl(bl, 128)],
                                           in_=trps[:, 0:1024].rearrange("p (c t) -> p c t", c=8),
                                           func=ACT.Copy).then_inc(*inc("ss"))
                        SEC("scalar", d_x3)
                        ss_x3[(d, bl)] = cnt["ss"]

                    def d_w1f(eng, d=d):
                        for fs in range(2):
                            if d == 0:
                                eng.wait_ge(sems["sp"], SP_ALL)
                                eng.wait_ge(sems["st"], STC_LAST)
                            else:
                                eng.wait_ge(sems["st"], st_mm1[(d - 1, 30 + fs)])
                            DMA(eng, "sd", out=wslA[fs % 2][:],
                                in_=wfcb[:, sl(fs, 128)].rearrange("(d p) e -> p d e", p=128))
                            sd_w1[(d, fs)] = cnt["sd"]
                    SEC("sync", d_w1f)

                    for fs in range(32):
                        def d_mm1(eng, d=d, fs=fs):
                            eng.wait_ge(sems["sd"], sd_w1[(d, fs)])
                            eng.wait_ge(sems["ss"], ss_x3[(d, 3)])
                            if fs >= 4:
                                eng.wait_ge(sems["sv"], sv_h[(d, fs - 4)])
                            elif d >= 1:
                                eng.wait_ge(sems["ss"], ss_init[d - 1])
                            for dc in range(8):
                                mm = eng.matmul(psD2[:, sl(fs % 4, 512)], wslA[fs % 2][:, dc:dc + 1, :].squeeze(),
                                                x3Tv[:, dc:dc + 1, :].squeeze(), start=(dc == 0), stop=(dc == 7),
                                                skip_group_check=True)
                                if dc == 7:
                                    mm.then_inc(*inc("st"))
                        SEC("tensor", d_mm1)
                        st_mm1[(d, fs)] = cnt["st"]

                        if fs + 2 < 32:
                            def d_w1n(eng, d=d, fs=fs + 2):
                                eng.wait_ge(sems["st"], st_mm1[(d, fs - 2)])
                                DMA(eng, "sd", out=wslA[fs % 2][:],
                                    in_=wfcb[:, sl(fs, 128)].rearrange("(d p) e -> p d e", p=128))
                            SEC("sync", d_w1n)
                            sd_w1[(d, fs + 2)] = cnt["sd"]

                        def d_relu(eng, d=d, fs=fs):
                            eng.wait_ge(sems["st"], st_mm1[(d, fs)])
                            if fs >= 2:
                                eng.wait_ge(sems["sv"], sv_h[(d, fs - 2)])
                            eng.activation(out=KQf32[:, sl(fs % 2, 512)],
                                           in_=psD2[:, sl(fs % 4, 512)],
                                           func=ACT.Relu).then_inc(*inc("ss"))
                        SEC("scalar", d_relu)
                        ss_r1[(d, fs)] = cnt["ss"]

                        def d_h(eng, d=d, fs=fs):
                            eng.wait_ge(sems["ss"], ss_r1[(d, fs)])
                            eng.tensor_tensor(lnTf[:, sl(fs, 512)],
                                              KQf32[:, sl(fs % 2, 512)],
                                              psD2[:, sl(fs % 4, 512)], ALU.mult).then_inc(*inc("sv"))
                        SEC("vector", d_h)
                        sv_h[(d, fs)] = cnt["sv"]

                    if d >= 1:
                        emit_scan_iters(d - 1)

                    def d_w2f(eng, d=d):
                        for fc in range(2):
                            if d == 0:
                                eng.wait_ge(sems["sv"], LNC_sv)
                            else:
                                eng.wait_ge(sems["st"], st_mm2[(d - 1, 30 + fc)])
                            DMA(eng, "sd", out=wpb[fc % 2], in_=wprojb[sl(fc, 128), :])
                            sd_w2[(d, fc)] = cnt["sd"]
                    SEC("sync", d_w2f)

                    pd_ts = [psA[:, 0:1024], psD2[:, 0:1024], psD2[:, 1024:2048], None]
                    for fc in range(32):
                        def d_mm2(eng, d=d, fc=fc):
                            eng.wait_ge(sems["sd"], sd_w2[(d, fc)])
                            if fc == 0:
                                eng.wait_ge(sems["sv"], sv_h[(d, 31)])
                                if d >= 1:
                                    eng.wait_ge(sems["ss"], ss_init[d - 1])
                            for ts in range(4):
                                for sp in range(2):
                                    if ts < 3:
                                        dst = pd_ts[ts][:, sl(sp, 512)]
                                    else:
                                        dst = psB[:, 0:512] if sp == 0 else psBo[:, 0:512]
                                    mm = eng.matmul(dst, lnTf[:, fc * 512 + ts * 128:fc * 512 + ts * 128 + 128],
                                                    wpb[fc % 2][:, sl(sp, 512)], start=(fc == 0), stop=(fc == 31),
                                                    skip_group_check=True)
                                    if ts == 3 and sp == 1:
                                        mm.then_inc(*inc("st"))
                        SEC("tensor", d_mm2)
                        st_mm2[(d, fc)] = cnt["st"]

                        if fc + 2 < 32:
                            def d_w2n(eng, d=d, fc=fc + 2):
                                eng.wait_ge(sems["st"], st_mm2[(d, fc - 2)])
                                DMA(eng, "sd", out=wpb[fc % 2], in_=wprojb[sl(fc, 128), :])
                            SEC("sync", d_w2n)
                            sd_w2[(d, fc + 2)] = cnt["sd"]

                    def d_init(eng, d=d):
                        eng.wait_ge(sems["st"], st_mm2[(d, 31)])
                        if d >= 1:
                            eng.wait_ge(sems["sv"], scan_end[d - 1])
                        for i in range(4):
                            for pap, off, w in ppiece[i]:
                                eng.activation(out=dscanM[i][:, off:off + w], in_=pap, func=ACT.Copy)
                                eng.activation(out=yadjs[i][:, off:off + w], in_=pap,
                                               func=ACT.Identity, bias=c_cv[:, 4:5])
                                sg = eng.activation(out=spks[i][:, off:off + w], in_=pap,
                                                    func=ACT.Sign, bias=c_cv[:, 3:4])
                        sg.then_inc(*inc("ss"))
                    SEC("scalar", d_init)
                    ss_init[d] = cnt["ss"]

                if DSTOP >= 1:
                    emit_scan_iters(min(4, DSTOP) - 1)
                if DSTOP >= 4:
                    emit_postln(scan_end[3])

            emit_phase_cd()

            SV_FIN = cnt["sv"]
            SS_FIN = cnt["ss"]
            SQ_FIN = cnt["sq"]

            def a_out(eng):
                eng.wait_ge(sems["sv"], SV_FIN)
                eng.wait_ge(sems["ss"], SS_FIN)
                eng.wait_ge(sems["sq"], SQ_FIN)
                DMA(eng, "so", out=out_d.rearrange("(b p) d -> p b d", p=128), in_=stream[:])
                eng.wait_ge(sems["so"], cnt["so"])
            SEC("sync", a_out)


    return nc


def _prep_inputs(inputs):
    f32 = np.float32
    import ml_dtypes
    bf = ml_dtypes.bfloat16
    X = np.ascontiguousarray(inputs["L0_attn_X"], dtype=f32)
    W_in = np.ascontiguousarray(inputs["L0_attn_W_in"], dtype=f32)
    b_in = np.ascontiguousarray(inputs["L0_attn_b_in"], dtype=f32)
    W_out = np.ascontiguousarray(inputs["L0_attn_W_out"], dtype=f32)
    b_out = np.ascontiguousarray(inputs["L0_attn_b_out"], dtype=f32)

    order = []
    for h in range(H):
        base = h * 4 * DH
        order += list(range(base + 2 * DH, base + 3 * DH))
        order += list(range(base + 0 * DH, base + 1 * DH))
        order += list(range(base + 3 * DH, base + 4 * DH))
        order += list(range(base + 1 * DH, base + 2 * DH))
    order = np.array(order)
    winT = np.ascontiguousarray(W_in[order].T)
    binr = np.ascontiguousarray(b_in[order].reshape(32, 128).T)
    woutT = np.ascontiguousarray(W_out.T.reshape(H, DH, D))
    bout_b = np.ascontiguousarray(np.broadcast_to(b_out, (128, D)))

    ident = np.eye(128, dtype=f32)
    ident_bf = ident.astype(bf)
    csel = np.zeros((128, 4), dtype=bf)
    csel[64:128, 0] = 1.0
    csel[64:128, 3] = 1.0
    ones1f = np.ones((1, 128), dtype=f32)
    cvals = np.zeros((128, 8), dtype=f32)
    cvals[:, 0] = LN_EPS
    cvals[:, 1] = SHEP_EPS
    cvals[:, 2] = 1.0
    cvals[:, 3] = -1.0
    cvals[:, 4] = -0.5
    onesrow = np.ones((1, 512), dtype=bf)

    Ws = np.ascontiguousarray(inputs["L0_ffn_Ws"], dtype=f32)
    bs = np.ascontiguousarray(inputs["L0_ffn_bs"], dtype=f32)
    W1 = np.ascontiguousarray(inputs["L0_ffn_W1"], dtype=f32)
    b1 = np.ascontiguousarray(inputs["L0_ffn_b1"], dtype=f32)
    W2 = np.ascontiguousarray(inputs["L0_ffn_W2"], dtype=f32)
    b2 = np.ascontiguousarray(inputs["L0_ffn_b2"], dtype=f32)
    pidx = np.arange(128)
    ind8f_h = (pidx[:, None] // 16 == np.arange(8)[None, :]).astype(f32)
    ind8Tf_h = np.ascontiguousarray(ind8f_h.T)
    selblk_h = (pidx[:, None] // 16 == pidx[None, :] // 16).astype(f32)
    triblk_h = ((pidx[:, None] // 16 == pidx[None, :] // 16) &
                (pidx[:, None] % 16 < pidx[None, :] % 16)).astype(f32)
    gidxp1_h = (np.arange(1024)[None, :] * 16 + (pidx % 16)[:, None] + 1).astype(f32)
    offv_h = ((pidx // 16) * 2048).astype(f32).reshape(128, 1)
    def r128(v):
        return np.ascontiguousarray(np.asarray(v, dtype=f32).reshape(8, 128).T)

    common = dict(winT=winT, binr=binr, woutT=woutT, bout_b=bout_b,
                  ident=ident, ident_bf=ident_bf, csel=csel, ones1f=ones1f, cvals=cvals,
                  onesrow=onesrow, ws_r=Ws, bsrow=bs.reshape(1, 8),
                  ind8f=ind8f_h, ind8Tf=ind8Tf_h, ind8f16=(ind8f_h / 16.0).astype(f32),
                  selblk=selblk_h, triblk=triblk_h, gidxp1=gidxp1_h, offv=offv_h,
                  ones8f=np.ones((8, 1), f32),
                  wkT=np.ascontiguousarray(np.asarray(inputs["L1_attn_Wk"], f32).T),
                  wvT=np.ascontiguousarray(np.asarray(inputs["L1_attn_Wv"], f32).T),
                  wrT=np.ascontiguousarray(np.asarray(inputs["L1_attn_Wr"], f32).T),
                  woT=np.ascontiguousarray(np.asarray(inputs["L1_attn_Wo"], f32).T),
                  wfcT=np.ascontiguousarray(np.asarray(inputs["L1_ffn_W_fc"], f32).T),
                  wprojT=np.ascontiguousarray(np.asarray(inputs["L1_ffn_W_proj"], f32).T),
                  decay_r=r128(inputs["L1_attn_spatial_decay"]),
                  first_r=r128(inputs["L1_attn_spatial_first"]),
                  mixk_r=r128(inputs["L1_attn_mix_k"]),
                  mixv_r=r128(inputs["L1_attn_mix_v"]),
                  mixr_r=r128(inputs["L1_attn_mix_r"]))
    in_maps = []
    for c in range(8):
        m = dict(common)
        m["x"] = np.ascontiguousarray(X[c])
        m["w1"] = W1[c]
        m["b1r"] = np.ascontiguousarray(b1[c].reshape(32, 128).T)
        m["w2"] = W2[c]
        m["b2_b"] = np.ascontiguousarray(np.broadcast_to(b2[c], (128, D)))
        cs = np.zeros((8, 1024), bf)
        cs[:, c * 128] = 1.0
        m["colsel"] = cs
        tg = (c * 2048 + np.arange(128)[None, :] * 16 + (pidx % 16)[:, None]).astype(f32)
        m["tidg"] = np.ascontiguousarray(np.repeat(tg, 4, axis=1))
        me = np.zeros((128, 128), f32)
        me[:, np.arange(16) * 8 + c] = 1.0
        m["myexp"] = me
        in_maps.append(m)
    return in_maps


def kernel(**inputs):
    nc = build_program()
    in_maps = _prep_inputs(inputs)
    res = run_bass_kernel_spmd(nc, in_maps, core_ids=list(range(8)))
    out = np.stack([res.results[c]["out"] for c in range(8)], axis=0)
    return np.ascontiguousarray(out.astype(np.float32))
